# version 06
# baseline (speedup 1.0000x reference)
import numpy as np
import ml_dtypes
from contextlib import ExitStack

import concourse.bass as bass
import concourse.mybir as mybir
from concourse.engine_type import EngineType
from concourse.bass_utils import run_bass_kernel_spmd

F32 = mybir.dt.float32
BF16 = mybir.dt.bfloat16
I32 = mybir.dt.int32
AF = mybir.ActivationFunctionType
ALU = mybir.AluOpType
BF = ml_dtypes.bfloat16

H = 2048
T = 1024
NH = H // 128
NT = T // 128
I0, I1 = 8192, 5632
NI0, NI1 = I0 // 128, I1 // 128
GATE_OFFSET = 6.906768
ENGINES = ("tensor", "vector", "scalar", "gpsimd", "sync")


class Prog:

    SBUF_SIZE = 212736
    PSUM_BANKS = 8

    def __init__(self):
        self.nc = bass.Bass("TRN2", target_bir_lowering=False)
        self.es = ExitStack()
        self.sched = []
        self.sems = {}
        self._const_vals = {}
        self._sb_arena = self.nc.alloc_sbuf_tensor(
            "__arena", [128, self.SBUF_SIZE], mybir.dt.uint8)
        self._sb_base = self.nc.lookup_mloc(self._sb_arena).addr
        self._sb_free = [(0, self.SBUF_SIZE, {})]
        self._ps_arena = self.nc.alloc_psum_tensor(
            "__psarena", [128, 4096], F32)
        self._ps_base = self.nc.lookup_mloc(self._ps_arena).addr
        self._ps_free = [(0, self.PSUM_BANKS, {})]
        self._alloc_info = {}
        self._waits = {}
        self.sb_peak = 0
        self._const_tile = self.sbuf("__consts", [128, 32], F32)

    @staticmethod
    def _take(freelist, need):
        for i, (start, size, conds) in enumerate(freelist):
            if size >= need:
                if size == need:
                    freelist.pop(i)
                else:
                    freelist[i] = (start + need, size - need, conds)
                return start, conds
        raise MemoryError(f"arena OOM: need {need}, free={freelist}")

    @staticmethod
    def _give(freelist, start, size, conds):
        freelist.append((start, size, conds))
        freelist.sort()
        merged = []
        for s, n, c in freelist:
            if merged and merged[-1][0] + merged[-1][1] == s:
                ps, pn, pc = merged[-1]
                mc = dict(pc)
                for k, v in c.items():
                    mc[k] = max(mc.get(k, 0), v)
                merged[-1] = (ps, pn + n, mc)
            else:
                merged.append((s, n, c))
        freelist[:] = merged

    def _place(self, h, addr):
        self.nc._tensor(h.name, h.shape, h.dtype, type="SB")
        mloc = self.nc.lookup_mloc(h)
        mloc.allocated = True
        mloc.addr = addr

    def parity(self):
        if not hasattr(self, "_parity"):
            pid = self.nc.partition_id([EngineType.Pool])
            self._parity = (pid % 2, (pid + 1) % 2)
        return self._parity

    def ds_copy(self, sh, local, waits, name=""):
        def cp(eng):
            slot, _ = self.parity()
            return eng.dma_start(out=sh.ap()[bass.ds(slot, 1)],
                                 in_=local.ap())
        self.op("gpsimd", cp, waits=waits, inc="st", inc_amt=16)
        return self.cnt("st")

    def pair_signal(self, idx, shf, nf, waits):
        def sig(eng):
            slot, _ = self.parity()
            return eng.dma_start(
                out=shf.ap()[bass.ds(slot, 1), idx:idx + 1],
                in_=nf[0:1, idx:idx + 1])
        self.op("gpsimd", sig, waits=waits, inc="st", inc_amt=16)

    def pair_poll(self, idx, shf, nf, ft, waits=()):
        nc = self.nc
        fp = self.sem("fpoll")[0]

        def poll(eng):
            _, other = self.parity()
            if not hasattr(self, "_pregs"):
                self._pregs = (
                    nc.alloc_registers("ptgt", engines=[EngineType.Pool]),
                    nc.alloc_registers("pcur", engines=[EngineType.Pool]),
                    nc.alloc_registers("pcnt", engines=[EngineType.Pool]))
            tgt, cur, cnt = self._pregs
            nc.regs_load(tgt, nf[0:1, idx:idx + 1])
            nc.regs_mov(cnt, 0)
            loop, bodyb, done = f"pl{idx}", f"plb{idx}", f"pld{idx}"
            nc.br(loop, engines=[EngineType.Pool])
            with nc.body(loop, valid_engines=[EngineType.Pool]):
                eng.dma_start(
                    out=ft[0:1, idx:idx + 1],
                    in_=shf.ap()[bass.ds(other, 1), idx:idx + 1]
                ).then_inc(fp, 16)
                for _ in range(24):
                    nc.regs_alu(cnt, cnt, 1, op=mybir.AluOpType.add)
                nc.regs_load(cur, ft[0:1, idx:idx + 1])
                nc.br_cmp(cur, tgt, done, bodyb, "IS_EQ",
                          engines=[EngineType.Pool])
                with nc.body(bodyb, valid_engines=[EngineType.Pool]):
                    nc.br_cmp(cnt, 20000000, loop, done, "IS_LT",
                              engines=[EngineType.Pool])
            nc.switch_bb(done)
            return None
        return self.op("gpsimd", poll, waits=list(waits), inc="pp")

    def sem(self, name):
        if name not in self.sems:
            h = self.es.enter_context(self.nc.semaphore(name))
            self.sems[name] = [h, 0]
        return self.sems[name]

    def cnt(self, name):
        return self.sems[name][1] if name in self.sems else 0

    def sbuf(self, name, shape, dtype):
        nbytes = int(np.prod(shape[1:])) * mybir.dt.size(dtype)
        nbytes = (nbytes + 63) & ~63
        start, conds = self._take(self._sb_free, nbytes)
        h = self.nc.virtual_sbuf_tensor(name, list(shape), dtype)
        self._place(h, self._sb_base + start)
        self._alloc_info[id(h)] = ("sb", start, nbytes)
        self._waits[id(h)] = conds
        used = self.SBUF_SIZE - sum(e[1] for e in self._sb_free)
        self.sb_peak = max(self.sb_peak, used)
        return h

    def psum(self, name, shape, dtype=F32):
        nbytes = int(np.prod(shape[1:])) * mybir.dt.size(dtype)
        banks = (nbytes + 2047) // 2048
        start, conds = self._take(self._ps_free, banks)
        h = self.nc.virtual_psum_tensor(name, list(shape), dtype)
        self.nc._tensor(h.name, h.shape, h.dtype, type="PSUM")
        mloc = self.nc.lookup_mloc(h)
        mloc.allocated = True
        mloc.addr = self._ps_base + start * 2048
        self._alloc_info[id(h)] = ("ps", start, banks)
        self._waits[id(h)] = conds
        return h

    def wof(self, *handles):
        out = {}
        for h in handles:
            for k, v in self._waits.get(id(h), {}).items():
                out[k] = max(out.get(k, 0), v)
        return list(out.items())

    def free(self, *handles):
        snap = {n: s[1] for n, s in self.sems.items() if n != "BAR"}
        for h in handles:
            kind, start, size = self._alloc_info.pop(id(h))
            self._waits.pop(id(h), None)
            self._give(self._sb_free if kind == "sb" else self._ps_free,
                       start, size, snap)

    def dram(self, name, shape, dtype, kind="Internal", addr_space="Local"):
        return self.nc.dram_tensor(name, shape, dtype, kind=kind,
                                   addr_space=addr_space)

    def op(self, engine, fn, waits=(), inc=None, inc_amt=1):
        if inc is not None:
            s = self.sem(inc)
            s[1] += inc_amt
            inc_h, inc_v = s[0], inc_amt
        else:
            inc_h = None
            inc_v = 0
        sem_handles = [(self.sem(n)[0], v) for (n, v) in waits if v > 0]

        def run(eng):
            for h, v in sem_handles:
                eng.wait_ge(h, v)
            ins = fn(eng)
            if inc_h is not None:
                if ins is None:
                    ins = eng.nop()
                ins.then_inc(inc_h, inc_v)
        self.sched.append((engine, run))
        return self.cnt(inc) if inc is not None else None

    def wait(self, engine, *waits):
        self.op(engine, lambda e: None, waits=waits)

    def barrier(self, engines=ENGINES):
        s = self.sem("BAR")
        for _ in engines:
            s[1] += 1
        target = s[1]
        for e in engines:
            def run(eng, h=s[0], target=target):
                eng.nop().then_inc(h, 1)
                eng.wait_ge(h, target)
            self.sched.append((e, run))

    def fconst(self, *vals):
        for val in vals:
            if val not in self._const_vals:
                self._const_vals[val] = len(self._const_vals)

    def _emit_consts(self):
        if not self._const_vals:
            return
        assert len(self._const_vals) <= 32
        tile = self._const_tile
        vals = dict(self._const_vals)

        cs = self.sem("CONST")
        cs[1] += 1

        def run(eng, h=cs[0]):
            for v, i in vals.items():
                eng.memset(tile[:, i:i + 1], float(v))
            eng.nop().then_inc(h, 1)
        self.sched.insert(0, ("vector", run))
        for e in ENGINES:
            if e == "vector":
                continue

            def wrun(eng, h=cs[0]):
                eng.wait_ge(h, 1)
            self.sched.insert(1, (e, wrun))
        for v, i in vals.items():
            self.nc.const_aps.register(F32, float(v), tile[:, i:i + 1])

    def finalize(self):
        self._emit_consts()
        per_engine = {e: [fn for (en, fn) in self.sched if en == e]
                      for e in ENGINES}
        with self.es, self.nc.Block() as block:
            @block.tensor
            def _(eng):
                for fn in per_engine["tensor"]:
                    fn(eng)

            @block.vector
            def _(eng):
                for fn in per_engine["vector"]:
                    fn(eng)

            @block.scalar
            def _(eng):
                for fn in per_engine["scalar"]:
                    fn(eng)

            @block.gpsimd
            def _(eng):
                for fn in per_engine["gpsimd"]:
                    fn(eng)

            @block.sync
            def _(eng):
                for fn in per_engine["sync"]:
                    fn(eng)
        return self.nc



def emit_ln(p, x_dram, xln, name):
    nm = name
    xf = p.sbuf(nm + "_xf", [128, NH * T], F32)
    sq = p.sbuf(nm + "_sq", [128, 2 * T], BF16)
    xb = p.sbuf(nm + "_xb", [128, 2 * T], BF16)
    w1 = p.sbuf(nm + "_w1", [128, 1], BF16)
    m_ps = p.psum(nm + "_mps", [1, T])
    v_ps = p.psum(nm + "_vps", [1, T])
    row = p.sbuf(nm + "_row", [1, 4 * T], F32)
    ones_f = p.sbuf(nm + "_onesf", [1, 128], F32)
    b_ps = p.psum(nm + "_bps", [128, T])
    r_ps = p.psum(nm + "_rps", [128, T])
    mb = p.sbuf(nm + "_mb", [128, T], F32)
    rb = p.sbuf(nm + "_rb", [128, T], BF16)
    tb = p.sbuf(nm + "_tb", [128, 2 * T], BF16)

    p.fconst(1e-5)
    p.op("vector", lambda e: e.memset(w1[:, :], 1.0 / H), waits=p.wof(w1), inc="v")
    p.op("vector", lambda e: e.memset(ones_f[:, :], 1.0), waits=p.wof(ones_f), inc="v")
    for c in range(NH):
        ld = p.op("sync", lambda e, c=c: e.dma_start(
            out=xf[:, c * T:(c + 1) * T], in_=x_dram[c * 128:(c + 1) * 128, :]),
            waits=(p.wof(xf) if c == 0 else ()), inc="ld", inc_amt=16)
        w = [("ld", ld)]
        if c >= 2:
            w.append(("t", p.cnt("t")))
        a = p.op("scalar", lambda e, c=c: e.activation(
            sq[:, (c % 2) * T:(c % 2 + 1) * T], xf[:, c * T:(c + 1) * T],
            AF.Square), waits=w + (p.wof(sq) if c == 0 else []), inc="a")
        v = p.op("vector", lambda e, c=c: e.tensor_copy(
            xb[:, (c % 2) * T:(c % 2 + 1) * T], xf[:, c * T:(c + 1) * T]),
            waits=w + (p.wof(xb) if c == 0 else []), inc="v")
        for half in range(2):
            p.op("tensor", lambda e, c=c, half=half: e.matmul(
                m_ps[0:1, half * 512:(half + 1) * 512], w1[:, 0:1],
                xb[:, (c % 2) * T + half * 512:(c % 2) * T + (half + 1) * 512],
                start=(c == 0), stop=(c == NH - 1)),
                waits=([("v", v)] + (p.wof(m_ps) if c == 0 else [])
                       if half == 0 else ()))
            p.op("tensor", lambda e, c=c, half=half: e.matmul(
                v_ps[0:1, half * 512:(half + 1) * 512], w1[:, 0:1],
                sq[:, (c % 2) * T + half * 512:(c % 2) * T + (half + 1) * 512],
                start=(c == 0), stop=(c == NH - 1)),
                waits=([("a", a)] + (p.wof(v_ps) if c == 0 else [])
                       if half == 0 else ()),
                inc=("t" if half == 1 else None))
    pe_stats = p.cnt("t")
    v1 = p.op("vector", lambda e: e.tensor_copy(row[0:1, 0:T], m_ps[0:1, :]),
              waits=[("t", pe_stats)] + p.wof(row), inc="v")
    v2 = p.op("vector", lambda e: e.tensor_tensor(
        row[0:1, T:2 * T], row[0:1, 0:T], row[0:1, 0:T], ALU.mult),
        waits=[("v", v1)], inc="v")
    v3 = p.op("vector", lambda e: e.tensor_tensor(
        row[0:1, 2 * T:3 * T], v_ps[0:1, :], row[0:1, T:2 * T], ALU.subtract),
        waits=[("v", v2)], inc="v")
    a1 = p.op("scalar", lambda e: e.activation(
        row[0:1, T:2 * T], row[0:1, 2 * T:3 * T], AF.Ln, bias=1e-5),
        waits=[("v", v3)], inc="a")
    a2 = p.op("scalar", lambda e: e.activation(
        row[0:1, 3 * T:4 * T], row[0:1, T:2 * T], AF.Exp, scale=-0.5),
        waits=[("a", a1)], inc="a")
    t1 = None
    for half in range(2):
        p.op("tensor", lambda e, half=half: e.matmul(
            b_ps[:, half * 512:(half + 1) * 512], ones_f[0:1, :],
            row[0:1, half * 512:(half + 1) * 512], start=True, stop=True),
            waits=([("v", v1)] + p.wof(b_ps) if half == 0 else ()))
        t1 = p.op("tensor", lambda e, half=half: e.matmul(
            r_ps[:, half * 512:(half + 1) * 512], ones_f[0:1, :],
            row[0:1, 3 * T + half * 512:3 * T + (half + 1) * 512],
            start=True, stop=True),
            waits=([("a", a2)] + p.wof(r_ps) if half == 0 else ()),
            inc=("t" if half == 1 else None))
    v4 = p.op("vector", lambda e: e.tensor_copy(mb[:, :], b_ps[:, :]),
              waits=[("t", t1)] + p.wof(mb), inc="v")
    v5 = p.op("vector", lambda e: e.tensor_copy(rb[:, :], r_ps[:, :]),
              waits=[("v", v4)] + p.wof(rb), inc="v")
    for c in range(NH):
        vv = p.op("vector", lambda e, c=c: e.tensor_tensor(
            tb[:, (c % 2) * T:(c % 2 + 1) * T], xf[:, c * T:(c + 1) * T],
            mb[:, :], ALU.subtract),
            waits=([("v", v5)] + p.wof(tb, xln) if c == 0 else ()), inc="v")
        p.op("vector", lambda e, c=c: e.tensor_tensor(
            xln[:, c * T:(c + 1) * T], tb[:, (c % 2) * T:(c % 2 + 1) * T],
            rb[:, :], ALU.mult), waits=[("v", vv)], inc="v")
    done = p.cnt("v")
    p.free(xf, sq, xb, w1, row, ones_f, mb, rb, tb, m_ps, v_ps, b_ps, r_ps)
    return done


def emit_colsum_rows(p, arena, ncols, name, ready_v):
    nm = name
    ones1 = p.sbuf(nm + "_o1", [128, 1], BF16)
    ps = p.psum(nm + "_ps", [1, T])
    hi = p.sbuf(nm + "_hi", [1, T], BF16)
    lo = p.sbuf(nm + "_lo", [1, T], BF16)
    hf = p.sbuf(nm + "_hf", [1, T], F32)
    p.op("vector", lambda e: e.memset(ones1[:, :], 1.0), waits=p.wof(ones1),
         inc="v")
    mv = p.cnt("v")
    t = None
    for c in range(ncols):
        for half in range(2):
            t = p.op("tensor", lambda e, c=c, half=half: e.matmul(
                ps[0:1, half * 512:(half + 1) * 512], ones1[:, 0:1],
                arena[:, c * T + half * 512:c * T + (half + 1) * 512],
                start=(c == 0), stop=(c == ncols - 1)),
                waits=([("v", max(ready_v, mv))] + p.wof(ps)
                       if (c == 0 and half == 0) else ()),
                inc=("t" if (c == ncols - 1 and half == 1) else None))
    v1 = p.op("vector", lambda e: e.tensor_copy(hi[0:1, :], ps[0:1, :]),
              waits=[("t", t)] + p.wof(hi, hf, lo), inc="v")
    v2 = p.op("vector", lambda e: e.tensor_copy(hf[0:1, :], hi[0:1, :]),
              waits=[("v", v1)], inc="v")
    v3 = p.op("vector", lambda e: e.tensor_tensor(
        lo[0:1, :], ps[0:1, :], hf[0:1, :], ALU.subtract),
        waits=[("v", v2)], inc="v")
    p.free(ps, hf, ones1)
    return hi, lo, v3, (hi, lo)


def emit_ffn(p, x_dram, xout_dram, wg_dram, wu_dram, wd_dram,
             sg_dram, su_dram, sd_dram, NI, name):
    nm = name
    xln = p.sbuf(nm + "_xln", [128, NH * T], BF16)
    lnv = emit_ln(p, x_dram, xln, nm + "_ln")
    hi, lo, xsv, rows_free = emit_colsum_rows(p, xln, NH, nm + "_xs", lnv)

    sg = p.sbuf(nm + "_sg", [128, NI], F32)
    su = p.sbuf(nm + "_su", [128, NI], F32)
    sd = p.sbuf(nm + "_sd", [128, NH], F32)
    sdh = p.sbuf(nm + "_sdh", [128, NH], F32)
    negk = p.sbuf(nm + "_negk", [1, 128], BF16)
    p.op("sync", lambda e: e.dma_start(out=sg[:, :], in_=sg_dram.ap()),
         waits=p.wof(sg, su, sd), inc="ld", inc_amt=16)
    p.op("sync", lambda e: e.dma_start(out=su[:, :], in_=su_dram.ap()),
         inc="ld", inc_amt=16)
    ldsc = p.op("sync", lambda e: e.dma_start(out=sd[:, :], in_=sd_dram.ap()),
                inc="ld", inc_amt=16)
    p.op("vector", lambda e: e.memset(negk[:, :], -128.0),
         waits=p.wof(negk, sdh), inc="v")
    p.op("vector", lambda e: e.tensor_scalar(
        sdh[:, :], sd[:, :], 0.5, None, ALU.mult),
        waits=[("ld", ldsc)], inc="v")
    prep_v = p.cnt("v")

    NBUF = 3
    harena = p.sbuf(nm + "_h", [128, NI * T], BF16)
    gbuf = p.sbuf(nm + "_gw", [128, NBUF * NH * 128], BF16)
    ubuf = p.sbuf(nm + "_uw", [128, NBUF * NH * 128], BF16)
    g_ps = [p.psum(nm + "_gps%d" % i, [128, T]) for i in range(2)]
    u_ps = [p.psum(nm + "_ups%d" % i, [128, T]) for i in range(2)]
    sgb = p.sbuf(nm + "_sgb", [128, 2 * T], BF16)
    gend = {}
    aend = {}
    hend = {}
    wend = {}
    for ic in range(NI):
        par = ic % 2
        wpar = ic % NBUF
        dw = []
        if ic == 0:
            dw = [("v", lnv)] + p.wof(gbuf, ubuf)
        if ic >= NBUF:
            dw = [("t", gend[ic - NBUF])]
        wv = p.op("gpsimd", lambda e, ic=ic, wpar=wpar: e.dma_start(
            out=gbuf[:, wpar * NH * 128:(wpar + 1) * NH * 128],
            in_=wg_dram[ic]), waits=dw, inc="w", inc_amt=16)
        wv = p.op("gpsimd", lambda e, ic=ic, wpar=wpar: e.dma_start(
            out=ubuf[:, wpar * NH * 128:(wpar + 1) * NH * 128],
            in_=wu_dram[ic]), inc="w", inc_amt=16)
        wend[ic] = wv
        tw = [("w", wend[ic]), ("v", max(prep_v, xsv))]
        if ic == 0:
            tw += p.wof(*g_ps, *u_ps)
        if ic >= 2:
            tw.append(("a", aend[ic - 2]))
            tw.append(("v", hend[ic - 2]))
        for hc in range(NH):
            for half in range(2):
                p.op("tensor", lambda e, wpar=wpar, par=par, hc=hc, half=half: e.matmul(
                    g_ps[par][:, half * 512:(half + 1) * 512],
                    gbuf[:, wpar * NH * 128 + hc * 128:wpar * NH * 128 + (hc + 1) * 128],
                    xln[:, hc * T + half * 512:hc * T + (half + 1) * 512],
                    start=(hc == 0), stop=False),
                    waits=(tw if (hc == 0 and half == 0) else ()))
        for half in range(2):
            p.op("tensor", lambda e, par=par, half=half: e.matmul(
                g_ps[par][:, half * 512:(half + 1) * 512], negk[0:1, :],
                hi[0:1, half * 512:(half + 1) * 512], start=False, stop=False))
            p.op("tensor", lambda e, par=par, half=half: e.matmul(
                g_ps[par][:, half * 512:(half + 1) * 512], negk[0:1, :],
                lo[0:1, half * 512:(half + 1) * 512], start=False, stop=True),
                inc=("t" if half == 1 else None))
        tgd = p.cnt("t")
        for hc in range(NH):
            for half in range(2):
                p.op("tensor", lambda e, wpar=wpar, par=par, hc=hc, half=half: e.matmul(
                    u_ps[par][:, half * 512:(half + 1) * 512],
                    ubuf[:, wpar * NH * 128 + hc * 128:wpar * NH * 128 + (hc + 1) * 128],
                    xln[:, hc * T + half * 512:hc * T + (half + 1) * 512],
                    start=(hc == 0), stop=False))
        for half in range(2):
            p.op("tensor", lambda e, par=par, half=half: e.matmul(
                u_ps[par][:, half * 512:(half + 1) * 512], negk[0:1, :],
                hi[0:1, half * 512:(half + 1) * 512], start=False, stop=False))
            p.op("tensor", lambda e, par=par, half=half: e.matmul(
                u_ps[par][:, half * 512:(half + 1) * 512], negk[0:1, :],
                lo[0:1, half * 512:(half + 1) * 512], start=False, stop=True),
                inc=("t" if half == 1 else None))
        tud = p.cnt("t")
        gend[ic] = tud
        aa = p.op("scalar", lambda e, ic=ic, par=par: e.activation(
            sgb[:, par * T:(par + 1) * T], g_ps[par][:, :], AF.Silu,
            scale=sg[:, ic:ic + 1]),
            waits=[("t", tgd), ("v", prep_v)] + (p.wof(sgb) if ic == 0 else []),
            inc="a")
        aend[ic] = aa
        hend[ic] = p.op("vector", lambda e, ic=ic, par=par: e.scalar_tensor_tensor(
            harena[:, ic * T:(ic + 1) * T], u_ps[par][:, :], su[:, ic:ic + 1],
            sgb[:, par * T:(par + 1) * T], ALU.mult, ALU.mult),
            waits=[("t", tud), ("a", aa)] + (p.wof(harena) if ic == 0 else []),
            inc="v")
    h_done_v = p.cnt("v")
    p.free(gbuf, ubuf, sgb, *g_ps, *u_ps)
    p.free(xln)

    hhi, hlo, hs_v, hrows_free = emit_colsum_rows(
        p, harena, NI, nm + "_hs", h_done_v)

    NBD = 2
    dbuf = p.sbuf(nm + "_dw", [128, NBD * NI * 128], BF16)
    d_ps = [p.psum(nm + "_dps%d" % i, [128, T]) for i in range(2)]
    xres = p.sbuf(nm + "_xr", [128, 2 * T], F32)
    xo = p.sbuf(nm + "_xo", [128, 2 * T], F32)
    st_base = p.cnt("st")
    dend = {}
    vend = {}
    for hc in range(NH):
        par = hc % 2
        wpar = hc % NBD
        ww = []
        if hc == 0:
            ww = [("v", h_done_v)] + p.wof(dbuf)
        if hc >= NBD:
            ww = [("t", dend[hc - NBD])]
        wv = p.op("gpsimd", lambda e, hc=hc, wpar=wpar: e.dma_start(
            out=dbuf[:, wpar * NI * 128:(wpar + 1) * NI * 128],
            in_=wd_dram[hc]), waits=ww, inc="w", inc_amt=16)
        lr = p.op("sync", lambda e, hc=hc, par=par: e.dma_start(
            out=xres[:, par * T:(par + 1) * T],
            in_=x_dram[hc * 128:(hc + 1) * 128, :]),
            waits=(p.wof(xres) if hc == 0 else
                   ([("v", vend[hc - 2])] if hc >= 2 else [])),
            inc="ld", inc_amt=16)
        tw = [("w", wv), ("v", hs_v)]
        if hc == 0:
            tw += p.wof(*d_ps)
        if hc >= 2:
            tw.append(("v", vend[hc - 2]))
        for ic in range(NI):
            for half in range(2):
                p.op("tensor", lambda e, wpar=wpar, par=par, ic=ic, half=half: e.matmul(
                    d_ps[par][:, half * 512:(half + 1) * 512],
                    dbuf[:, wpar * NI * 128 + ic * 128:wpar * NI * 128 + (ic + 1) * 128],
                    harena[:, ic * T + half * 512:ic * T + (half + 1) * 512],
                    start=(ic == 0), stop=False),
                    waits=(tw if (ic == 0 and half == 0) else ()))
        for half in range(2):
            p.op("tensor", lambda e, par=par, half=half: e.matmul(
                d_ps[par][:, half * 512:(half + 1) * 512], negk[0:1, :],
                hhi[0:1, half * 512:(half + 1) * 512], start=False, stop=False))
            p.op("tensor", lambda e, par=par, half=half: e.matmul(
                d_ps[par][:, half * 512:(half + 1) * 512], negk[0:1, :],
                hlo[0:1, half * 512:(half + 1) * 512], start=False, stop=True),
                inc=("t" if half == 1 else None))
        tdd = p.cnt("t")
        dend[hc] = tdd
        vw = [("t", tdd), ("ld", lr)]
        if hc == 0:
            vw += p.wof(xo)
        if hc >= 2:
            vw.append(("st", st_base + 16 * (hc - 1)))
        vo = p.op("vector", lambda e, hc=hc, par=par: e.scalar_tensor_tensor(
            xo[:, par * T:(par + 1) * T], d_ps[par][:, :], sdh[:, hc:hc + 1],
            xres[:, par * T:(par + 1) * T], ALU.mult, ALU.add),
            waits=vw, inc="v")
        vend[hc] = vo
        p.op("sync", lambda e, hc=hc, par=par: e.dma_start(
            out=xout_dram[hc * 128:(hc + 1) * 128, :],
            in_=xo[:, par * T:(par + 1) * T]), waits=[("v", vo)],
            inc="st", inc_amt=16)
    st_done = p.cnt("st")
    p.free(harena, dbuf, xres, xo, sg, su, sd, sdh, negk,
           *d_ps, *rows_free, *hrows_free)
    return st_done


def emit_proj(p, xln, w_dram, out_dram, n_out, mode, name, xln_ready_v,
              cast=True, stq="sync"):
    nm = name
    NO = n_out // 128
    if not callable(out_dram):
        dram_h = out_dram
        out_dram = lambda oc: dram_h[oc * 128:(oc + 1) * 128, :]
    wsb = p.sbuf(nm + "_w", [128, NH * n_out], BF16)
    ps = [p.psum(nm + "_ps%d" % i, [128, 512]) for i in range(2)]
    stage = p.sbuf(nm + "_st", [128, 2 * max(n_out, T)], BF16)
    wv = None
    wq = "w" if cast else "ld"
    weng = "gpsimd" if cast else "sync"
    for hc in range(NH):
        wv = p.op(weng, lambda e, hc=hc: e.dma_start(
            out=wsb[:, hc * n_out:(hc + 1) * n_out], in_=w_dram[hc]),
            waits=(p.wof(wsb) if hc == 0 else ()), inc=wq, inc_amt=16)
    st_base = p.cnt("st")
    k = 0
    outer = NT if mode == "tm" else NO
    inner = n_out // 512 if mode == "tm" else T // 512
    for oc in range(outer):
        par = oc % 2
        for it in range(inner):
            pp = ps[k % 2]
            tw = [(wq, wv), ("v", xln_ready_v)]
            if k == 0:
                tw += p.wof(*ps)
            if k >= 2:
                tw.append(("v", p.cnt("v")))
            tt = None
            for hc in range(NH):
                if mode == "tm":
                    fn = (lambda e, oc=oc, it=it, hc=hc, pp=pp: e.matmul(
                        pp[:, :],
                        xln[:, hc * T + oc * 128:hc * T + (oc + 1) * 128],
                        wsb[:, hc * n_out + it * 512:hc * n_out + (it + 1) * 512],
                        start=(hc == 0), stop=(hc == NH - 1)))
                else:
                    fn = (lambda e, oc=oc, it=it, hc=hc, pp=pp: e.matmul(
                        pp[:, :],
                        wsb[:, hc * n_out + oc * 128:hc * n_out + (oc + 1) * 128],
                        xln[:, hc * T + it * 512:hc * T + (it + 1) * 512],
                        start=(hc == 0), stop=(hc == NH - 1)))
                tt = p.op("tensor", fn, waits=(tw if hc == 0 else ()),
                          inc=("t" if hc == NH - 1 else None))
            width = n_out if mode == "tm" else T
            cw = [("t", tt)]
            if k <= 1:
                cw += p.wof(stage)
            if oc >= 2 and it == 0:
                cw.append(("st", st_base + 16 * (oc - 1)))
            p.op("vector", lambda e, it=it, par=par, pp=pp, width=width: e.tensor_copy(
                stage[:, par * width + it * 512:par * width + (it + 1) * 512],
                pp[:, :]), waits=cw, inc="v")
            k += 1
        sv = p.cnt("v")
        p.op(stq, lambda e, oc=oc, par=par, width=width: e.dma_start(
            out=out_dram(oc),
            in_=stage[:, par * width:(par + 1) * width]),
            waits=[("v", sv)], inc="st", inc_amt=16)
    p.free(wsb, stage, *ps)
    return p.cnt("st")



def _bcast_mid(ap, count, pos=1):
    import dataclasses
    dims = list(ap.ap)
    dims.insert(pos, [0, count])
    return dataclasses.replace(ap, ap=type(ap.ap)(dims))


def emit_attn0(p, x_dram, xout_dram, qf_dram, kf_dram, v_dram, wo_dram,
               ident_dram, name, gate=()):
    nm = name
    et_dram = p.dram(nm + "_et", [16, 128, H], BF16)
    ksb = p.sbuf(nm + "_k", [128, 16 * H], BF16)
    qsb = p.sbuf(nm + "_q", [128, 16 * H], BF16)
    rz = p.sbuf(nm + "_rz", [128, 16], F32)
    mx = p.sbuf(nm + "_mx", [128, 16], F32)
    zp = p.sbuf(nm + "_zp", [128, 16], F32)
    esb = p.sbuf(nm + "_e", [128, 2 * H], BF16)
    etst = p.sbuf(nm + "_ets", [128, 2 * H], BF16)
    ident = p.sbuf(nm + "_id", [128, 128], BF16)
    t_ps = [p.psum(nm + "_tps%d" % i, [128, 128], BF16) for i in range(2)]
    kl = ql = None
    for c in range(16):
        hh, r = c // 8, c % 8
        kl = p.op("sync", lambda e, c=c, hh=hh, r=r: e.dma_start(
            out=ksb[:, c * H:(c + 1) * H],
            in_=kf_dram.ap()[hh, r * 128:(r + 1) * 128, :]),
            waits=(list(gate) + p.wof(ksb, qsb) if c == 0 else ()),
            inc="ld", inc_amt=16)
        ql = p.op("sync", lambda e, c=c, hh=hh, r=r: e.dma_start(
            out=qsb[:, c * H:(c + 1) * H],
            in_=qf_dram.ap()[hh, r * 128:(r + 1) * 128, :]),
            inc="ld", inc_amt=16)
    mvv = p.op("vector", lambda e: None,
               waits=p.wof(ident, rz, mx, zp, esb, etst), inc="v")
    idl = p.op("sync", lambda e: e.dma_start(out=ident[:, :], in_=ident_dram.ap()),
               waits=[("v", mvv)], inc="ld", inc_amt=16)
    ssb = p.sbuf(nm + "_ssb", [128, 2 * H], F32)
    s_ps2 = [p.psum(nm + "_sps%d" % i, [128, 512]) for i in range(2)]
    kk = 0
    scp = {}
    expa = {}
    tend = {}
    for et in range(16):
        pb = (et % 2) * H
        mb_ = (et % 2) * 8
        for ft in range(4):
            pp = s_ps2[kk % 2]
            tw = [("ld", max(ql, idl))]
            if kk == 0:
                tw += p.wof(*s_ps2) + p.wof(ssb)
            if kk >= 2:
                et2, ft2 = divmod(kk - 2, 4)
                tw.append(("v", scp[(et2, ft2)]))
            if ft == 0 and et >= 2:
                tw.append(("a", expa[et - 2]))
            for sc in range(16):
                p.op("tensor", lambda e, et=et, ft=ft, sc=sc, pp=pp: e.matmul(
                    pp[:, :],
                    qsb[:, sc * H + et * 128:sc * H + (et + 1) * 128],
                    ksb[:, sc * H + ft * 512:sc * H + (ft + 1) * 512],
                    start=(sc == 0), stop=(sc == 15)),
                    waits=(tw if sc == 0 else ()),
                    inc=("t" if sc == 15 else None))
            ts = p.cnt("t")
            scp[(et, ft)] = p.op("vector", lambda e, ft=ft, pp=pp, pb=pb: e.tensor_copy(
                ssb[:, pb + ft * 512:pb + (ft + 1) * 512], pp[:, :]),
                waits=[("t", ts)], inc="v")
            kk += 1
        rw = [("v", scp[(et, 3)])]
        if et >= 1:
            rw.append(("a", expa[et - 1]))
        if et >= 2:
            rw.append(("a", expa[et - 2]))
        for ft in range(4):
            p.op("vector", lambda e, ft=ft, pb=pb, mb_=mb_: e.tensor_reduce(
                mx[:, mb_ + ft:mb_ + ft + 1], ssb[:, pb + ft * 512:pb + (ft + 1) * 512],
                mybir.AxisListType.X, ALU.max),
                waits=(rw if ft == 0 else ()), inc="v")
        vv = p.op("vector", lambda e, mb_=mb_: e.tensor_reduce(
            mx[:, mb_ + 4:mb_ + 5], mx[:, mb_:mb_ + 4], mybir.AxisListType.X,
            ALU.max), waits=[("v", p.cnt("v"))], inc="v")
        vv = p.op("vector", lambda e, mb_=mb_: e.tensor_scalar(
            mx[:, mb_ + 5:mb_ + 6], mx[:, mb_ + 4:mb_ + 5], -0.125, None,
            ALU.mult), waits=[("v", vv)], inc="v")
        aa = None
        for ft in range(4):
            aa = p.op("scalar", lambda e, ft=ft, pb=pb, mb_=mb_: e.activation(
                esb[:, pb + ft * 512:pb + (ft + 1) * 512],
                ssb[:, pb + ft * 512:pb + (ft + 1) * 512],
                AF.Exp, bias=mx[:, mb_ + 5:mb_ + 6], scale=0.125,
                accum_out=zp[:, mb_ + ft:mb_ + ft + 1]),
                waits=([("v", vv)] + ([("t", tend[et - 2])] if et >= 2 else [])
                       if ft == 0 else ()), inc="a")
        expa[et] = aa
        vz = p.op("vector", lambda e, mb_=mb_: e.tensor_reduce(
            zp[:, mb_ + 4:mb_ + 5], zp[:, mb_:mb_ + 4], mybir.AxisListType.X,
            ALU.add), waits=[("a", aa)], inc="v")
        vz = p.op("vector", lambda e, et=et, mb_=mb_: e.reciprocal(
            rz[:, et:et + 1], zp[:, mb_ + 4:mb_ + 5]),
            waits=[("v", vz)], inc="v")
        par = et % 2
        for fc in range(16):
            tt = p.op("tensor", lambda e, fc=fc, pb=pb: e.transpose(
                t_ps[fc % 2][:, :],
                esb[:, pb + fc * 128:pb + (fc + 1) * 128], ident[:, :]),
                waits=([("a", aa)] +
                       (p.wof(*t_ps) if et == 0 and fc == 0 else []) +
                       ([("v", p.cnt("v"))] if (fc >= 2 or et >= 1) else [])),
                inc="t")
            p.op("vector", lambda e, fc=fc, par=par: e.tensor_copy(
                etst[:, par * H + fc * 128:par * H + (fc + 1) * 128],
                t_ps[fc % 2][:, :]),
                waits=[("t", p.cnt("t"))] +
                      ([("st", p.cnt("st"))] if fc == 0 and et >= 2 else []),
                inc="v")
        tend[et] = p.cnt("t")
        sv = p.cnt("v")
        p.op("sync", lambda e, et=et, par=par: e.dma_start(
            out=et_dram[et], in_=etst[:, par * H:(par + 1) * H]),
            waits=[("v", sv)], inc="st", inc_amt=16)
    phaseb_st = p.cnt("st")
    p.free(ksb, qsb, esb, ssb, *s_ps2, *t_ps)

    vsb = p.sbuf(nm + "_v", [128, 16 * T], BF16)
    ao = p.sbuf(nm + "_ao", [128, 16 * T], BF16)
    etb = p.sbuf(nm + "_etb", [128, 2 * H], BF16)
    a_ps = [p.psum(nm + "_aps%d" % i, [128, 512]) for i in range(2)]
    vl = None
    for c in range(16):
        vl = p.op("sync", lambda e, c=c: e.dma_start(
            out=vsb[:, c * T:(c + 1) * T], in_=v_dram[c * 128:(c + 1) * 128, :]),
            waits=(p.wof(vsb, ao, etb) if c == 0 else ()), inc="ld", inc_amt=16)
    k = 0
    for et in range(16):
        par = et % 2
        el = p.op("sync", lambda e, et=et, par=par: e.dma_start(
            out=etb[:, par * H:(par + 1) * H], in_=et_dram[et]),
            waits=([("st", phaseb_st)] if et == 0 else
                   ([("t", p.cnt("t"))] if et >= 2 else [])),
            inc="ld", inc_amt=16)
        for sh in range(2):
            pp = a_ps[k % 2]
            tw = [("ld", max(el, vl))]
            if k == 0:
                tw += p.wof(*a_ps)
            if k >= 2:
                tw.append(("v", p.cnt("v")))
            tt = None
            for fc in range(16):
                tt = p.op("tensor", lambda e, et=et, par=par, sh=sh, fc=fc, pp=pp: e.matmul(
                    pp[:, :], etb[:, par * H + fc * 128:par * H + (fc + 1) * 128],
                    vsb[:, fc * T + sh * 512:fc * T + (sh + 1) * 512],
                    start=(fc == 0), stop=(fc == 15)),
                    waits=(tw if fc == 0 else ()),
                    inc=("t" if fc == 15 else None))
            p.op("vector", lambda e, et=et, sh=sh, pp=pp: e.tensor_scalar(
                ao[:, et * T + sh * 512:et * T + (sh + 1) * 512], pp[:, :],
                rz[:, et:et + 1], None, ALU.mult),
                waits=[("t", tt)], inc="v")
            k += 1
    ao_v = p.cnt("v")
    p.free(etb, *a_ps)

    wsb = p.sbuf(nm + "_wo", [128, NH * H], BF16)
    w_ps = [p.psum(nm + "_wps%d" % i, [128, T]) for i in range(2)]
    xres = p.sbuf(nm + "_xr", [128, 2 * T], F32)
    xo = p.sbuf(nm + "_xo", [128, 2 * T], F32)
    wv = None
    for hc in range(NH):
        wv = p.op("gpsimd", lambda e, hc=hc: e.dma_start(
            out=wsb[:, hc * H:(hc + 1) * H], in_=wo_dram[hc]),
            waits=(p.wof(wsb) if hc == 0 else ()), inc="w", inc_amt=16)
    st_base = p.cnt("st")
    for oc in range(NH):
        par = oc % 2
        lr = p.op("sync", lambda e, oc=oc, par=par: e.dma_start(
            out=xres[:, par * T:(par + 1) * T],
            in_=x_dram[oc * 128:(oc + 1) * 128, :]),
            waits=(p.wof(xres, xo) if oc == 0 else
                   ([("v", p.cnt("v"))] if oc >= 2 else [])),
            inc="ld", inc_amt=16)
        tw = [("w", wv), ("v", ao_v)]
        if oc == 0:
            tw += p.wof(*w_ps)
        if oc >= 2:
            tw.append(("v", p.cnt("v")))
        for ec in range(16):
            for half in range(2):
                p.op("tensor", lambda e, oc=oc, par=par, ec=ec, half=half: e.matmul(
                    w_ps[par][:, half * 512:(half + 1) * 512],
                    wsb[:, ec * H + oc * 128:ec * H + (oc + 1) * 128],
                    ao[:, ec * T + half * 512:ec * T + (half + 1) * 512],
                    start=(ec == 0), stop=(ec == 15)),
                    waits=(tw if (ec == 0 and half == 0) else ()),
                    inc=("t" if (ec == 15 and half == 1) else None))
        tdd = p.cnt("t")
        vw = [("t", tdd), ("ld", lr)]
        if oc >= 2:
            vw.append(("st", st_base + 16 * (oc - 1)))
        vo = p.op("vector", lambda e, oc=oc, par=par: e.tensor_tensor(
            xo[:, par * T:(par + 1) * T], w_ps[par][:, :],
            xres[:, par * T:(par + 1) * T], ALU.add),
            waits=vw, inc="v")
        p.op("sync", lambda e, oc=oc, par=par: e.dma_start(
            out=xout_dram[oc * 128:(oc + 1) * 128, :],
            in_=xo[:, par * T:(par + 1) * T]), waits=[("v", vo)],
            inc="st", inc_amt=16)
    st_done = p.cnt("st")
    p.free(vsb, ao, wsb, xres, xo, rz, mx, zp, ident, etst, *w_ps)
    return st_done


def emit_proj_g(p, xln, wg_dram, lgl_dram, identf_dram, name, xln_ready_v):
    nm = name
    wsb = p.sbuf(nm + "_w", [128, NH * 8], BF16)
    g_ps = p.psum(nm + "_ps", [8, T])
    sc1 = p.sbuf(nm + "_s1", [8, T], F32)
    sc2 = p.sbuf(nm + "_s2", [8, T], F32)
    sc3 = p.sbuf(nm + "_s3", [8, T], F32)
    mt = p.sbuf(nm + "_mt", [128, T], F32)
    idf = p.sbuf(nm + "_idf", [128, 128], F32)
    lgt = p.sbuf(nm + "_lgt", [128, 8 * 8], F32)
    t_ps = p.psum(nm + "_tps", [128, 128])
    p.fconst(GATE_OFFSET, 1.0)
    mz = p.op("vector", lambda e: e.memset(mt[:, :], 0.0),
              waits=p.wof(mt, idf, lgt, t_ps), inc="v")
    idl = p.op("sync", lambda e: e.dma_start(
        out=idf[:, :], in_=identf_dram.ap()), inc="ld", inc_amt=16)
    wv = None
    for hc in range(NH):
        wv = p.op("gpsimd", lambda e, hc=hc: e.dma_start(
            out=wsb[:, hc * 8:(hc + 1) * 8], in_=wg_dram[hc]),
            waits=(p.wof(wsb) if hc == 0 else ()), inc="w", inc_amt=16)
    tt = None
    for hc in range(NH):
        for half in range(2):
            tt = p.op("tensor", lambda e, hc=hc, half=half: e.matmul(
                g_ps[0:8, half * 512:(half + 1) * 512],
                wsb[:, hc * 8:(hc + 1) * 8],
                xln[:, hc * T + half * 512:hc * T + (half + 1) * 512],
                start=(hc == 0), stop=(hc == NH - 1)),
                waits=([("w", wv), ("v", xln_ready_v)] + p.wof(g_ps)
                       if (hc == 0 and half == 0) else ()),
                inc=("t" if (hc == NH - 1 and half == 1) else None))
    a1 = p.op("scalar", lambda e: e.activation(
        sc1[:, :], g_ps[0:8, :], AF.Abs, bias=GATE_OFFSET),
        waits=[("t", tt)] + p.wof(sc1, sc2, sc3), inc="a")
    a2 = p.op("scalar", lambda e: e.activation(
        sc2[:, :], sc1[:, :], AF.Exp, scale=-1.0), waits=[("a", a1)], inc="a")
    a3 = p.op("scalar", lambda e: e.activation(
        sc1[:, :], sc2[:, :], AF.Ln, bias=1.0), waits=[("a", a2)], inc="a")
    v1 = p.op("vector", lambda e: e.tensor_scalar(
        sc3[:, :], g_ps[0:8, :], GATE_OFFSET, 0.0, ALU.add, ALU.min),
        waits=[("t", tt)], inc="v")
    v2 = p.op("vector", lambda e: e.tensor_tensor(
        mt[0:8, :], sc3[0:8, :], sc1[0:8, :], ALU.subtract),
        waits=[("v", v1), ("a", a3), ("v", mz)], inc="v")
    for ch in range(NT):
        tt2 = p.op("tensor", lambda e, ch=ch: e.transpose(
            t_ps[:, :], mt[:, ch * 128:(ch + 1) * 128], idf[:, :]),
            waits=([("v", v2), ("ld", idl)] + p.wof(t_ps) if ch == 0 else
                   [("v", p.cnt("v"))]), inc="t")
        p.op("vector", lambda e, ch=ch: e.tensor_copy(
            lgt[:, ch * 8:(ch + 1) * 8], t_ps[:, 0:8]),
            waits=[("t", tt2)], inc="v")
    lv = p.cnt("v")
    for ch in range(NT):
        p.op("sync", lambda e, ch=ch: e.dma_start(
            out=lgl_dram[ch * 128:(ch + 1) * 128, :],
            in_=lgt[:, ch * 8:(ch + 1) * 8]),
            waits=([("v", lv)] if ch == 0 else ()), inc="st", inc_amt=16)
    st = p.cnt("st")
    p.free(wsb, g_ps, sc1, sc2, sc3, mt, idf, lgt, t_ps)
    return st


def _yoff(k):
    return (k // 3) * 512 + (k % 3) * 129


def emit_attn1(p, x_dram, xout_dram, q_dram, sh_k1, sh_v1, sh_lg, lgl_dram,
               psel_dram, csel_dram, vb_dram, masks_dram,
               tri_dram, ident_dram, wc_dram, bc_dram, name, gate=()):
    nm = name
    NKV, NQ, D = 8, 16, 128
    ksb = p.sbuf(nm + "_k", [128, NKV * 2 * T], BF16)
    vsb = p.sbuf(nm + "_v", [128, 16 * NKV * 129], BF16)
    qsb = p.sbuf(nm + "_q", [128, NKV * NT * 2 * D], BF16)
    lgf = p.sbuf(nm + "_lgf", [128, 16 * 8], F32)
    lgo = p.sbuf(nm + "_lgo", [128, 8 * 8], F32)
    tri = p.sbuf(nm + "_tri", [128, 128], F32)
    onesf = p.sbuf(nm + "_1f", [128, 128], F32)
    psel = p.sbuf(nm + "_psel", [128, 16 * 8], F32)
    csel = p.sbuf(nm + "_csel", [128, 16], F32)
    vb = p.sbuf(nm + "_vb", [128, 8 * 16], F32)
    msk = p.sbuf(nm + "_msk", [128, 8 * 2 * 128], BF16)
    ident = p.sbuf(nm + "_id", [128, 128], BF16)
    cumg = p.sbuf(nm + "_cum", [128, 16 * 8], F32)
    ocum = p.sbuf(nm + "_ocum", [128, 8 * 8], F32)
    mrow = p.sbuf(nm + "_mrow", [1, 64], F32)
    crow = p.sbuf(nm + "_crow", [1, 8], F32)
    msb = p.sbuf(nm + "_msb", [8, 8], F32)
    mgrid = p.sbuf(nm + "_mg", [128, 64], F32)
    esf = p.sbuf(nm + "_esf", [128, 8 * 128], F32)
    ugrid = p.sbuf(nm + "_ug", [128, 64], F32)
    dsc = p.sbuf(nm + "_dsc", [128, 128], F32)
    ld0 = p.cnt("ld")
    first = p.wof(ksb, vsb, qsb, lgf, lgo, tri, onesf, psel, csel, vb, msk,
                  ident, cumg, ocum, mrow, crow, msb, mgrid, esf, ugrid, dsc)
    l = p.op("sync", lambda e: e.dma_start(out=tri[:, :], in_=tri_dram.ap()),
             waits=list(gate) + first, inc="ld", inc_amt=16)
    l = p.op("sync", lambda e: e.dma_start(out=ident[:, :], in_=ident_dram.ap()),
             inc="ld", inc_amt=16)
    vones = p.op("vector", lambda e: e.memset(
        vsb[:, :].rearrange("p (m d) -> p m d", d=129)[:, :, 128:129], 1.0),
        waits=list(gate) + p.wof(vsb), inc="v")
    for c in range(16):
        hh, r = c // 8, c % 8
        l = p.op("sync", lambda e, c=c, hh=hh, r=r: e.dma_start(
            out=lgf[:, c * 8:(c + 1) * 8],
            in_=sh_lg.ap()[hh, r * 128:(r + 1) * 128, :]),
            inc="ld", inc_amt=16)
        l = p.op("sync", lambda e, c=c: e.dma_start(
            out=psel[:, c * 8:(c + 1) * 8], in_=psel_dram[c]), inc="ld", inc_amt=16)
        l = p.op("sync", lambda e, c=c: e.dma_start(
            out=csel[:, c:c + 1], in_=csel_dram[c]), inc="ld", inc_amt=16)
        l = p.op("sync", lambda e, c=c, hh=hh, r=r: e.dma_start(
            out=vsb[:, c * NKV * 129:(c + 1) * NKV * 129]
            .rearrange("p (k d) -> p k d", d=129)[:, :, 0:128],
            in_=sh_v1.ap()[hh, r * 128:(r + 1) * 128, :]
            .rearrange("p (k d) -> p k d", d=128)),
            waits=([("v", vones)] if c == 0 else ()), inc="ld", inc_amt=16)
    for c in range(8):
        l = p.op("sync", lambda e, c=c: e.dma_start(
            out=lgo[:, c * 8:(c + 1) * 8],
            in_=lgl_dram[c * 128:(c + 1) * 128, :]),
            inc="ld", inc_amt=16)
        l = p.op("sync", lambda e, c=c: e.dma_start(
            out=vb[:, c * 16:(c + 1) * 16], in_=vb_dram[c]), inc="ld", inc_amt=16)
        l = p.op("sync", lambda e, c=c: e.dma_start(
            out=msk[:, c * 256:c * 256 + 128], in_=masks_dram[c, 0]),
            inc="ld", inc_amt=16)
        l = p.op("sync", lambda e, c=c: e.dma_start(
            out=msk[:, c * 256 + 128:(c + 1) * 256], in_=masks_dram[c, 1]),
            inc="ld", inc_amt=16)
    for kv in range(NKV):
        for half in range(2):
            l = p.op("sync", lambda e, kv=kv, half=half: e.dma_start(
                out=ksb[:, kv * 2 * T + half * T:kv * 2 * T + (half + 1) * T],
                in_=sh_k1.ap()[half, kv * 128:(kv + 1) * 128, :]),
                inc="ld", inc_amt=16)
    qv = qsb.ap().rearrange("p (k c q d) -> p k c q d", k=NKV, c=NT, q=2, d=D)
    for qh in range(NQ):
        kv, pp = qh // 2, qh % 2
        l = p.op("sync", lambda e, qh=qh, kv=kv, pp=pp: e.dma_start(
            out=qv[:, kv, :, pp, :], in_=q_dram[qh * 128:(qh + 1) * 128, :]
            .rearrange("p (c d) -> p c d", c=NT)), inc="ld", inc_amt=16)
    ldall = p.cnt("ld")
    p.op("vector", lambda e: e.memset(onesf[:, :], 1.0), inc="v")
    mset = p.cnt("v")

    c_ps = p.psum(nm + "_cps", [128, 8])
    m_ps = p.psum(nm + "_mps", [8, 8])
    cy_ps = p.psum(nm + "_cyps", [1, 8])
    tt = None
    for c in range(16):
        tt = p.op("tensor", lambda e, c=c: e.matmul(
            cy_ps[0:1, :], csel[:, c:c + 1], lgf[:, c * 8:(c + 1) * 8],
            start=(c == 0), stop=(c == 15)),
            waits=([("ld", ldall)] + p.wof(cy_ps) if c == 0 else ()),
            inc=("t" if c == 15 else None))
    vcy = p.op("vector", lambda e: e.tensor_copy(crow[0:1, :], cy_ps[0:1, :]),
               waits=[("t", tt)], inc="v")
    tt = None
    for c in range(16):
        tt = p.op("tensor", lambda e, c=c: e.matmul(
            m_ps[0:8, :], psel[:, c * 8:(c + 1) * 8], lgf[:, c * 8:(c + 1) * 8],
            start=(c == 0), stop=(c == 15)),
            waits=([("ld", ldall)] + p.wof(m_ps) if c == 0 else ()),
            inc=("t" if c == 15 else None))
    vm = p.op("vector", lambda e: e.tensor_copy(msb[:, :], m_ps[0:8, :]),
              waits=[("t", tt)], inc="v")
    mtmp = p.dram(nm + "_mtmp", [8, 8], F32)
    dm = p.op("sync", lambda e: e.dma_start(out=mtmp.ap(), in_=msb[:, :]),
              waits=[("v", vm)], inc="ld", inc_amt=16)
    dm = p.op("sync", lambda e: e.dma_start(
        out=mrow[0:1, :], in_=mtmp.ap().rearrange("a b -> (a b)")),
        waits=[("ld", dm)], inc="ld", inc_amt=16)
    mg_ps = p.psum(nm + "_mgps", [128, 64])
    tt = p.op("tensor", lambda e: e.matmul(
        mg_ps[:, :], onesf[0:1, 0:128], mrow[0:1, :], start=True, stop=True),
        waits=[("ld", dm), ("v", mset)] + p.wof(mg_ps), inc="t")
    vmg = p.op("vector", lambda e: e.tensor_copy(mgrid[:, :], mg_ps[:, :]),
               waits=[("t", p.cnt("t"))], inc="v")
    for c in range(16):
        tt = None
        for j in range(c):
            tt = p.op("tensor", lambda e, j=j: e.matmul(
                c_ps[:, :], onesf[:, :], lgf[:, j * 8:(j + 1) * 8],
                start=(j == 0), stop=False),
                waits=([("ld", ldall), ("v", p.cnt("v"))] + p.wof(c_ps)
                       if (c == 0 and j == 0) else
                       ([("v", p.cnt("v"))] if j == 0 else ())))
        tt = p.op("tensor", lambda e, c=c: e.matmul(
            c_ps[:, :], tri[:, :], lgf[:, c * 8:(c + 1) * 8],
            start=(c == 0), stop=True),
            waits=([("ld", ldall), ("v", p.cnt("v"))] + p.wof(c_ps)
                   if c == 0 else ([("v", p.cnt("v"))] if c == 0 else ())),
            inc="t")
        p.op("vector", lambda e, c=c: e.tensor_copy(
            cumg[:, c * 8:(c + 1) * 8], c_ps[:, :]),
            waits=[("t", p.cnt("t"))], inc="v")
    for c in range(8):
        tt = None
        for j in range(c):
            tt = p.op("tensor", lambda e, j=j, c=c: e.matmul(
                c_ps[:, :], onesf[:, :], lgo[:, j * 8:(j + 1) * 8],
                start=(j == 0), stop=False),
                waits=([("v", p.cnt("v"))] if j == 0 else ()))
        tt = p.op("tensor", lambda e, c=c: e.matmul(
            c_ps[:, :], tri[:, :], lgo[:, c * 8:(c + 1) * 8],
            start=(c == 0), stop=False),
            waits=([("v", p.cnt("v"))] if c == 0 else ()))
        tt = p.op("tensor", lambda e: e.matmul(
            c_ps[:, :], onesf[0:1, 0:128], crow[0:1, :],
            start=False, stop=True), waits=[("v", vcy)], inc="t")
        p.op("vector", lambda e, c=c: e.tensor_copy(
            ocum[:, c * 8:(c + 1) * 8], c_ps[:, :]),
            waits=[("t", p.cnt("t"))], inc="v")
    cum_v = p.cnt("v")
    for i in range(8):
        v1 = p.op("vector", lambda e, i=i: e.tensor_tensor(
            dsc[:, :].rearrange("p (c h) -> p c h", c=16),
            _bcast_mid(mgrid[:, i * 8:(i + 1) * 8], 16),
            cumg[:, :].rearrange("p (c h) -> p c h", c=16), ALU.subtract),
            waits=([("v", max(cum_v, vmg))] if i == 0 else
                   [("a", p.cnt("a"))]), inc="v")
        v2 = p.op("vector", lambda e, i=i: e.tensor_scalar(
            dsc[:, :], dsc[:, :], 60.0, None, ALU.min),
            waits=[("v", v1)], inc="v")
        a1 = p.op("scalar", lambda e, i=i: e.activation(
            esf[:, i * 128:(i + 1) * 128], dsc[:, :], AF.Exp),
            waits=[("v", v2)] + (p.wof() if i else []), inc="a")
        v3 = p.op("vector", lambda e, i=i: e.tensor_tensor(
            esf[:, i * 128:(i + 1) * 128].rearrange("p (c h) -> p c h", c=16),
            esf[:, i * 128:(i + 1) * 128].rearrange("p (c h) -> p c h", c=16),
            _bcast_mid(vb[:, i * 16:(i + 1) * 16], 8, pos=2),
            ALU.mult), waits=[("a", a1)], inc="v")
        v4 = p.op("vector", lambda e, i=i: e.tensor_tensor(
            dsc[:, 0:8], ocum[:, i * 8:(i + 1) * 8],
            mgrid[:, i * 8:(i + 1) * 8], ALU.subtract),
            waits=[("v", v3)], inc="v")
        a2 = p.op("scalar", lambda e, i=i: e.activation(
            ugrid[:, i * 8:(i + 1) * 8], dsc[:, 0:8], AF.Exp),
            waits=[("v", v4)], inc="a")
    prep_a = p.cnt("a")
    prep_v = p.cnt("v")
    p.free(c_ps, m_ps, cy_ps, mg_ps, psel, csel, tri, lgf, lgo, dsc, msb,
           mrow, crow, mgrid)

    y_tm = p.sbuf(nm + "_ytm", [128, 8 * NQ * D], BF16)
    sqb = p.sbuf(nm + "_sq", [128, 2 * 1024], BF16)
    atb = p.sbuf(nm + "_at", [128, 2 * 1024], BF16)
    nsc = p.sbuf(nm + "_nsc", [128, 4 * 8], F32)
    s_ps = [p.psum(nm + "_sps%d" % i, [128, 1024]) for i in range(2)]
    y_ps = p.psum(nm + "_yps", [128, 1536])
    slot = 0
    acts = {}
    pend_av = None
    for grp in range(2):
        for i in range(8):
            for j in range(9 + i):
                par = slot % 2
                tw = [("a", prep_a), ("v", prep_v), ("ld", ldall)]
                if slot == 0:
                    tw += p.wof(*s_ps, y_ps)
                if slot >= 2:
                    tw.append(("a", acts[slot - 2]))
                for kk in range(4):
                    kvh = grp * 4 + kk
                    p.op("tensor", lambda e, i=i, j=j, kvh=kvh, kk=kk, par=par: e.matmul(
                        s_ps[par][:, kk * 256:(kk + 1) * 256],
                        ksb[:, kvh * 2 * T + j * 128:kvh * 2 * T + (j + 1) * 128],
                        qsb[:, kvh * NT * 256 + i * 256:kvh * NT * 256 + (i + 1) * 256],
                        start=True, stop=True),
                        waits=(tw if kk == 0 else ()),
                        inc=("t" if kk == 3 else None))
                tqk = p.cnt("t")
                if pend_av is not None:
                    pend_av()
                    pend_av = None
                aa = p.op("scalar", lambda e, par=par: e.activation(
                    sqb[:, par * 1024:(par + 1) * 1024], s_ps[par][:, :],
                    AF.Square, scale=0.08838834764831845),
                    waits=[("t", tqk)] +
                          (p.wof(sqb, atb, nsc, y_tm) if slot == 0 else []) +
                          ([("v", p.cnt("v"))] if slot >= 2 else []),
                    inc="a")
                acts[slot] = aa
                vv = p.op("vector", lambda e, i=i, j=j, grp=grp, par=par: e.tensor_tensor(
                    atb[:, par * 1024:(par + 1) * 1024]
                    .rearrange("p (k t) -> p k t", k=4),
                    sqb[:, par * 1024:(par + 1) * 1024]
                    .rearrange("p (k t) -> p k t", k=4),
                    _bcast_mid(esf[:, i * 128 + j * 8 + grp * 4:
                                   i * 128 + j * 8 + grp * 4 + 4], 256, pos=2),
                    ALU.mult),
                    waits=[("a", aa)] + ([("t", p.cnt("t"))] if slot >= 2 else []),
                    inc="v")
                if j == i or j == i + 8:
                    sel = 0 if j == i else 1
                    vv = p.op("vector", lambda e, i=i, sel=sel, par=par: e.tensor_tensor(
                        atb[:, par * 1024:(par + 1) * 1024]
                        .rearrange("p (k t) -> p k t", k=8),
                        atb[:, par * 1024:(par + 1) * 1024]
                        .rearrange("p (k t) -> p k t", k=8),
                        _bcast_mid(msk[:, i * 256 + sel * 128:
                                       i * 256 + (sel + 1) * 128], 8),
                        ALU.mult), waits=[("v", vv)], inc="v")

                def make_av(i=i, j=j, grp=grp, par=par, vv=vv):
                    def emit():
                        for kk in range(4):
                            kvh = grp * 4 + kk
                            for pp in range(2):
                                gi = kk * 2 + pp
                                p.op("tensor", lambda e, i=i, j=j, kvh=kvh,
                                     kk=kk, pp=pp, gi=gi, par=par: e.matmul(
                                    y_ps[:, _yoff(gi):_yoff(gi) + 129],
                                    atb[:, par * 1024 + (kk * 2 + pp) * 128:
                                        par * 1024 + (kk * 2 + pp + 1) * 128],
                                    vsb[:, j * NKV * 129 + kvh * 129:
                                        j * NKV * 129 + (kvh + 1) * 129],
                                    start=(j == 0 and gi in (0, 3, 6)),
                                    stop=(j == 8 + i)),
                                    waits=([("v", vv)] if (kk == 0 and pp == 0)
                                           else ()),
                                    inc=("t" if (kk == 3 and pp == 1) else None))
                    return emit
                pend_av = make_av()
                slot += 1
            pend_av()
            pend_av = None
            p.op("tensor", lambda e: e.matmul(
                y_ps[0:1, 1535:1536], onesf[0:1, 0:1], onesf[0:1, 0:1],
                start=True, stop=True), inc="t")
            tav = p.cnt("t")
            for gi2 in range(8):
                p.op("vector", lambda e, gi2=gi2: e.tensor_copy(
                    nsc[:, gi2:gi2 + 1], y_ps[:, _yoff(gi2) + 128:_yoff(gi2) + 129]),
                    waits=([("t", tav)] if gi2 == 0 else ()), inc="v")
            vd = p.op("vector", lambda e, i=i, grp=grp: e.tensor_tensor(
                nsc[:, 8:16].rearrange("p (k q) -> p k q", k=4),
                nsc[:, 0:8].rearrange("p (k q) -> p k q", k=4),
                _bcast_mid(ugrid[:, i * 8 + grp * 4:i * 8 + grp * 4 + 4], 2,
                           pos=2), ALU.mult), waits=[("v", p.cnt("v"))], inc="v")
            vd = p.op("vector", lambda e: e.tensor_scalar(
                nsc[:, 8:16], nsc[:, 8:16], 1e-6, None, ALU.add),
                waits=[("v", vd)], inc="v")
            vd = p.op("vector", lambda e: e.reciprocal(
                nsc[:, 16:24], nsc[:, 8:16]), waits=[("v", vd)], inc="v")
            vd = p.op("vector", lambda e, i=i, grp=grp: e.tensor_tensor(
                nsc[:, 24:32].rearrange("p (k q) -> p k q", k=4),
                nsc[:, 16:24].rearrange("p (k q) -> p k q", k=4),
                _bcast_mid(ugrid[:, i * 8 + grp * 4:i * 8 + grp * 4 + 4], 2,
                           pos=2), ALU.mult), waits=[("v", vd)], inc="v")
            for gi2 in range(8):
                qh = grp * 8 + gi2
                vd = p.op("vector", lambda e, i=i, qh=qh, gi2=gi2: e.tensor_scalar(
                    y_tm[:, i * 2048 + qh * 128:i * 2048 + (qh + 1) * 128],
                    y_ps[:, _yoff(gi2):_yoff(gi2) + 128],
                    nsc[:, 24 + gi2:25 + gi2], None, ALU.mult),
                    waits=([("v", vd)] if gi2 == 0 else ()), inc="v")
    ydone_v = p.cnt("v")
    p.free(sqb, atb, nsc, *s_ps, y_ps, ksb, qsb, vsb, esf, ugrid, cumg, ocum,
           vb, msk)

    yfm = p.sbuf(nm + "_yfm", [128, NQ * T], BF16)
    t_ps = [p.psum(nm + "_tps%d" % i2, [128, 128], BF16) for i2 in range(2)]
    k = 0
    for qh in range(NQ):
        for i in range(8):
            tt = p.op("tensor", lambda e, qh=qh, i=i, k=k: e.transpose(
                t_ps[k % 2][:, :],
                y_tm[:, i * 2048 + qh * 128:i * 2048 + (qh + 1) * 128],
                ident[:, :]),
                waits=([("v", ydone_v)] + p.wof(*t_ps, yfm) if k == 0 else
                       ([("v", p.cnt("v"))] if k >= 2 else ())),
                inc="t")
            p.op("vector", lambda e, qh=qh, i=i, k=k: e.tensor_copy(
                yfm[:, qh * T + i * 128:qh * T + (i + 1) * 128],
                t_ps[k % 2][:, :]),
                waits=[("t", p.cnt("t"))], inc="v")
            k += 1
    yfm_v = p.cnt("v")
    p.free(y_tm, *t_ps, ident)

    wsb = p.sbuf(nm + "_wc", [128, NH * H], BF16)
    bcs = p.sbuf(nm + "_bc", [128, 16], F32)
    w_ps = [p.psum(nm + "_wps%d" % i2, [128, T]) for i2 in range(2)]
    xres = p.sbuf(nm + "_xr", [128, 2 * T], F32)
    xo = p.sbuf(nm + "_xo", [128, 2 * T], F32)
    wv = None
    for hc in range(NH):
        wv = p.op("gpsimd", lambda e, hc=hc: e.dma_start(
            out=wsb[:, hc * H:(hc + 1) * H], in_=wc_dram[hc]),
            waits=(p.wof(wsb) if hc == 0 else ()), inc="w", inc_amt=16)
    bl = p.op("sync", lambda e: e.dma_start(out=bcs[:, :], in_=bc_dram.ap()),
              waits=p.wof(bcs), inc="ld", inc_amt=16)
    st_base = p.cnt("st")
    for oc in range(NH):
        par = oc % 2
        lr = p.op("sync", lambda e, oc=oc, par=par: e.dma_start(
            out=xres[:, par * T:(par + 1) * T],
            in_=x_dram[oc * 128:(oc + 1) * 128, :]),
            waits=(p.wof(xres, xo) if oc == 0 else
                   ([("v", p.cnt("v"))] if oc >= 2 else [])),
            inc="ld", inc_amt=16)
        tw = [("w", wv), ("v", yfm_v)]
        if oc == 0:
            tw += p.wof(*w_ps)
        if oc >= 2:
            tw.append(("v", p.cnt("v")))
        for qc in range(NQ):
            for half in range(2):
                p.op("tensor", lambda e, oc=oc, par=par, qc=qc, half=half: e.matmul(
                    w_ps[par][:, half * 512:(half + 1) * 512],
                    wsb[:, qc * H + oc * 128:qc * H + (oc + 1) * 128],
                    yfm[:, qc * T + half * 512:qc * T + (half + 1) * 512],
                    start=(qc == 0), stop=(qc == NQ - 1)),
                    waits=(tw if (qc == 0 and half == 0) else ()),
                    inc=("t" if (qc == NQ - 1 and half == 1) else None))
        tdd = p.cnt("t")
        vw = [("t", tdd), ("ld", max(lr, bl))]
        if oc >= 2:
            vw.append(("st", st_base + 16 * (oc - 1)))
        vo = p.op("vector", lambda e, oc=oc, par=par: e.scalar_tensor_tensor(
            xo[:, par * T:(par + 1) * T], w_ps[par][:, :], bcs[:, oc:oc + 1],
            xres[:, par * T:(par + 1) * T], ALU.add, ALU.add),
            waits=vw, inc="v")
        p.op("sync", lambda e, oc=oc, par=par: e.dma_start(
            out=xout_dram[oc * 128:(oc + 1) * 128, :],
            in_=xo[:, par * T:(par + 1) * T]), waits=[("v", vo)],
            inc="st", inc_amt=16)
    st_done = p.cnt("st")
    p.free(yfm, wsb, bcs, xres, xo, *w_ps)
    return st_done



def build_fused():
    p = Prog()
    x_in = p.dram("x_in", [H, T], F32, kind="ExternalInput")
    wg0 = p.dram("wg0", [NI0, 128, NH * 128], I32, kind="ExternalInput")
    wu0 = p.dram("wu0", [NI0, 128, NH * 128], I32, kind="ExternalInput")
    wd0 = p.dram("wd0", [NH, 128, NI0 * 128], I32, kind="ExternalInput")
    sg0 = p.dram("sg0", [128, NI0], F32, kind="ExternalInput")
    su0 = p.dram("su0", [128, NI0], F32, kind="ExternalInput")
    sd0 = p.dram("sd0", [128, NH], F32, kind="ExternalInput")
    wq0 = p.dram("wq0", [NH, 128, H], F32, kind="ExternalInput")
    wk0 = p.dram("wk0", [NH, 128, H], F32, kind="ExternalInput")
    wv0 = p.dram("wv0", [NH, 128, H], F32, kind="ExternalInput")
    wo0 = p.dram("wo0", [NH, 128, H], F32, kind="ExternalInput")
    wg1 = p.dram("wg1", [NI1, 128, NH * 128], I32, kind="ExternalInput")
    wu1 = p.dram("wu1", [NI1, 128, NH * 128], I32, kind="ExternalInput")
    wd1 = p.dram("wd1", [NH, 128, NI1 * 128], I32, kind="ExternalInput")
    sg1 = p.dram("sg1", [128, NI1], F32, kind="ExternalInput")
    su1 = p.dram("su1", [128, NI1], F32, kind="ExternalInput")
    sd1 = p.dram("sd1", [128, NH], F32, kind="ExternalInput")
    wq1 = p.dram("wq1", [NH, 128, H], F32, kind="ExternalInput")
    wk1 = p.dram("wk1", [NH, 128, 1024], F32, kind="ExternalInput")
    wv1 = p.dram("wv1", [NH, 128, 1024], F32, kind="ExternalInput")
    wgg1 = p.dram("wgg1", [NH, 128, 8], F32, kind="ExternalInput")
    wc = p.dram("wc", [NH, 128, H], F32, kind="ExternalInput")
    bc = p.dram("bc", [128, 16], F32, kind="ExternalInput")
    ident = p.dram("ident", [128, 128], BF16, kind="ExternalInput")
    identf = p.dram("identf", [128, 128], F32, kind="ExternalInput")
    trif = p.dram("trif", [128, 128], F32, kind="ExternalInput")
    psel = p.dram("psel", [16, 128, 8], F32, kind="ExternalInput")
    csel = p.dram("csel", [16, 128, 1], F32, kind="ExternalInput")
    vb = p.dram("vb", [8, 128, 16], F32, kind="ExternalInput")
    masks = p.dram("masks", [8, 2, 128, 128], BF16, kind="ExternalInput")
    nflags = p.dram("nflags", [1, 8], I32, kind="ExternalInput")
    out = p.dram("out", [H, T], F32, kind="ExternalOutput")
    x1 = p.dram("x1", [H, T], F32)
    x2 = p.dram("x2", [H, T], F32)
    x3 = p.dram("x3", [H, T], F32)
    x4 = p.dram("x4", [H, T], F32)
    x5 = p.dram("x5", [H, T], F32)
    v0 = p.dram("v0", [H, T], BF16)
    q1d = p.dram("q1d", [H, T], BF16)
    q0l = p.dram("q0l", [T, H], BF16)
    k0l = p.dram("k0l", [T, H], BF16)
    k1l = p.dram("k1l", [1024, T], BF16)
    v1l = p.dram("v1l", [T, 1024], BF16)
    lgl = p.dram("lgl", [T, 8], F32)
    shq0 = p.dram("shq0", [2, T, H], BF16, addr_space="Shared")
    shk0 = p.dram("shk0", [2, T, H], BF16, addr_space="Shared")
    shk1 = p.dram("shk1", [2, 1024, T], BF16, addr_space="Shared")
    shv1 = p.dram("shv1", [2, T, 1024], BF16, addr_space="Shared")
    shlg = p.dram("shlg", [2, T, 8], F32, addr_space="Shared")
    shf = p.dram("shf", [2, 8], I32, addr_space="Shared")

    nf = p.sbuf("nfb", [1, 8], I32)
    ft = p.sbuf("ftb", [1, 8], I32)
    p.op("gpsimd", lambda e: e.memset(ft[0:1, :], 0),
         waits=p.wof(nf, ft), inc="w")
    nfw = p.op("gpsimd", lambda e: e.dma_start(
        out=nf[0:1, :], in_=nflags.ap()), inc="w", inc_amt=16)

    emit_ffn(p, x_in, x1, wg0, wu0, wd0, sg0, su0, sd0, NI0, "f0a")
    p.wait("sync", ("st", p.cnt("st")))
    p.barrier()
    xln = p.sbuf("l1_xln", [128, NH * T], BF16)
    lnv = emit_ln(p, x1, xln, "l1_ln")
    emit_proj(p, xln, wq0, q0l, H, "tm", "pq", lnv)
    stq0 = p.ds_copy(shq0, q0l, waits=[("st", p.cnt("st")), ("w", nfw)])
    emit_proj(p, xln, wk0, k0l, H, "tm", "pk", lnv)
    stqk = p.ds_copy(shk0, k0l, waits=[("st", p.cnt("st"))])
    p.pair_signal(0, shf, nf, waits=[("st", stqk)])
    emit_proj(p, xln, wv0, v0, H, "fm", "pv", lnv)
    pollA = p.pair_poll(0, shf, nf, ft, waits=[("w", nfw)])
    p.free(xln)
    p.wait("sync", ("st", p.cnt("st")))
    p.barrier()
    emit_attn0(p, x1, x2, shq0, shk0, v0, wo0, ident, "a0",
               gate=[("pp", pollA)])
    p.wait("sync", ("st", p.cnt("st")))
    p.barrier()
    emit_ffn(p, x2, x3, wg0, wu0, wd0, sg0, su0, sd0, NI0, "f0b")
    p.wait("sync", ("st", p.cnt("st")))
    p.barrier()
    emit_ffn(p, x3, x4, wg1, wu1, wd1, sg1, su1, sd1, NI1, "f1a")
    p.wait("sync", ("st", p.cnt("st")))
    p.barrier()
    xln2 = p.sbuf("l2_xln", [128, NH * T], BF16)
    lnv2 = emit_ln(p, x4, xln2, "l2_ln")
    emit_proj(p, xln2, wq1, q1d, H, "fm", "pq1", lnv2)
    emit_proj(p, xln2, wk1, k1l, 1024, "fm", "pk1", lnv2)
    p.ds_copy(shk1, k1l, waits=[("st", p.cnt("st"))])
    emit_proj(p, xln2, wv1, v1l, 1024, "tm", "pv1", lnv2)
    p.ds_copy(shv1, v1l, waits=[("st", p.cnt("st"))])
    emit_proj_g(p, xln2, wgg1, lgl, identf, "pg1", lnv2)
    lgst = p.ds_copy(shlg, lgl, waits=[("st", p.cnt("st"))])
    p.pair_signal(1, shf, nf, waits=[("st", lgst)])
    pollB = p.pair_poll(1, shf, nf, ft, waits=[("w", nfw)])
    p.free(xln2)
    p.wait("sync", ("st", p.cnt("st")))
    p.barrier()
    emit_attn1(p, x4, x5, q1d, shk1, shv1, shlg, lgl, psel, csel, vb, masks,
               trif, ident, wc, bc, "a1", gate=[("pp", pollB)])
    p.wait("sync", ("st", p.cnt("st")))
    p.barrier()
    emit_ffn(p, x5, out, wg1, wu1, wd1, sg1, su1, sd1, NI1, "f1b")
    p.wait("sync", ("st", p.cnt("st")), ("w", p.cnt("w")), ("ld", p.cnt("ld")))
    return p.finalize()



def _block_lhsT(w_t, nin, nout):
    NK, NO = nin // 128, nout // 128
    a = w_t.reshape(NK, 128, NO, 128)
    return np.ascontiguousarray(a.transpose(2, 1, 0, 3).reshape(NO, 128, NK * 128))


def _scale_cols(s):
    return np.ascontiguousarray(np.asarray(s).reshape(-1, 128).T)


def _nat_chunks(w):
    w = np.asarray(w)
    return np.ascontiguousarray(w.reshape(w.shape[0] // 128, 128, w.shape[1]))


def _shard_x(x):
    out = []
    for b in range(4):
        for hhalf in range(2):
            out.append(np.ascontiguousarray(
                x[b, hhalf * T:(hhalf + 1) * T, :].T))
    return out


_progs = {}

import os as _os
_NONCE_BASE = (int.from_bytes(_os.urandom(4), "little") & 0x1fffffff) | (1 << 24)
_nonce_ctr = 0


def _get_prog(name):
    if name not in _progs:
        _progs[name] = {"fused": build_fused}[name]()
    return _progs[name]


def _prep_weights(inputs):
    w = {}
    w["wg0"] = _block_lhsT(np.ascontiguousarray(np.asarray(inputs["L0_ffn_gate_w"]).T), H, I0)
    w["wu0"] = _block_lhsT(np.ascontiguousarray(np.asarray(inputs["L0_ffn_up_w"]).T), H, I0)
    w["wd0"] = _block_lhsT(np.ascontiguousarray(np.asarray(inputs["L0_ffn_down_w"]).T), I0, H)
    w["sg0"] = _scale_cols(inputs["L0_ffn_gate_scale"])
    w["su0"] = _scale_cols(inputs["L0_ffn_up_scale"])
    w["sd0"] = _scale_cols(inputs["L0_ffn_down_scale"])
    w["wq0"] = _nat_chunks(inputs["L0_attn_Wq"])
    w["wk0"] = _nat_chunks(inputs["L0_attn_Wk"])
    w["wv0"] = _nat_chunks(inputs["L0_attn_Wv"])
    w["wo0"] = _nat_chunks(inputs["L0_attn_Wo"])
    w["wg1"] = _block_lhsT(np.ascontiguousarray(np.asarray(inputs["L1_ffn_wq_gate"]).T), H, I1)
    w["wu1"] = _block_lhsT(np.ascontiguousarray(np.asarray(inputs["L1_ffn_wq_up"]).T), H, I1)
    w["wd1"] = _block_lhsT(np.ascontiguousarray(np.asarray(inputs["L1_ffn_wq_down"]).T), I1, H)
    w["sg1"] = _scale_cols(inputs["L1_ffn_s_gate"])
    w["su1"] = _scale_cols(inputs["L1_ffn_s_up"])
    w["sd1"] = _scale_cols(inputs["L1_ffn_s_down"])
    w["wq1"] = _nat_chunks(inputs["L1_attn_Wq"])
    w["wk1"] = _nat_chunks(inputs["L1_attn_Wk"])
    w["wv1"] = _nat_chunks(inputs["L1_attn_Wv"])
    w["wgg1"] = _nat_chunks(inputs["L1_attn_Wg"])
    w["wc"] = _nat_chunks(inputs["L1_attn_Wc"])
    w["bc"] = _scale_cols(inputs["L1_attn_bc"])
    return w


def _const_inputs():
    c = {}
    c["ident_f32"] = np.eye(128, dtype=np.float32)
    c["ident_bf16"] = np.eye(128, dtype=np.float32).astype(BF)
    c["trif"] = np.triu(np.ones((128, 128), dtype=np.float32))
    half = {}
    for hh in range(2):
        d = {}
        ps = np.zeros((16, 128, 8), dtype=np.float32)
        for i in range(8):
            lim = (8 * hh + i) * 128
            for cch in range(16):
                for s in range(128):
                    if cch * 128 + s <= lim:
                        ps[cch, s, i] = 1.0
        d["psel"] = ps
        cs = np.zeros((16, 128, 1), dtype=np.float32)
        cs[:8 * hh] = 1.0
        d["csel"] = cs
        vbm = np.zeros((8, 128, 16), dtype=np.float32)
        for i in range(8):
            gi = 8 * hh + i
            vbm[i, :, :gi + 1] = 1.0
        d["vb"] = vbm
        mk = np.zeros((8, 2, 128, 128), dtype=np.float32)
        tri = np.triu(np.ones((128, 128), dtype=np.float32))
        for i in range(8):
            if hh == 0:
                mk[i, 0] = tri
                mk[i, 1] = 0.0
            else:
                mk[i, 0] = 1.0
                mk[i, 1] = tri
        d["masks"] = mk.astype(BF)
        half[hh] = d
    c["half"] = half
    return c


def kernel(**inputs):
    inputs = {k: np.asarray(v) for k, v in inputs.items()}
    x = inputs["L0_attn_x"]
    w = _prep_weights(inputs)
    cst = _const_inputs()
    xs = _shard_x(x)
    cores = list(range(8))

    global _nonce_ctr
    _nonce_ctr += 16
    nonce = (_NONCE_BASE + _nonce_ctr) & 0x3fffffff
    nflags = np.zeros((1, 8), np.int32)
    nflags[0, 0] = nonce + 1
    nflags[0, 1] = nonce + 2

    nc = _get_prog("fused")
    in_maps = []
    for c in cores:
        hh = c % 2
        d = cst["half"][hh]
        in_maps.append(dict(
            x_in=xs[c],
            wg0=w["wg0"], wu0=w["wu0"], wd0=w["wd0"],
            sg0=w["sg0"], su0=w["su0"], sd0=w["sd0"],
            wq0=w["wq0"], wk0=w["wk0"], wv0=w["wv0"], wo0=w["wo0"],
            wg1=w["wg1"], wu1=w["wu1"], wd1=w["wd1"],
            sg1=w["sg1"], su1=w["su1"], sd1=w["sd1"],
            wq1=w["wq1"], wk1=w["wk1"], wv1=w["wv1"], wgg1=w["wgg1"],
            wc=w["wc"], bc=w["bc"],
            ident=cst["ident_bf16"], identf=cst["ident_f32"],
            trif=cst["trif"],
            psel=d["psel"], csel=d["csel"], vb=d["vb"], masks=d["masks"],
            nflags=nflags))
    res = run_bass_kernel_spmd(nc, in_maps, core_ids=cores).results

    out = np.empty((4, 2048, H), dtype=np.float32)
    for c in cores:
        b, hh = c // 2, c % 2
        out[b, hh * T:(hh + 1) * T, :] = res[c]["out"].T
    return out


# version 07
# speedup vs baseline: 1.1748x; 1.1748x over previous
import numpy as np
import ml_dtypes
from contextlib import ExitStack

import concourse.bass as bass
import concourse.mybir as mybir
from concourse.engine_type import EngineType
from concourse.bass_utils import run_bass_kernel_spmd

F32 = mybir.dt.float32
BF16 = mybir.dt.bfloat16
I32 = mybir.dt.int32
AF = mybir.ActivationFunctionType
ALU = mybir.AluOpType
BF = ml_dtypes.bfloat16

H = 2048
T = 1024
NH = H // 128
NT = T // 128
I0, I1 = 8192, 5632
NI0, NI1 = I0 // 128, I1 // 128
GATE_OFFSET = 6.906768
ENGINES = ("tensor", "vector", "scalar", "gpsimd", "sync")


class Prog:

    SBUF_SIZE = 212736
    PSUM_BANKS = 8

    def __init__(self):
        self.nc = bass.Bass("TRN2", target_bir_lowering=False)
        self.es = ExitStack()
        self.sched = []
        self.sems = {}
        self._const_vals = {}
        self._sb_arena = self.nc.alloc_sbuf_tensor(
            "__arena", [128, self.SBUF_SIZE], mybir.dt.uint8)
        self._sb_base = self.nc.lookup_mloc(self._sb_arena).addr
        self._sb_free = [(0, self.SBUF_SIZE, {})]
        self._ps_arena = self.nc.alloc_psum_tensor(
            "__psarena", [128, 4096], F32)
        self._ps_base = self.nc.lookup_mloc(self._ps_arena).addr
        self._ps_free = [(0, self.PSUM_BANKS, {})]
        self._alloc_info = {}
        self._waits = {}
        self.sb_peak = 0
        self._const_tile = self.sbuf("__consts", [128, 32], F32)

    @staticmethod
    def _take(freelist, need):
        for i, (start, size, conds) in enumerate(freelist):
            if size >= need:
                if size == need:
                    freelist.pop(i)
                else:
                    freelist[i] = (start + need, size - need, conds)
                return start, conds
        raise MemoryError(f"arena OOM: need {need}, free={freelist}")

    @staticmethod
    def _give(freelist, start, size, conds):
        freelist.append((start, size, conds))
        freelist.sort()
        merged = []
        for s, n, c in freelist:
            if merged and merged[-1][0] + merged[-1][1] == s:
                ps, pn, pc = merged[-1]
                mc = dict(pc)
                for k, v in c.items():
                    mc[k] = max(mc.get(k, 0), v)
                merged[-1] = (ps, pn + n, mc)
            else:
                merged.append((s, n, c))
        freelist[:] = merged

    def _place(self, h, addr):
        self.nc._tensor(h.name, h.shape, h.dtype, type="SB")
        mloc = self.nc.lookup_mloc(h)
        mloc.allocated = True
        mloc.addr = addr

    def parity(self):
        if not hasattr(self, "_parity"):
            pid = self.nc.partition_id([EngineType.Pool])
            self._parity = (pid % 2, (pid + 1) % 2)
        return self._parity

    def ds_copy(self, sh, local, waits, name=""):
        def cp(eng):
            slot, _ = self.parity()
            return eng.dma_start(out=sh.ap()[bass.ds(slot, 1)],
                                 in_=local.ap())
        self.op("gpsimd", cp, waits=waits, inc="st", inc_amt=16)
        return self.cnt("st")

    def pair_signal(self, idx, shf, nf, waits):
        def sig(eng):
            slot, _ = self.parity()
            return eng.dma_start(
                out=shf.ap()[bass.ds(slot, 1), idx:idx + 1],
                in_=nf[0:1, idx:idx + 1])
        self.op("gpsimd", sig, waits=waits, inc="st", inc_amt=16)

    def pair_poll(self, idx, shf, nf, ft, waits=()):
        nc = self.nc
        fp = self.sem("fpoll")[0]

        def poll(eng):
            _, other = self.parity()
            if not hasattr(self, "_pregs"):
                self._pregs = (
                    nc.alloc_registers("ptgt", engines=[EngineType.Pool]),
                    nc.alloc_registers("pcur", engines=[EngineType.Pool]),
                    nc.alloc_registers("pcnt", engines=[EngineType.Pool]))
                self._pscr = self.sbuf("_pollscr", [1, 2048], F32)
            tgt, cur, cnt = self._pregs
            scr = self._pscr
            nc.regs_load(tgt, nf[0:1, idx:idx + 1])
            nc.regs_mov(cnt, 0)
            loop, bodyb, done = f"pl{idx}", f"plb{idx}", f"pld{idx}"
            nc.br(loop, engines=[EngineType.Pool])
            with nc.body(loop, valid_engines=[EngineType.Pool]):
                eng.dma_start(
                    out=ft[0:1, idx:idx + 1],
                    in_=shf.ap()[bass.ds(other, 1), idx:idx + 1]
                ).then_inc(fp, 16)
                for _ in range(4):
                    eng.memset(scr[0:1, :], 0.0)
                for _ in range(24):
                    nc.regs_alu(cnt, cnt, 1, op=mybir.AluOpType.add)
                nc.regs_load(cur, ft[0:1, idx:idx + 1])
                nc.br_cmp(cur, tgt, done, bodyb, "IS_EQ",
                          engines=[EngineType.Pool])
                with nc.body(bodyb, valid_engines=[EngineType.Pool]):
                    nc.br_cmp(cnt, 2000000, loop, done, "IS_LT",
                              engines=[EngineType.Pool])
            nc.switch_bb(done)
            return None
        return self.op("gpsimd", poll, waits=list(waits), inc="pp")

    def sem(self, name):
        if name not in self.sems:
            h = self.es.enter_context(self.nc.semaphore(name))
            self.sems[name] = [h, 0]
        return self.sems[name]

    def cnt(self, name):
        return self.sems[name][1] if name in self.sems else 0

    def sbuf(self, name, shape, dtype):
        nbytes = int(np.prod(shape[1:])) * mybir.dt.size(dtype)
        nbytes = (nbytes + 63) & ~63
        start, conds = self._take(self._sb_free, nbytes)
        h = self.nc.virtual_sbuf_tensor(name, list(shape), dtype)
        self._place(h, self._sb_base + start)
        self._alloc_info[id(h)] = ("sb", start, nbytes)
        self._waits[id(h)] = conds
        used = self.SBUF_SIZE - sum(e[1] for e in self._sb_free)
        self.sb_peak = max(self.sb_peak, used)
        return h

    def psum(self, name, shape, dtype=F32):
        nbytes = int(np.prod(shape[1:])) * mybir.dt.size(dtype)
        banks = (nbytes + 2047) // 2048
        start, conds = self._take(self._ps_free, banks)
        h = self.nc.virtual_psum_tensor(name, list(shape), dtype)
        self.nc._tensor(h.name, h.shape, h.dtype, type="PSUM")
        mloc = self.nc.lookup_mloc(h)
        mloc.allocated = True
        mloc.addr = self._ps_base + start * 2048
        self._alloc_info[id(h)] = ("ps", start, banks)
        self._waits[id(h)] = conds
        return h

    def wof(self, *handles):
        out = {}
        for h in handles:
            for k, v in self._waits.get(id(h), {}).items():
                out[k] = max(out.get(k, 0), v)
        return list(out.items())

    def free(self, *handles):
        snap = {n: s[1] for n, s in self.sems.items() if n != "BAR"}
        for h in handles:
            kind, start, size = self._alloc_info.pop(id(h))
            self._waits.pop(id(h), None)
            self._give(self._sb_free if kind == "sb" else self._ps_free,
                       start, size, snap)

    def dram(self, name, shape, dtype, kind="Internal", addr_space="Local"):
        return self.nc.dram_tensor(name, shape, dtype, kind=kind,
                                   addr_space=addr_space)

    def op(self, engine, fn, waits=(), inc=None, inc_amt=1):
        if inc is not None:
            s = self.sem(inc)
            s[1] += inc_amt
            inc_h, inc_v = s[0], inc_amt
        else:
            inc_h = None
            inc_v = 0
        sem_handles = [(self.sem(n)[0], v) for (n, v) in waits if v > 0]

        def run(eng):
            for h, v in sem_handles:
                eng.wait_ge(h, v)
            ins = fn(eng)
            if inc_h is not None:
                if ins is None:
                    ins = eng.nop()
                ins.then_inc(inc_h, inc_v)
        self.sched.append((engine, run))
        return self.cnt(inc) if inc is not None else None

    def wait(self, engine, *waits):
        self.op(engine, lambda e: None, waits=waits)

    def barrier(self, engines=ENGINES):
        s = self.sem("BAR")
        for _ in engines:
            s[1] += 1
        target = s[1]
        for e in engines:
            def run(eng, h=s[0], target=target):
                eng.nop().then_inc(h, 1)
                eng.wait_ge(h, target)
            self.sched.append((e, run))

    def fconst(self, *vals):
        for val in vals:
            if val not in self._const_vals:
                self._const_vals[val] = len(self._const_vals)

    def _emit_consts(self):
        if not self._const_vals:
            return
        assert len(self._const_vals) <= 32
        tile = self._const_tile
        vals = dict(self._const_vals)

        cs = self.sem("CONST")
        cs[1] += 1

        def run(eng, h=cs[0]):
            for v, i in vals.items():
                eng.memset(tile[:, i:i + 1], float(v))
            eng.nop().then_inc(h, 1)
        self.sched.insert(0, ("vector", run))
        for e in ENGINES:
            if e == "vector":
                continue

            def wrun(eng, h=cs[0]):
                eng.wait_ge(h, 1)
            self.sched.insert(1, (e, wrun))
        for v, i in vals.items():
            self.nc.const_aps.register(F32, float(v), tile[:, i:i + 1])

    def finalize(self):
        self._emit_consts()
        per_engine = {e: [fn for (en, fn) in self.sched if en == e]
                      for e in ENGINES}
        with self.es, self.nc.Block() as block:
            @block.tensor
            def _(eng):
                for fn in per_engine["tensor"]:
                    fn(eng)

            @block.vector
            def _(eng):
                for fn in per_engine["vector"]:
                    fn(eng)

            @block.scalar
            def _(eng):
                for fn in per_engine["scalar"]:
                    fn(eng)

            @block.gpsimd
            def _(eng):
                for fn in per_engine["gpsimd"]:
                    fn(eng)

            @block.sync
            def _(eng):
                for fn in per_engine["sync"]:
                    fn(eng)
        return self.nc



def emit_ln(p, x_dram, xln, name):
    nm = name
    xf = p.sbuf(nm + "_xf", [128, NH * T], F32)
    sq = p.sbuf(nm + "_sq", [128, 2 * T], BF16)
    xb = p.sbuf(nm + "_xb", [128, 2 * T], BF16)
    w1 = p.sbuf(nm + "_w1", [128, 1], BF16)
    m_ps = p.psum(nm + "_mps", [1, T])
    v_ps = p.psum(nm + "_vps", [1, T])
    row = p.sbuf(nm + "_row", [1, 4 * T], F32)
    ones_f = p.sbuf(nm + "_onesf", [1, 128], F32)
    b_ps = p.psum(nm + "_bps", [128, T])
    r_ps = p.psum(nm + "_rps", [128, T])
    mb = p.sbuf(nm + "_mb", [128, T], F32)
    rb = p.sbuf(nm + "_rb", [128, T], BF16)
    tb = p.sbuf(nm + "_tb", [128, 2 * T], BF16)

    p.fconst(1e-5)
    p.op("vector", lambda e: e.memset(w1[:, :], 1.0 / H), waits=p.wof(w1), inc="v")
    p.op("vector", lambda e: e.memset(ones_f[:, :], 1.0), waits=p.wof(ones_f), inc="v")
    for c in range(NH):
        ld = p.op("sync", lambda e, c=c: e.dma_start(
            out=xf[:, c * T:(c + 1) * T], in_=x_dram[c * 128:(c + 1) * 128, :]),
            waits=(p.wof(xf) if c == 0 else ()), inc="ld", inc_amt=16)
        w = [("ld", ld)]
        if c >= 2:
            w.append(("t", p.cnt("t")))
        a = p.op("scalar", lambda e, c=c: e.activation(
            sq[:, (c % 2) * T:(c % 2 + 1) * T], xf[:, c * T:(c + 1) * T],
            AF.Square), waits=w + (p.wof(sq) if c == 0 else []), inc="a")
        v = p.op("vector", lambda e, c=c: e.tensor_copy(
            xb[:, (c % 2) * T:(c % 2 + 1) * T], xf[:, c * T:(c + 1) * T]),
            waits=w + (p.wof(xb) if c == 0 else []), inc="v")
        for half in range(2):
            p.op("tensor", lambda e, c=c, half=half: e.matmul(
                m_ps[0:1, half * 512:(half + 1) * 512], w1[:, 0:1],
                xb[:, (c % 2) * T + half * 512:(c % 2) * T + (half + 1) * 512],
                start=(c == 0), stop=(c == NH - 1)),
                waits=([("v", v)] + (p.wof(m_ps) if c == 0 else [])
                       if half == 0 else ()))
            p.op("tensor", lambda e, c=c, half=half: e.matmul(
                v_ps[0:1, half * 512:(half + 1) * 512], w1[:, 0:1],
                sq[:, (c % 2) * T + half * 512:(c % 2) * T + (half + 1) * 512],
                start=(c == 0), stop=(c == NH - 1)),
                waits=([("a", a)] + (p.wof(v_ps) if c == 0 else [])
                       if half == 0 else ()),
                inc=("t" if half == 1 else None))
    pe_stats = p.cnt("t")
    v1 = p.op("vector", lambda e: e.tensor_copy(row[0:1, 0:T], m_ps[0:1, :]),
              waits=[("t", pe_stats)] + p.wof(row), inc="v")
    v2 = p.op("vector", lambda e: e.tensor_tensor(
        row[0:1, T:2 * T], row[0:1, 0:T], row[0:1, 0:T], ALU.mult),
        waits=[("v", v1)], inc="v")
    v3 = p.op("vector", lambda e: e.tensor_tensor(
        row[0:1, 2 * T:3 * T], v_ps[0:1, :], row[0:1, T:2 * T], ALU.subtract),
        waits=[("v", v2)], inc="v")
    a1 = p.op("scalar", lambda e: e.activation(
        row[0:1, T:2 * T], row[0:1, 2 * T:3 * T], AF.Ln, bias=1e-5),
        waits=[("v", v3)], inc="a")
    a2 = p.op("scalar", lambda e: e.activation(
        row[0:1, 3 * T:4 * T], row[0:1, T:2 * T], AF.Exp, scale=-0.5),
        waits=[("a", a1)], inc="a")
    t1 = None
    for half in range(2):
        p.op("tensor", lambda e, half=half: e.matmul(
            b_ps[:, half * 512:(half + 1) * 512], ones_f[0:1, :],
            row[0:1, half * 512:(half + 1) * 512], start=True, stop=True),
            waits=([("v", v1)] + p.wof(b_ps) if half == 0 else ()))
        t1 = p.op("tensor", lambda e, half=half: e.matmul(
            r_ps[:, half * 512:(half + 1) * 512], ones_f[0:1, :],
            row[0:1, 3 * T + half * 512:3 * T + (half + 1) * 512],
            start=True, stop=True),
            waits=([("a", a2)] + p.wof(r_ps) if half == 0 else ()),
            inc=("t" if half == 1 else None))
    v4 = p.op("vector", lambda e: e.tensor_copy(mb[:, :], b_ps[:, :]),
              waits=[("t", t1)] + p.wof(mb), inc="v")
    v5 = p.op("vector", lambda e: e.tensor_copy(rb[:, :], r_ps[:, :]),
              waits=[("v", v4)] + p.wof(rb), inc="v")
    for c in range(NH):
        vv = p.op("vector", lambda e, c=c: e.tensor_tensor(
            tb[:, (c % 2) * T:(c % 2 + 1) * T], xf[:, c * T:(c + 1) * T],
            mb[:, :], ALU.subtract),
            waits=([("v", v5)] + p.wof(tb, xln) if c == 0 else ()), inc="v")
        p.op("vector", lambda e, c=c: e.tensor_tensor(
            xln[:, c * T:(c + 1) * T], tb[:, (c % 2) * T:(c % 2 + 1) * T],
            rb[:, :], ALU.mult), waits=[("v", vv)], inc="v")
    done = p.cnt("v")
    p.free(xf, sq, xb, w1, row, ones_f, mb, rb, tb, m_ps, v_ps, b_ps, r_ps)
    return done


def emit_colsum_rows(p, arena, ncols, name, ready_v):
    nm = name
    ones1 = p.sbuf(nm + "_o1", [128, 1], BF16)
    ps = p.psum(nm + "_ps", [1, T])
    hi = p.sbuf(nm + "_hi", [1, T], BF16)
    lo = p.sbuf(nm + "_lo", [1, T], BF16)
    hf = p.sbuf(nm + "_hf", [1, T], F32)
    p.op("vector", lambda e: e.memset(ones1[:, :], 1.0), waits=p.wof(ones1),
         inc="v")
    mv = p.cnt("v")
    t = None
    for c in range(ncols):
        for half in range(2):
            t = p.op("tensor", lambda e, c=c, half=half: e.matmul(
                ps[0:1, half * 512:(half + 1) * 512], ones1[:, 0:1],
                arena[:, c * T + half * 512:c * T + (half + 1) * 512],
                start=(c == 0), stop=(c == ncols - 1)),
                waits=([("v", max(ready_v, mv))] + p.wof(ps)
                       if (c == 0 and half == 0) else ()),
                inc=("t" if (c == ncols - 1 and half == 1) else None))
    v1 = p.op("vector", lambda e: e.tensor_copy(hi[0:1, :], ps[0:1, :]),
              waits=[("t", t)] + p.wof(hi, hf, lo), inc="v")
    v2 = p.op("vector", lambda e: e.tensor_copy(hf[0:1, :], hi[0:1, :]),
              waits=[("v", v1)], inc="v")
    v3 = p.op("vector", lambda e: e.tensor_tensor(
        lo[0:1, :], ps[0:1, :], hf[0:1, :], ALU.subtract),
        waits=[("v", v2)], inc="v")
    p.free(ps, hf, ones1)
    return hi, lo, v3, (hi, lo)


def emit_ffn(p, x_dram, xout_dram, wg_dram, wu_dram, wd_dram,
             sg_dram, su_dram, sd_dram, NI, name):
    nm = name
    xln = p.sbuf(nm + "_xln", [128, NH * T], BF16)
    lnv = emit_ln(p, x_dram, xln, nm + "_ln")
    hi, lo, xsv, rows_free = emit_colsum_rows(p, xln, NH, nm + "_xs", lnv)

    sg = p.sbuf(nm + "_sg", [128, NI], F32)
    su = p.sbuf(nm + "_su", [128, NI], F32)
    sd = p.sbuf(nm + "_sd", [128, NH], F32)
    sdh = p.sbuf(nm + "_sdh", [128, NH], F32)
    negk = p.sbuf(nm + "_negk", [1, 128], BF16)
    p.op("sync", lambda e: e.dma_start(out=sg[:, :], in_=sg_dram.ap()),
         waits=p.wof(sg, su, sd), inc="ld", inc_amt=16)
    p.op("sync", lambda e: e.dma_start(out=su[:, :], in_=su_dram.ap()),
         inc="ld", inc_amt=16)
    ldsc = p.op("sync", lambda e: e.dma_start(out=sd[:, :], in_=sd_dram.ap()),
                inc="ld", inc_amt=16)
    p.op("vector", lambda e: e.memset(negk[:, :], -128.0),
         waits=p.wof(negk, sdh), inc="v")
    p.op("vector", lambda e: e.tensor_scalar(
        sdh[:, :], sd[:, :], 0.5, None, ALU.mult),
        waits=[("ld", ldsc)], inc="v")
    prep_v = p.cnt("v")

    NBUF = 3
    harena = p.sbuf(nm + "_h", [128, NI * T], BF16)
    gbuf = p.sbuf(nm + "_gw", [128, NBUF * NH * 128], BF16)
    ubuf = p.sbuf(nm + "_uw", [128, NBUF * NH * 128], BF16)
    g_ps = [p.psum(nm + "_gps%d" % i, [128, T]) for i in range(2)]
    u_ps = [p.psum(nm + "_ups%d" % i, [128, T]) for i in range(2)]
    sgb = p.sbuf(nm + "_sgb", [128, 2 * T], BF16)
    gend = {}
    aend = {}
    hend = {}
    wend = {}
    for ic in range(NI):
        par = ic % 2
        wpar = ic % NBUF
        dw = []
        if ic == 0:
            dw = [("v", lnv)] + p.wof(gbuf, ubuf)
        if ic >= NBUF:
            dw = [("t", gend[ic - NBUF])]
        wv = p.op("gpsimd", lambda e, ic=ic, wpar=wpar: e.dma_start(
            out=gbuf[:, wpar * NH * 128:(wpar + 1) * NH * 128],
            in_=wg_dram[ic]), waits=dw, inc="w", inc_amt=16)
        wv = p.op("gpsimd", lambda e, ic=ic, wpar=wpar: e.dma_start(
            out=ubuf[:, wpar * NH * 128:(wpar + 1) * NH * 128],
            in_=wu_dram[ic]), inc="w", inc_amt=16)
        wend[ic] = wv
        tw = [("w", wend[ic]), ("v", max(prep_v, xsv))]
        if ic == 0:
            tw += p.wof(*g_ps, *u_ps)
        if ic >= 2:
            tw.append(("a", aend[ic - 2]))
            tw.append(("v", hend[ic - 2]))
        for hc in range(NH):
            for half in range(2):
                p.op("tensor", lambda e, wpar=wpar, par=par, hc=hc, half=half: e.matmul(
                    g_ps[par][:, half * 512:(half + 1) * 512],
                    gbuf[:, wpar * NH * 128 + hc * 128:wpar * NH * 128 + (hc + 1) * 128],
                    xln[:, hc * T + half * 512:hc * T + (half + 1) * 512],
                    start=(hc == 0), stop=False),
                    waits=(tw if (hc == 0 and half == 0) else ()))
        for half in range(2):
            p.op("tensor", lambda e, par=par, half=half: e.matmul(
                g_ps[par][:, half * 512:(half + 1) * 512], negk[0:1, :],
                hi[0:1, half * 512:(half + 1) * 512], start=False, stop=False))
            p.op("tensor", lambda e, par=par, half=half: e.matmul(
                g_ps[par][:, half * 512:(half + 1) * 512], negk[0:1, :],
                lo[0:1, half * 512:(half + 1) * 512], start=False, stop=True),
                inc=("t" if half == 1 else None))
        tgd = p.cnt("t")
        for hc in range(NH):
            for half in range(2):
                p.op("tensor", lambda e, wpar=wpar, par=par, hc=hc, half=half: e.matmul(
                    u_ps[par][:, half * 512:(half + 1) * 512],
                    ubuf[:, wpar * NH * 128 + hc * 128:wpar * NH * 128 + (hc + 1) * 128],
                    xln[:, hc * T + half * 512:hc * T + (half + 1) * 512],
                    start=(hc == 0), stop=False))
        for half in range(2):
            p.op("tensor", lambda e, par=par, half=half: e.matmul(
                u_ps[par][:, half * 512:(half + 1) * 512], negk[0:1, :],
                hi[0:1, half * 512:(half + 1) * 512], start=False, stop=False))
            p.op("tensor", lambda e, par=par, half=half: e.matmul(
                u_ps[par][:, half * 512:(half + 1) * 512], negk[0:1, :],
                lo[0:1, half * 512:(half + 1) * 512], start=False, stop=True),
                inc=("t" if half == 1 else None))
        tud = p.cnt("t")
        gend[ic] = tud
        aa = p.op("scalar", lambda e, ic=ic, par=par: e.activation(
            sgb[:, par * T:(par + 1) * T], g_ps[par][:, :], AF.Silu,
            scale=sg[:, ic:ic + 1]),
            waits=[("t", tgd), ("v", prep_v)] + (p.wof(sgb) if ic == 0 else []),
            inc="a")
        aend[ic] = aa
        hend[ic] = p.op("vector", lambda e, ic=ic, par=par: e.scalar_tensor_tensor(
            harena[:, ic * T:(ic + 1) * T], u_ps[par][:, :], su[:, ic:ic + 1],
            sgb[:, par * T:(par + 1) * T], ALU.mult, ALU.mult),
            waits=[("t", tud), ("a", aa)] + (p.wof(harena) if ic == 0 else []),
            inc="v")
    h_done_v = p.cnt("v")
    p.free(gbuf, ubuf, sgb, *g_ps, *u_ps)
    p.free(xln)

    hhi, hlo, hs_v, hrows_free = emit_colsum_rows(
        p, harena, NI, nm + "_hs", h_done_v)

    NBD = 2
    dbuf = p.sbuf(nm + "_dw", [128, NBD * NI * 128], BF16)
    d_ps = [p.psum(nm + "_dps%d" % i, [128, T]) for i in range(2)]
    xres = p.sbuf(nm + "_xr", [128, 2 * T], F32)
    xo = p.sbuf(nm + "_xo", [128, 2 * T], F32)
    st_base = p.cnt("st")
    dend = {}
    vend = {}
    for hc in range(NH):
        par = hc % 2
        wpar = hc % NBD
        ww = []
        if hc == 0:
            ww = [("v", h_done_v)] + p.wof(dbuf)
        if hc >= NBD:
            ww = [("t", dend[hc - NBD])]
        wv = p.op("gpsimd", lambda e, hc=hc, wpar=wpar: e.dma_start(
            out=dbuf[:, wpar * NI * 128:(wpar + 1) * NI * 128],
            in_=wd_dram[hc]), waits=ww, inc="w", inc_amt=16)
        lr = p.op("sync", lambda e, hc=hc, par=par: e.dma_start(
            out=xres[:, par * T:(par + 1) * T],
            in_=x_dram[hc * 128:(hc + 1) * 128, :]),
            waits=(p.wof(xres) if hc == 0 else
                   ([("v", vend[hc - 2])] if hc >= 2 else [])),
            inc="ld", inc_amt=16)
        tw = [("w", wv), ("v", hs_v)]
        if hc == 0:
            tw += p.wof(*d_ps)
        if hc >= 2:
            tw.append(("v", vend[hc - 2]))
        for ic in range(NI):
            for half in range(2):
                p.op("tensor", lambda e, wpar=wpar, par=par, ic=ic, half=half: e.matmul(
                    d_ps[par][:, half * 512:(half + 1) * 512],
                    dbuf[:, wpar * NI * 128 + ic * 128:wpar * NI * 128 + (ic + 1) * 128],
                    harena[:, ic * T + half * 512:ic * T + (half + 1) * 512],
                    start=(ic == 0), stop=False),
                    waits=(tw if (ic == 0 and half == 0) else ()))
        for half in range(2):
            p.op("tensor", lambda e, par=par, half=half: e.matmul(
                d_ps[par][:, half * 512:(half + 1) * 512], negk[0:1, :],
                hhi[0:1, half * 512:(half + 1) * 512], start=False, stop=False))
            p.op("tensor", lambda e, par=par, half=half: e.matmul(
                d_ps[par][:, half * 512:(half + 1) * 512], negk[0:1, :],
                hlo[0:1, half * 512:(half + 1) * 512], start=False, stop=True),
                inc=("t" if half == 1 else None))
        tdd = p.cnt("t")
        dend[hc] = tdd
        vw = [("t", tdd), ("ld", lr)]
        if hc == 0:
            vw += p.wof(xo)
        if hc >= 2:
            vw.append(("st", st_base + 16 * (hc - 1)))
        vo = p.op("vector", lambda e, hc=hc, par=par: e.scalar_tensor_tensor(
            xo[:, par * T:(par + 1) * T], d_ps[par][:, :], sdh[:, hc:hc + 1],
            xres[:, par * T:(par + 1) * T], ALU.mult, ALU.add),
            waits=vw, inc="v")
        vend[hc] = vo
        p.op("sync", lambda e, hc=hc, par=par: e.dma_start(
            out=xout_dram[hc * 128:(hc + 1) * 128, :],
            in_=xo[:, par * T:(par + 1) * T]), waits=[("v", vo)],
            inc="st", inc_amt=16)
    st_done = p.cnt("st")
    p.free(harena, dbuf, xres, xo, sg, su, sd, sdh, negk,
           *d_ps, *rows_free, *hrows_free)
    return st_done


def emit_proj(p, xln, w_dram, out_dram, n_out, mode, name, xln_ready_v,
              cast=True, stq="sync"):
    nm = name
    NO = n_out // 128
    if not callable(out_dram):
        dram_h = out_dram
        out_dram = lambda oc: dram_h[oc * 128:(oc + 1) * 128, :]
    wsb = p.sbuf(nm + "_w", [128, NH * n_out], BF16)
    ps = [p.psum(nm + "_ps%d" % i, [128, 512]) for i in range(2)]
    stage = p.sbuf(nm + "_st", [128, 2 * max(n_out, T)], BF16)
    wv = None
    wq = "w" if cast else "ld"
    weng = "gpsimd" if cast else "sync"
    for hc in range(NH):
        wv = p.op(weng, lambda e, hc=hc: e.dma_start(
            out=wsb[:, hc * n_out:(hc + 1) * n_out], in_=w_dram[hc]),
            waits=(p.wof(wsb) if hc == 0 else ()), inc=wq, inc_amt=16)
    st_base = p.cnt("st")
    k = 0
    outer = NT if mode == "tm" else NO
    inner = n_out // 512 if mode == "tm" else T // 512
    for oc in range(outer):
        par = oc % 2
        for it in range(inner):
            pp = ps[k % 2]
            tw = [(wq, wv), ("v", xln_ready_v)]
            if k == 0:
                tw += p.wof(*ps)
            if k >= 2:
                tw.append(("v", p.cnt("v")))
            tt = None
            for hc in range(NH):
                if mode == "tm":
                    fn = (lambda e, oc=oc, it=it, hc=hc, pp=pp: e.matmul(
                        pp[:, :],
                        xln[:, hc * T + oc * 128:hc * T + (oc + 1) * 128],
                        wsb[:, hc * n_out + it * 512:hc * n_out + (it + 1) * 512],
                        start=(hc == 0), stop=(hc == NH - 1)))
                else:
                    fn = (lambda e, oc=oc, it=it, hc=hc, pp=pp: e.matmul(
                        pp[:, :],
                        wsb[:, hc * n_out + oc * 128:hc * n_out + (oc + 1) * 128],
                        xln[:, hc * T + it * 512:hc * T + (it + 1) * 512],
                        start=(hc == 0), stop=(hc == NH - 1)))
                tt = p.op("tensor", fn, waits=(tw if hc == 0 else ()),
                          inc=("t" if hc == NH - 1 else None))
            width = n_out if mode == "tm" else T
            cw = [("t", tt)]
            if k <= 1:
                cw += p.wof(stage)
            if oc >= 2 and it == 0:
                cw.append(("st", st_base + 16 * (oc - 1)))
            p.op("vector", lambda e, it=it, par=par, pp=pp, width=width: e.tensor_copy(
                stage[:, par * width + it * 512:par * width + (it + 1) * 512],
                pp[:, :]), waits=cw, inc="v")
            k += 1
        sv = p.cnt("v")
        p.op(stq, lambda e, oc=oc, par=par, width=width: e.dma_start(
            out=out_dram(oc),
            in_=stage[:, par * width:(par + 1) * width]),
            waits=[("v", sv)], inc="st", inc_amt=16)
    p.free(wsb, stage, *ps)
    return p.cnt("st")



def _bcast_mid(ap, count, pos=1):
    import dataclasses
    dims = list(ap.ap)
    dims.insert(pos, [0, count])
    return dataclasses.replace(ap, ap=type(ap.ap)(dims))


def emit_attn0(p, x_dram, xout_dram, qf_dram, kf_dram, v_dram, wo_dram,
               ident_dram, name, gate=()):
    nm = name
    et_dram = p.dram(nm + "_et", [16, 128, H], BF16)
    ksb = p.sbuf(nm + "_k", [128, 16 * H], BF16)
    qsb = p.sbuf(nm + "_q", [128, 16 * H], BF16)
    rz = p.sbuf(nm + "_rz", [128, 16], F32)
    mx = p.sbuf(nm + "_mx", [128, 16], F32)
    zp = p.sbuf(nm + "_zp", [128, 16], F32)
    esb = p.sbuf(nm + "_e", [128, 2 * H], BF16)
    etst = p.sbuf(nm + "_ets", [128, 2 * H], BF16)
    ident = p.sbuf(nm + "_id", [128, 128], BF16)
    t_ps = [p.psum(nm + "_tps%d" % i, [128, 128], BF16) for i in range(2)]
    kl = ql = None
    for c in range(16):
        hh, r = c // 8, c % 8
        kl = p.op("sync", lambda e, c=c, hh=hh, r=r: e.dma_start(
            out=ksb[:, c * H:(c + 1) * H],
            in_=kf_dram.ap()[hh, r * 128:(r + 1) * 128, :]),
            waits=(list(gate) + p.wof(ksb, qsb) if c == 0 else ()),
            inc="ld", inc_amt=16)
        ql = p.op("sync", lambda e, c=c, hh=hh, r=r: e.dma_start(
            out=qsb[:, c * H:(c + 1) * H],
            in_=qf_dram.ap()[hh, r * 128:(r + 1) * 128, :]),
            inc="ld", inc_amt=16)
    mvv = p.op("vector", lambda e: None,
               waits=p.wof(ident, rz, mx, zp, esb, etst), inc="v")
    idl = p.op("sync", lambda e: e.dma_start(out=ident[:, :], in_=ident_dram.ap()),
               waits=[("v", mvv)], inc="ld", inc_amt=16)
    ssb = p.sbuf(nm + "_ssb", [128, 2 * H], F32)
    s_ps2 = [p.psum(nm + "_sps%d" % i, [128, 512]) for i in range(2)]
    kk = 0
    scp = {}
    expa = {}
    tend = {}
    for et in range(16):
        pb = (et % 2) * H
        mb_ = (et % 2) * 8
        for ft in range(4):
            pp = s_ps2[kk % 2]
            tw = [("ld", max(ql, idl))]
            if kk == 0:
                tw += p.wof(*s_ps2) + p.wof(ssb)
            if kk >= 2:
                et2, ft2 = divmod(kk - 2, 4)
                tw.append(("v", scp[(et2, ft2)]))
            if ft == 0 and et >= 2:
                tw.append(("a", expa[et - 2]))
            for sc in range(16):
                p.op("tensor", lambda e, et=et, ft=ft, sc=sc, pp=pp: e.matmul(
                    pp[:, :],
                    qsb[:, sc * H + et * 128:sc * H + (et + 1) * 128],
                    ksb[:, sc * H + ft * 512:sc * H + (ft + 1) * 512],
                    start=(sc == 0), stop=(sc == 15)),
                    waits=(tw if sc == 0 else ()),
                    inc=("t" if sc == 15 else None))
            ts = p.cnt("t")
            scp[(et, ft)] = p.op("vector", lambda e, ft=ft, pp=pp, pb=pb: e.tensor_copy(
                ssb[:, pb + ft * 512:pb + (ft + 1) * 512], pp[:, :]),
                waits=[("t", ts)], inc="v")
            kk += 1
        rw = [("v", scp[(et, 3)])]
        if et >= 1:
            rw.append(("a", expa[et - 1]))
        if et >= 2:
            rw.append(("a", expa[et - 2]))
        for ft in range(4):
            p.op("vector", lambda e, ft=ft, pb=pb, mb_=mb_: e.tensor_reduce(
                mx[:, mb_ + ft:mb_ + ft + 1], ssb[:, pb + ft * 512:pb + (ft + 1) * 512],
                mybir.AxisListType.X, ALU.max),
                waits=(rw if ft == 0 else ()), inc="v")
        vv = p.op("vector", lambda e, mb_=mb_: e.tensor_reduce(
            mx[:, mb_ + 4:mb_ + 5], mx[:, mb_:mb_ + 4], mybir.AxisListType.X,
            ALU.max), waits=[("v", p.cnt("v"))], inc="v")
        vv = p.op("vector", lambda e, mb_=mb_: e.tensor_scalar(
            mx[:, mb_ + 5:mb_ + 6], mx[:, mb_ + 4:mb_ + 5], -0.125, None,
            ALU.mult), waits=[("v", vv)], inc="v")
        aa = None
        for ft in range(4):
            aa = p.op("scalar", lambda e, ft=ft, pb=pb, mb_=mb_: e.activation(
                esb[:, pb + ft * 512:pb + (ft + 1) * 512],
                ssb[:, pb + ft * 512:pb + (ft + 1) * 512],
                AF.Exp, bias=mx[:, mb_ + 5:mb_ + 6], scale=0.125,
                accum_out=zp[:, mb_ + ft:mb_ + ft + 1]),
                waits=([("v", vv)] + ([("t", tend[et - 2])] if et >= 2 else [])
                       if ft == 0 else ()), inc="a")
        expa[et] = aa
        vz = p.op("vector", lambda e, mb_=mb_: e.tensor_reduce(
            zp[:, mb_ + 4:mb_ + 5], zp[:, mb_:mb_ + 4], mybir.AxisListType.X,
            ALU.add), waits=[("a", aa)], inc="v")
        vz = p.op("vector", lambda e, et=et, mb_=mb_: e.reciprocal(
            rz[:, et:et + 1], zp[:, mb_ + 4:mb_ + 5]),
            waits=[("v", vz)], inc="v")
        par = et % 2
        for fc in range(16):
            tt = p.op("tensor", lambda e, fc=fc, pb=pb: e.transpose(
                t_ps[fc % 2][:, :],
                esb[:, pb + fc * 128:pb + (fc + 1) * 128], ident[:, :]),
                waits=([("a", aa)] +
                       (p.wof(*t_ps) if et == 0 and fc == 0 else []) +
                       ([("v", p.cnt("v"))] if (fc >= 2 or et >= 1) else [])),
                inc="t")
            p.op("vector", lambda e, fc=fc, par=par: e.tensor_copy(
                etst[:, par * H + fc * 128:par * H + (fc + 1) * 128],
                t_ps[fc % 2][:, :]),
                waits=[("t", p.cnt("t"))] +
                      ([("st", p.cnt("st"))] if fc == 0 and et >= 2 else []),
                inc="v")
        tend[et] = p.cnt("t")
        sv = p.cnt("v")
        p.op("sync", lambda e, et=et, par=par: e.dma_start(
            out=et_dram[et], in_=etst[:, par * H:(par + 1) * H]),
            waits=[("v", sv)], inc="st", inc_amt=16)
    phaseb_st = p.cnt("st")
    p.free(ksb, qsb, esb, ssb, *s_ps2, *t_ps)

    vsb = p.sbuf(nm + "_v", [128, 16 * T], BF16)
    ao = p.sbuf(nm + "_ao", [128, 16 * T], BF16)
    etb = p.sbuf(nm + "_etb", [128, 2 * H], BF16)
    a_ps = [p.psum(nm + "_aps%d" % i, [128, 512]) for i in range(2)]
    vl = None
    for c in range(16):
        vl = p.op("sync", lambda e, c=c: e.dma_start(
            out=vsb[:, c * T:(c + 1) * T], in_=v_dram[c * 128:(c + 1) * 128, :]),
            waits=(p.wof(vsb, ao, etb) if c == 0 else ()), inc="ld", inc_amt=16)
    k = 0
    for et in range(16):
        par = et % 2
        el = p.op("sync", lambda e, et=et, par=par: e.dma_start(
            out=etb[:, par * H:(par + 1) * H], in_=et_dram[et]),
            waits=([("st", phaseb_st)] if et == 0 else
                   ([("t", p.cnt("t"))] if et >= 2 else [])),
            inc="ld", inc_amt=16)
        for sh in range(2):
            pp = a_ps[k % 2]
            tw = [("ld", max(el, vl))]
            if k == 0:
                tw += p.wof(*a_ps)
            if k >= 2:
                tw.append(("v", p.cnt("v")))
            tt = None
            for fc in range(16):
                tt = p.op("tensor", lambda e, et=et, par=par, sh=sh, fc=fc, pp=pp: e.matmul(
                    pp[:, :], etb[:, par * H + fc * 128:par * H + (fc + 1) * 128],
                    vsb[:, fc * T + sh * 512:fc * T + (sh + 1) * 512],
                    start=(fc == 0), stop=(fc == 15)),
                    waits=(tw if fc == 0 else ()),
                    inc=("t" if fc == 15 else None))
            p.op("vector", lambda e, et=et, sh=sh, pp=pp: e.tensor_scalar(
                ao[:, et * T + sh * 512:et * T + (sh + 1) * 512], pp[:, :],
                rz[:, et:et + 1], None, ALU.mult),
                waits=[("t", tt)], inc="v")
            k += 1
    ao_v = p.cnt("v")
    p.free(etb, *a_ps)

    wsb = p.sbuf(nm + "_wo", [128, NH * H], BF16)
    w_ps = [p.psum(nm + "_wps%d" % i, [128, T]) for i in range(2)]
    xres = p.sbuf(nm + "_xr", [128, 2 * T], F32)
    xo = p.sbuf(nm + "_xo", [128, 2 * T], F32)
    wv = None
    for hc in range(NH):
        wv = p.op("gpsimd", lambda e, hc=hc: e.dma_start(
            out=wsb[:, hc * H:(hc + 1) * H], in_=wo_dram[hc]),
            waits=(p.wof(wsb) if hc == 0 else ()), inc="w", inc_amt=16)
    st_base = p.cnt("st")
    for oc in range(NH):
        par = oc % 2
        lr = p.op("sync", lambda e, oc=oc, par=par: e.dma_start(
            out=xres[:, par * T:(par + 1) * T],
            in_=x_dram[oc * 128:(oc + 1) * 128, :]),
            waits=(p.wof(xres, xo) if oc == 0 else
                   ([("v", p.cnt("v"))] if oc >= 2 else [])),
            inc="ld", inc_amt=16)
        tw = [("w", wv), ("v", ao_v)]
        if oc == 0:
            tw += p.wof(*w_ps)
        if oc >= 2:
            tw.append(("v", p.cnt("v")))
        for ec in range(16):
            for half in range(2):
                p.op("tensor", lambda e, oc=oc, par=par, ec=ec, half=half: e.matmul(
                    w_ps[par][:, half * 512:(half + 1) * 512],
                    wsb[:, ec * H + oc * 128:ec * H + (oc + 1) * 128],
                    ao[:, ec * T + half * 512:ec * T + (half + 1) * 512],
                    start=(ec == 0), stop=(ec == 15)),
                    waits=(tw if (ec == 0 and half == 0) else ()),
                    inc=("t" if (ec == 15 and half == 1) else None))
        tdd = p.cnt("t")
        vw = [("t", tdd), ("ld", lr)]
        if oc >= 2:
            vw.append(("st", st_base + 16 * (oc - 1)))
        vo = p.op("vector", lambda e, oc=oc, par=par: e.tensor_tensor(
            xo[:, par * T:(par + 1) * T], w_ps[par][:, :],
            xres[:, par * T:(par + 1) * T], ALU.add),
            waits=vw, inc="v")
        p.op("sync", lambda e, oc=oc, par=par: e.dma_start(
            out=xout_dram[oc * 128:(oc + 1) * 128, :],
            in_=xo[:, par * T:(par + 1) * T]), waits=[("v", vo)],
            inc="st", inc_amt=16)
    st_done = p.cnt("st")
    p.free(vsb, ao, wsb, xres, xo, rz, mx, zp, ident, etst, *w_ps)
    return st_done


def emit_proj_g(p, xln, wg_dram, lgl_dram, identf_dram, name, xln_ready_v):
    nm = name
    wsb = p.sbuf(nm + "_w", [128, NH * 8], BF16)
    g_ps = p.psum(nm + "_ps", [8, T])
    sc1 = p.sbuf(nm + "_s1", [8, T], F32)
    sc2 = p.sbuf(nm + "_s2", [8, T], F32)
    sc3 = p.sbuf(nm + "_s3", [8, T], F32)
    mt = p.sbuf(nm + "_mt", [128, T], F32)
    idf = p.sbuf(nm + "_idf", [128, 128], F32)
    lgt = p.sbuf(nm + "_lgt", [128, 8 * 8], F32)
    t_ps = p.psum(nm + "_tps", [128, 128])
    p.fconst(GATE_OFFSET, 1.0)
    mz = p.op("vector", lambda e: e.memset(mt[:, :], 0.0),
              waits=p.wof(mt, idf, lgt, t_ps), inc="v")
    idl = p.op("sync", lambda e: e.dma_start(
        out=idf[:, :], in_=identf_dram.ap()), inc="ld", inc_amt=16)
    wv = None
    for hc in range(NH):
        wv = p.op("gpsimd", lambda e, hc=hc: e.dma_start(
            out=wsb[:, hc * 8:(hc + 1) * 8], in_=wg_dram[hc]),
            waits=(p.wof(wsb) if hc == 0 else ()), inc="w", inc_amt=16)
    tt = None
    for hc in range(NH):
        for half in range(2):
            tt = p.op("tensor", lambda e, hc=hc, half=half: e.matmul(
                g_ps[0:8, half * 512:(half + 1) * 512],
                wsb[:, hc * 8:(hc + 1) * 8],
                xln[:, hc * T + half * 512:hc * T + (half + 1) * 512],
                start=(hc == 0), stop=(hc == NH - 1)),
                waits=([("w", wv), ("v", xln_ready_v)] + p.wof(g_ps)
                       if (hc == 0 and half == 0) else ()),
                inc=("t" if (hc == NH - 1 and half == 1) else None))
    a1 = p.op("scalar", lambda e: e.activation(
        sc1[:, :], g_ps[0:8, :], AF.Abs, bias=GATE_OFFSET),
        waits=[("t", tt)] + p.wof(sc1, sc2, sc3), inc="a")
    a2 = p.op("scalar", lambda e: e.activation(
        sc2[:, :], sc1[:, :], AF.Exp, scale=-1.0), waits=[("a", a1)], inc="a")
    a3 = p.op("scalar", lambda e: e.activation(
        sc1[:, :], sc2[:, :], AF.Ln, bias=1.0), waits=[("a", a2)], inc="a")
    v1 = p.op("vector", lambda e: e.tensor_scalar(
        sc3[:, :], g_ps[0:8, :], GATE_OFFSET, 0.0, ALU.add, ALU.min),
        waits=[("t", tt)], inc="v")
    v2 = p.op("vector", lambda e: e.tensor_tensor(
        mt[0:8, :], sc3[0:8, :], sc1[0:8, :], ALU.subtract),
        waits=[("v", v1), ("a", a3), ("v", mz)], inc="v")
    for ch in range(NT):
        tt2 = p.op("tensor", lambda e, ch=ch: e.transpose(
            t_ps[:, :], mt[:, ch * 128:(ch + 1) * 128], idf[:, :]),
            waits=([("v", v2), ("ld", idl)] + p.wof(t_ps) if ch == 0 else
                   [("v", p.cnt("v"))]), inc="t")
        p.op("vector", lambda e, ch=ch: e.tensor_copy(
            lgt[:, ch * 8:(ch + 1) * 8], t_ps[:, 0:8]),
            waits=[("t", tt2)], inc="v")
    lv = p.cnt("v")
    for ch in range(NT):
        p.op("sync", lambda e, ch=ch: e.dma_start(
            out=lgl_dram[ch * 128:(ch + 1) * 128, :],
            in_=lgt[:, ch * 8:(ch + 1) * 8]),
            waits=([("v", lv)] if ch == 0 else ()), inc="st", inc_amt=16)
    st = p.cnt("st")
    p.free(wsb, g_ps, sc1, sc2, sc3, mt, idf, lgt, t_ps)
    return st


def _yoff(k):
    return (k // 3) * 512 + (k % 3) * 129


def emit_attn1(p, x_dram, xout_dram, q_dram, sh_k1, sh_v1, sh_lg, lgl_dram,
               psel_dram, csel_dram, vb_dram, masks_dram,
               tri_dram, ident_dram, wc_dram, bc_dram, name, gate=()):
    nm = name
    NKV, NQ, D = 8, 16, 128
    ksb = p.sbuf(nm + "_k", [128, NKV * 2 * T], BF16)
    vsb = p.sbuf(nm + "_v", [128, 16 * NKV * 129], BF16)
    qsb = p.sbuf(nm + "_q", [128, NKV * NT * 2 * D], BF16)
    lgf = p.sbuf(nm + "_lgf", [128, 16 * 8], F32)
    lgo = p.sbuf(nm + "_lgo", [128, 8 * 8], F32)
    tri = p.sbuf(nm + "_tri", [128, 128], F32)
    onesf = p.sbuf(nm + "_1f", [128, 128], F32)
    psel = p.sbuf(nm + "_psel", [128, 16 * 8], F32)
    csel = p.sbuf(nm + "_csel", [128, 16], F32)
    vb = p.sbuf(nm + "_vb", [128, 8 * 16], F32)
    msk = p.sbuf(nm + "_msk", [128, 8 * 2 * 128], BF16)
    ident = p.sbuf(nm + "_id", [128, 128], BF16)
    cumg = p.sbuf(nm + "_cum", [128, 16 * 8], F32)
    ocum = p.sbuf(nm + "_ocum", [128, 8 * 8], F32)
    mrow = p.sbuf(nm + "_mrow", [1, 64], F32)
    crow = p.sbuf(nm + "_crow", [1, 8], F32)
    msb = p.sbuf(nm + "_msb", [8, 8], F32)
    mgrid = p.sbuf(nm + "_mg", [128, 64], F32)
    esf = p.sbuf(nm + "_esf", [128, 8 * 128], F32)
    ugrid = p.sbuf(nm + "_ug", [128, 64], F32)
    dsc = p.sbuf(nm + "_dsc", [128, 128], F32)
    ld0 = p.cnt("ld")
    first = p.wof(ksb, vsb, qsb, lgf, lgo, tri, onesf, psel, csel, vb, msk,
                  ident, cumg, ocum, mrow, crow, msb, mgrid, esf, ugrid, dsc)
    l = p.op("sync", lambda e: e.dma_start(out=tri[:, :], in_=tri_dram.ap()),
             waits=list(gate) + first, inc="ld", inc_amt=16)
    l = p.op("sync", lambda e: e.dma_start(out=ident[:, :], in_=ident_dram.ap()),
             inc="ld", inc_amt=16)
    vones = p.op("vector", lambda e: e.memset(
        vsb[:, :].rearrange("p (m d) -> p m d", d=129)[:, :, 128:129], 1.0),
        waits=list(gate) + p.wof(vsb), inc="v")
    for c in range(16):
        hh, r = c // 8, c % 8
        l = p.op("sync", lambda e, c=c, hh=hh, r=r: e.dma_start(
            out=lgf[:, c * 8:(c + 1) * 8],
            in_=sh_lg.ap()[hh, r * 128:(r + 1) * 128, :]),
            inc="ld", inc_amt=16)
        l = p.op("sync", lambda e, c=c: e.dma_start(
            out=psel[:, c * 8:(c + 1) * 8], in_=psel_dram[c]), inc="ld", inc_amt=16)
        l = p.op("sync", lambda e, c=c: e.dma_start(
            out=csel[:, c:c + 1], in_=csel_dram[c]), inc="ld", inc_amt=16)
        l = p.op("sync", lambda e, c=c, hh=hh, r=r: e.dma_start(
            out=vsb[:, c * NKV * 129:(c + 1) * NKV * 129]
            .rearrange("p (k d) -> p k d", d=129)[:, :, 0:128],
            in_=sh_v1.ap()[hh, r * 128:(r + 1) * 128, :]
            .rearrange("p (k d) -> p k d", d=128)),
            waits=([("v", vones)] if c == 0 else ()), inc="ld", inc_amt=16)
    for c in range(8):
        l = p.op("sync", lambda e, c=c: e.dma_start(
            out=lgo[:, c * 8:(c + 1) * 8],
            in_=lgl_dram[c * 128:(c + 1) * 128, :]),
            inc="ld", inc_amt=16)
        l = p.op("sync", lambda e, c=c: e.dma_start(
            out=vb[:, c * 16:(c + 1) * 16], in_=vb_dram[c]), inc="ld", inc_amt=16)
        l = p.op("sync", lambda e, c=c: e.dma_start(
            out=msk[:, c * 256:c * 256 + 128], in_=masks_dram[c, 0]),
            inc="ld", inc_amt=16)
        l = p.op("sync", lambda e, c=c: e.dma_start(
            out=msk[:, c * 256 + 128:(c + 1) * 256], in_=masks_dram[c, 1]),
            inc="ld", inc_amt=16)
    for kv in range(NKV):
        for half in range(2):
            l = p.op("sync", lambda e, kv=kv, half=half: e.dma_start(
                out=ksb[:, kv * 2 * T + half * T:kv * 2 * T + (half + 1) * T],
                in_=sh_k1.ap()[half, kv * 128:(kv + 1) * 128, :]),
                inc="ld", inc_amt=16)
    qv = qsb.ap().rearrange("p (k c q d) -> p k c q d", k=NKV, c=NT, q=2, d=D)
    for qh in range(NQ):
        kv, pp = qh // 2, qh % 2
        l = p.op("sync", lambda e, qh=qh, kv=kv, pp=pp: e.dma_start(
            out=qv[:, kv, :, pp, :], in_=q_dram[qh * 128:(qh + 1) * 128, :]
            .rearrange("p (c d) -> p c d", c=NT)), inc="ld", inc_amt=16)
    ldall = p.cnt("ld")
    p.op("vector", lambda e: e.memset(onesf[:, :], 1.0), inc="v")
    mset = p.cnt("v")

    c_ps = p.psum(nm + "_cps", [128, 8])
    m_ps = p.psum(nm + "_mps", [8, 8])
    cy_ps = p.psum(nm + "_cyps", [1, 8])
    tt = None
    for c in range(16):
        tt = p.op("tensor", lambda e, c=c: e.matmul(
            cy_ps[0:1, :], csel[:, c:c + 1], lgf[:, c * 8:(c + 1) * 8],
            start=(c == 0), stop=(c == 15)),
            waits=([("ld", ldall)] + p.wof(cy_ps) if c == 0 else ()),
            inc=("t" if c == 15 else None))
    vcy = p.op("vector", lambda e: e.tensor_copy(crow[0:1, :], cy_ps[0:1, :]),
               waits=[("t", tt)], inc="v")
    tt = None
    for c in range(16):
        tt = p.op("tensor", lambda e, c=c: e.matmul(
            m_ps[0:8, :], psel[:, c * 8:(c + 1) * 8], lgf[:, c * 8:(c + 1) * 8],
            start=(c == 0), stop=(c == 15)),
            waits=([("ld", ldall)] + p.wof(m_ps) if c == 0 else ()),
            inc=("t" if c == 15 else None))
    vm = p.op("vector", lambda e: e.tensor_copy(msb[:, :], m_ps[0:8, :]),
              waits=[("t", tt)], inc="v")
    mtmp = p.dram(nm + "_mtmp", [8, 8], F32)
    dm = p.op("sync", lambda e: e.dma_start(out=mtmp.ap(), in_=msb[:, :]),
              waits=[("v", vm)], inc="ld", inc_amt=16)
    dm = p.op("sync", lambda e: e.dma_start(
        out=mrow[0:1, :], in_=mtmp.ap().rearrange("a b -> (a b)")),
        waits=[("ld", dm)], inc="ld", inc_amt=16)
    mg_ps = p.psum(nm + "_mgps", [128, 64])
    tt = p.op("tensor", lambda e: e.matmul(
        mg_ps[:, :], onesf[0:1, 0:128], mrow[0:1, :], start=True, stop=True),
        waits=[("ld", dm), ("v", mset)] + p.wof(mg_ps), inc="t")
    vmg = p.op("vector", lambda e: e.tensor_copy(mgrid[:, :], mg_ps[:, :]),
               waits=[("t", p.cnt("t"))], inc="v")
    for c in range(16):
        tt = None
        for j in range(c):
            tt = p.op("tensor", lambda e, j=j: e.matmul(
                c_ps[:, :], onesf[:, :], lgf[:, j * 8:(j + 1) * 8],
                start=(j == 0), stop=False),
                waits=([("ld", ldall), ("v", p.cnt("v"))] + p.wof(c_ps)
                       if (c == 0 and j == 0) else
                       ([("v", p.cnt("v"))] if j == 0 else ())))
        tt = p.op("tensor", lambda e, c=c: e.matmul(
            c_ps[:, :], tri[:, :], lgf[:, c * 8:(c + 1) * 8],
            start=(c == 0), stop=True),
            waits=([("ld", ldall), ("v", p.cnt("v"))] + p.wof(c_ps)
                   if c == 0 else ([("v", p.cnt("v"))] if c == 0 else ())),
            inc="t")
        p.op("vector", lambda e, c=c: e.tensor_copy(
            cumg[:, c * 8:(c + 1) * 8], c_ps[:, :]),
            waits=[("t", p.cnt("t"))], inc="v")
    for c in range(8):
        tt = None
        for j in range(c):
            tt = p.op("tensor", lambda e, j=j, c=c: e.matmul(
                c_ps[:, :], onesf[:, :], lgo[:, j * 8:(j + 1) * 8],
                start=(j == 0), stop=False),
                waits=([("v", p.cnt("v"))] if j == 0 else ()))
        tt = p.op("tensor", lambda e, c=c: e.matmul(
            c_ps[:, :], tri[:, :], lgo[:, c * 8:(c + 1) * 8],
            start=(c == 0), stop=False),
            waits=([("v", p.cnt("v"))] if c == 0 else ()))
        tt = p.op("tensor", lambda e: e.matmul(
            c_ps[:, :], onesf[0:1, 0:128], crow[0:1, :],
            start=False, stop=True), waits=[("v", vcy)], inc="t")
        p.op("vector", lambda e, c=c: e.tensor_copy(
            ocum[:, c * 8:(c + 1) * 8], c_ps[:, :]),
            waits=[("t", p.cnt("t"))], inc="v")
    cum_v = p.cnt("v")
    for i in range(8):
        v1 = p.op("vector", lambda e, i=i: e.tensor_tensor(
            dsc[:, :].rearrange("p (c h) -> p c h", c=16),
            _bcast_mid(mgrid[:, i * 8:(i + 1) * 8], 16),
            cumg[:, :].rearrange("p (c h) -> p c h", c=16), ALU.subtract),
            waits=([("v", max(cum_v, vmg))] if i == 0 else
                   [("a", p.cnt("a"))]), inc="v")
        v2 = p.op("vector", lambda e, i=i: e.tensor_scalar(
            dsc[:, :], dsc[:, :], 60.0, None, ALU.min),
            waits=[("v", v1)], inc="v")
        a1 = p.op("scalar", lambda e, i=i: e.activation(
            esf[:, i * 128:(i + 1) * 128], dsc[:, :], AF.Exp),
            waits=[("v", v2)] + (p.wof() if i else []), inc="a")
        v3 = p.op("vector", lambda e, i=i: e.tensor_tensor(
            esf[:, i * 128:(i + 1) * 128].rearrange("p (c h) -> p c h", c=16),
            esf[:, i * 128:(i + 1) * 128].rearrange("p (c h) -> p c h", c=16),
            _bcast_mid(vb[:, i * 16:(i + 1) * 16], 8, pos=2),
            ALU.mult), waits=[("a", a1)], inc="v")
        v4 = p.op("vector", lambda e, i=i: e.tensor_tensor(
            dsc[:, 0:8], ocum[:, i * 8:(i + 1) * 8],
            mgrid[:, i * 8:(i + 1) * 8], ALU.subtract),
            waits=[("v", v3)], inc="v")
        a2 = p.op("scalar", lambda e, i=i: e.activation(
            ugrid[:, i * 8:(i + 1) * 8], dsc[:, 0:8], AF.Exp),
            waits=[("v", v4)], inc="a")
    prep_a = p.cnt("a")
    prep_v = p.cnt("v")
    p.free(c_ps, m_ps, cy_ps, mg_ps, psel, csel, tri, lgf, lgo, dsc, msb,
           mrow, crow, mgrid)

    y_tm = p.sbuf(nm + "_ytm", [128, 8 * NQ * D], BF16)
    sqb = p.sbuf(nm + "_sq", [128, 2 * 1024], BF16)
    atb = p.sbuf(nm + "_at", [128, 2 * 1024], BF16)
    nsc = p.sbuf(nm + "_nsc", [128, 4 * 8], F32)
    s_ps = [p.psum(nm + "_sps%d" % i, [128, 1024]) for i in range(2)]
    y_ps = p.psum(nm + "_yps", [128, 1536])
    slot = 0
    acts = {}
    pend_av = None
    for grp in range(2):
        for i in range(8):
            for j in range(9 + i):
                par = slot % 2
                tw = [("a", prep_a), ("v", prep_v), ("ld", ldall)]
                if slot == 0:
                    tw += p.wof(*s_ps, y_ps)
                if slot >= 2:
                    tw.append(("a", acts[slot - 2]))
                for kk in range(4):
                    kvh = grp * 4 + kk
                    p.op("tensor", lambda e, i=i, j=j, kvh=kvh, kk=kk, par=par: e.matmul(
                        s_ps[par][:, kk * 256:(kk + 1) * 256],
                        ksb[:, kvh * 2 * T + j * 128:kvh * 2 * T + (j + 1) * 128],
                        qsb[:, kvh * NT * 256 + i * 256:kvh * NT * 256 + (i + 1) * 256],
                        start=True, stop=True),
                        waits=(tw if kk == 0 else ()),
                        inc=("t" if kk == 3 else None))
                tqk = p.cnt("t")
                if pend_av is not None:
                    pend_av()
                    pend_av = None
                aa = p.op("scalar", lambda e, par=par: e.activation(
                    sqb[:, par * 1024:(par + 1) * 1024], s_ps[par][:, :],
                    AF.Square, scale=0.08838834764831845),
                    waits=[("t", tqk)] +
                          (p.wof(sqb, atb, nsc, y_tm) if slot == 0 else []) +
                          ([("v", p.cnt("v"))] if slot >= 2 else []),
                    inc="a")
                acts[slot] = aa
                vv = p.op("vector", lambda e, i=i, j=j, grp=grp, par=par: e.tensor_tensor(
                    atb[:, par * 1024:(par + 1) * 1024]
                    .rearrange("p (k t) -> p k t", k=4),
                    sqb[:, par * 1024:(par + 1) * 1024]
                    .rearrange("p (k t) -> p k t", k=4),
                    _bcast_mid(esf[:, i * 128 + j * 8 + grp * 4:
                                   i * 128 + j * 8 + grp * 4 + 4], 256, pos=2),
                    ALU.mult),
                    waits=[("a", aa)] + ([("t", p.cnt("t"))] if slot >= 2 else []),
                    inc="v")
                if j == i or j == i + 8:
                    sel = 0 if j == i else 1
                    vv = p.op("vector", lambda e, i=i, sel=sel, par=par: e.tensor_tensor(
                        atb[:, par * 1024:(par + 1) * 1024]
                        .rearrange("p (k t) -> p k t", k=8),
                        atb[:, par * 1024:(par + 1) * 1024]
                        .rearrange("p (k t) -> p k t", k=8),
                        _bcast_mid(msk[:, i * 256 + sel * 128:
                                       i * 256 + (sel + 1) * 128], 8),
                        ALU.mult), waits=[("v", vv)], inc="v")

                def make_av(i=i, j=j, grp=grp, par=par, vv=vv):
                    def emit():
                        for kk in range(4):
                            kvh = grp * 4 + kk
                            for pp in range(2):
                                gi = kk * 2 + pp
                                p.op("tensor", lambda e, i=i, j=j, kvh=kvh,
                                     kk=kk, pp=pp, gi=gi, par=par: e.matmul(
                                    y_ps[:, _yoff(gi):_yoff(gi) + 129],
                                    atb[:, par * 1024 + (kk * 2 + pp) * 128:
                                        par * 1024 + (kk * 2 + pp + 1) * 128],
                                    vsb[:, j * NKV * 129 + kvh * 129:
                                        j * NKV * 129 + (kvh + 1) * 129],
                                    start=(j == 0 and gi in (0, 3, 6)),
                                    stop=(j == 8 + i)),
                                    waits=([("v", vv)] if (kk == 0 and pp == 0)
                                           else ()),
                                    inc=("t" if (kk == 3 and pp == 1) else None))
                    return emit
                pend_av = make_av()
                slot += 1
            pend_av()
            pend_av = None
            p.op("tensor", lambda e: e.matmul(
                y_ps[0:1, 1535:1536], onesf[0:1, 0:1], onesf[0:1, 0:1],
                start=True, stop=True), inc="t")
            tav = p.cnt("t")
            for gi2 in range(8):
                p.op("vector", lambda e, gi2=gi2: e.tensor_copy(
                    nsc[:, gi2:gi2 + 1], y_ps[:, _yoff(gi2) + 128:_yoff(gi2) + 129]),
                    waits=([("t", tav)] if gi2 == 0 else ()), inc="v")
            vd = p.op("vector", lambda e, i=i, grp=grp: e.tensor_tensor(
                nsc[:, 8:16].rearrange("p (k q) -> p k q", k=4),
                nsc[:, 0:8].rearrange("p (k q) -> p k q", k=4),
                _bcast_mid(ugrid[:, i * 8 + grp * 4:i * 8 + grp * 4 + 4], 2,
                           pos=2), ALU.mult), waits=[("v", p.cnt("v"))], inc="v")
            vd = p.op("vector", lambda e: e.tensor_scalar(
                nsc[:, 8:16], nsc[:, 8:16], 1e-6, None, ALU.add),
                waits=[("v", vd)], inc="v")
            vd = p.op("vector", lambda e: e.reciprocal(
                nsc[:, 16:24], nsc[:, 8:16]), waits=[("v", vd)], inc="v")
            vd = p.op("vector", lambda e, i=i, grp=grp: e.tensor_tensor(
                nsc[:, 24:32].rearrange("p (k q) -> p k q", k=4),
                nsc[:, 16:24].rearrange("p (k q) -> p k q", k=4),
                _bcast_mid(ugrid[:, i * 8 + grp * 4:i * 8 + grp * 4 + 4], 2,
                           pos=2), ALU.mult), waits=[("v", vd)], inc="v")
            for gi2 in range(8):
                qh = grp * 8 + gi2
                vd = p.op("vector", lambda e, i=i, qh=qh, gi2=gi2: e.tensor_scalar(
                    y_tm[:, i * 2048 + qh * 128:i * 2048 + (qh + 1) * 128],
                    y_ps[:, _yoff(gi2):_yoff(gi2) + 128],
                    nsc[:, 24 + gi2:25 + gi2], None, ALU.mult),
                    waits=([("v", vd)] if gi2 == 0 else ()), inc="v")
    ydone_v = p.cnt("v")
    p.free(sqb, atb, nsc, *s_ps, y_ps, ksb, qsb, vsb, esf, ugrid, cumg, ocum,
           vb, msk)

    yfm = p.sbuf(nm + "_yfm", [128, NQ * T], BF16)
    t_ps = [p.psum(nm + "_tps%d" % i2, [128, 128], BF16) for i2 in range(2)]
    k = 0
    for qh in range(NQ):
        for i in range(8):
            tt = p.op("tensor", lambda e, qh=qh, i=i, k=k: e.transpose(
                t_ps[k % 2][:, :],
                y_tm[:, i * 2048 + qh * 128:i * 2048 + (qh + 1) * 128],
                ident[:, :]),
                waits=([("v", ydone_v)] + p.wof(*t_ps, yfm) if k == 0 else
                       ([("v", p.cnt("v"))] if k >= 2 else ())),
                inc="t")
            p.op("vector", lambda e, qh=qh, i=i, k=k: e.tensor_copy(
                yfm[:, qh * T + i * 128:qh * T + (i + 1) * 128],
                t_ps[k % 2][:, :]),
                waits=[("t", p.cnt("t"))], inc="v")
            k += 1
    yfm_v = p.cnt("v")
    p.free(y_tm, *t_ps, ident)

    wsb = p.sbuf(nm + "_wc", [128, NH * H], BF16)
    bcs = p.sbuf(nm + "_bc", [128, 16], F32)
    w_ps = [p.psum(nm + "_wps%d" % i2, [128, T]) for i2 in range(2)]
    xres = p.sbuf(nm + "_xr", [128, 2 * T], F32)
    xo = p.sbuf(nm + "_xo", [128, 2 * T], F32)
    wv = None
    for hc in range(NH):
        wv = p.op("gpsimd", lambda e, hc=hc: e.dma_start(
            out=wsb[:, hc * H:(hc + 1) * H], in_=wc_dram[hc]),
            waits=(p.wof(wsb) if hc == 0 else ()), inc="w", inc_amt=16)
    bl = p.op("sync", lambda e: e.dma_start(out=bcs[:, :], in_=bc_dram.ap()),
              waits=p.wof(bcs), inc="ld", inc_amt=16)
    st_base = p.cnt("st")
    for oc in range(NH):
        par = oc % 2
        lr = p.op("sync", lambda e, oc=oc, par=par: e.dma_start(
            out=xres[:, par * T:(par + 1) * T],
            in_=x_dram[oc * 128:(oc + 1) * 128, :]),
            waits=(p.wof(xres, xo) if oc == 0 else
                   ([("v", p.cnt("v"))] if oc >= 2 else [])),
            inc="ld", inc_amt=16)
        tw = [("w", wv), ("v", yfm_v)]
        if oc == 0:
            tw += p.wof(*w_ps)
        if oc >= 2:
            tw.append(("v", p.cnt("v")))
        for qc in range(NQ):
            for half in range(2):
                p.op("tensor", lambda e, oc=oc, par=par, qc=qc, half=half: e.matmul(
                    w_ps[par][:, half * 512:(half + 1) * 512],
                    wsb[:, qc * H + oc * 128:qc * H + (oc + 1) * 128],
                    yfm[:, qc * T + half * 512:qc * T + (half + 1) * 512],
                    start=(qc == 0), stop=(qc == NQ - 1)),
                    waits=(tw if (qc == 0 and half == 0) else ()),
                    inc=("t" if (qc == NQ - 1 and half == 1) else None))
        tdd = p.cnt("t")
        vw = [("t", tdd), ("ld", max(lr, bl))]
        if oc >= 2:
            vw.append(("st", st_base + 16 * (oc - 1)))
        vo = p.op("vector", lambda e, oc=oc, par=par: e.scalar_tensor_tensor(
            xo[:, par * T:(par + 1) * T], w_ps[par][:, :], bcs[:, oc:oc + 1],
            xres[:, par * T:(par + 1) * T], ALU.add, ALU.add),
            waits=vw, inc="v")
        p.op("sync", lambda e, oc=oc, par=par: e.dma_start(
            out=xout_dram[oc * 128:(oc + 1) * 128, :],
            in_=xo[:, par * T:(par + 1) * T]), waits=[("v", vo)],
            inc="st", inc_amt=16)
    st_done = p.cnt("st")
    p.free(yfm, wsb, bcs, xres, xo, *w_ps)
    return st_done



def build_fused():
    p = Prog()
    x_in = p.dram("x_in", [H, T], F32, kind="ExternalInput")
    wg0 = p.dram("wg0", [NI0, 128, NH * 128], I32, kind="ExternalInput")
    wu0 = p.dram("wu0", [NI0, 128, NH * 128], I32, kind="ExternalInput")
    wd0 = p.dram("wd0", [NH, 128, NI0 * 128], I32, kind="ExternalInput")
    sg0 = p.dram("sg0", [128, NI0], F32, kind="ExternalInput")
    su0 = p.dram("su0", [128, NI0], F32, kind="ExternalInput")
    sd0 = p.dram("sd0", [128, NH], F32, kind="ExternalInput")
    wq0 = p.dram("wq0", [NH, 128, H], F32, kind="ExternalInput")
    wk0 = p.dram("wk0", [NH, 128, H], F32, kind="ExternalInput")
    wv0 = p.dram("wv0", [NH, 128, H], F32, kind="ExternalInput")
    wo0 = p.dram("wo0", [NH, 128, H], F32, kind="ExternalInput")
    wg1 = p.dram("wg1", [NI1, 128, NH * 128], I32, kind="ExternalInput")
    wu1 = p.dram("wu1", [NI1, 128, NH * 128], I32, kind="ExternalInput")
    wd1 = p.dram("wd1", [NH, 128, NI1 * 128], I32, kind="ExternalInput")
    sg1 = p.dram("sg1", [128, NI1], F32, kind="ExternalInput")
    su1 = p.dram("su1", [128, NI1], F32, kind="ExternalInput")
    sd1 = p.dram("sd1", [128, NH], F32, kind="ExternalInput")
    wq1 = p.dram("wq1", [NH, 128, H], F32, kind="ExternalInput")
    wk1 = p.dram("wk1", [NH, 128, 1024], F32, kind="ExternalInput")
    wv1 = p.dram("wv1", [NH, 128, 1024], F32, kind="ExternalInput")
    wgg1 = p.dram("wgg1", [NH, 128, 8], F32, kind="ExternalInput")
    wc = p.dram("wc", [NH, 128, H], F32, kind="ExternalInput")
    bc = p.dram("bc", [128, 16], F32, kind="ExternalInput")
    ident = p.dram("ident", [128, 128], BF16, kind="ExternalInput")
    identf = p.dram("identf", [128, 128], F32, kind="ExternalInput")
    trif = p.dram("trif", [128, 128], F32, kind="ExternalInput")
    psel = p.dram("psel", [16, 128, 8], F32, kind="ExternalInput")
    csel = p.dram("csel", [16, 128, 1], F32, kind="ExternalInput")
    vb = p.dram("vb", [8, 128, 16], F32, kind="ExternalInput")
    masks = p.dram("masks", [8, 2, 128, 128], BF16, kind="ExternalInput")
    nflags = p.dram("nflags", [1, 8], I32, kind="ExternalInput")
    out = p.dram("out", [H, T], F32, kind="ExternalOutput")
    x1 = p.dram("x1", [H, T], F32)
    x2 = p.dram("x2", [H, T], F32)
    x3 = p.dram("x3", [H, T], F32)
    x4 = p.dram("x4", [H, T], F32)
    x5 = p.dram("x5", [H, T], F32)
    v0 = p.dram("v0", [H, T], BF16)
    q1d = p.dram("q1d", [H, T], BF16)
    q0l = p.dram("q0l", [T, H], BF16)
    k0l = p.dram("k0l", [T, H], BF16)
    k1l = p.dram("k1l", [1024, T], BF16)
    v1l = p.dram("v1l", [T, 1024], BF16)
    lgl = p.dram("lgl", [T, 8], F32)
    shq0 = p.dram("shq0", [2, T, H], BF16, addr_space="Shared")
    shk0 = p.dram("shk0", [2, T, H], BF16, addr_space="Shared")
    shk1 = p.dram("shk1", [2, 1024, T], BF16, addr_space="Shared")
    shv1 = p.dram("shv1", [2, T, 1024], BF16, addr_space="Shared")
    shlg = p.dram("shlg", [2, T, 8], F32, addr_space="Shared")
    shf = p.dram("shf", [2, 8], I32, addr_space="Shared")

    nf = p.sbuf("nfb", [1, 8], I32)
    ft = p.sbuf("ftb", [1, 8], I32)
    p.op("gpsimd", lambda e: e.memset(ft[0:1, :], 0),
         waits=p.wof(nf, ft), inc="w")
    nfw = p.op("gpsimd", lambda e: e.dma_start(
        out=nf[0:1, :], in_=nflags.ap()), inc="w", inc_amt=16)

    emit_ffn(p, x_in, x1, wg0, wu0, wd0, sg0, su0, sd0, NI0, "f0a")
    p.wait("sync", ("st", p.cnt("st")))
    p.barrier()
    xln = p.sbuf("l1_xln", [128, NH * T], BF16)
    lnv = emit_ln(p, x1, xln, "l1_ln")
    emit_proj(p, xln, wq0, q0l, H, "tm", "pq", lnv)
    stq0 = p.ds_copy(shq0, q0l, waits=[("st", p.cnt("st")), ("w", nfw)])
    emit_proj(p, xln, wk0, k0l, H, "tm", "pk", lnv)
    stqk = p.ds_copy(shk0, k0l, waits=[("st", p.cnt("st"))])
    p.pair_signal(0, shf, nf, waits=[("st", stqk)])
    emit_proj(p, xln, wv0, v0, H, "fm", "pv", lnv)
    pollA = p.pair_poll(0, shf, nf, ft, waits=[("w", nfw)])
    p.free(xln)
    p.wait("sync", ("st", p.cnt("st")))
    p.barrier()
    emit_attn0(p, x1, x2, shq0, shk0, v0, wo0, ident, "a0",
               gate=[("pp", pollA)])
    p.wait("sync", ("st", p.cnt("st")))
    p.barrier()
    emit_ffn(p, x2, x3, wg0, wu0, wd0, sg0, su0, sd0, NI0, "f0b")
    p.wait("sync", ("st", p.cnt("st")))
    p.barrier()
    emit_ffn(p, x3, x4, wg1, wu1, wd1, sg1, su1, sd1, NI1, "f1a")
    p.wait("sync", ("st", p.cnt("st")))
    p.barrier()
    xln2 = p.sbuf("l2_xln", [128, NH * T], BF16)
    lnv2 = emit_ln(p, x4, xln2, "l2_ln")
    emit_proj(p, xln2, wq1, q1d, H, "fm", "pq1", lnv2)
    emit_proj(p, xln2, wk1, k1l, 1024, "fm", "pk1", lnv2)
    p.ds_copy(shk1, k1l, waits=[("st", p.cnt("st"))])
    emit_proj(p, xln2, wv1, v1l, 1024, "tm", "pv1", lnv2)
    p.ds_copy(shv1, v1l, waits=[("st", p.cnt("st"))])
    emit_proj_g(p, xln2, wgg1, lgl, identf, "pg1", lnv2)
    lgst = p.ds_copy(shlg, lgl, waits=[("st", p.cnt("st"))])
    p.pair_signal(1, shf, nf, waits=[("st", lgst)])
    pollB = p.pair_poll(1, shf, nf, ft, waits=[("w", nfw)])
    p.free(xln2)
    p.wait("sync", ("st", p.cnt("st")))
    p.barrier()
    emit_attn1(p, x4, x5, q1d, shk1, shv1, shlg, lgl, psel, csel, vb, masks,
               trif, ident, wc, bc, "a1", gate=[("pp", pollB)])
    p.wait("sync", ("st", p.cnt("st")))
    p.barrier()
    emit_ffn(p, x5, out, wg1, wu1, wd1, sg1, su1, sd1, NI1, "f1b")
    p.wait("sync", ("st", p.cnt("st")), ("w", p.cnt("w")), ("ld", p.cnt("ld")))
    return p.finalize()



def _block_lhsT(w_t, nin, nout):
    NK, NO = nin // 128, nout // 128
    a = w_t.reshape(NK, 128, NO, 128)
    return np.ascontiguousarray(a.transpose(2, 1, 0, 3).reshape(NO, 128, NK * 128))


def _scale_cols(s):
    return np.ascontiguousarray(np.asarray(s).reshape(-1, 128).T)


def _nat_chunks(w):
    w = np.asarray(w)
    return np.ascontiguousarray(w.reshape(w.shape[0] // 128, 128, w.shape[1]))


def _shard_x(x):
    out = []
    for b in range(4):
        for hhalf in range(2):
            out.append(np.ascontiguousarray(
                x[b, hhalf * T:(hhalf + 1) * T, :].T))
    return out


_progs = {}

import os as _os
_NONCE_BASE = (int.from_bytes(_os.urandom(4), "little") & 0x1fffffff) | (1 << 24)
_nonce_ctr = 0


def _get_prog(name):
    if name not in _progs:
        _progs[name] = {"fused": build_fused}[name]()
    return _progs[name]


def _prep_weights(inputs):
    w = {}
    w["wg0"] = _block_lhsT(np.ascontiguousarray(np.asarray(inputs["L0_ffn_gate_w"]).T), H, I0)
    w["wu0"] = _block_lhsT(np.ascontiguousarray(np.asarray(inputs["L0_ffn_up_w"]).T), H, I0)
    w["wd0"] = _block_lhsT(np.ascontiguousarray(np.asarray(inputs["L0_ffn_down_w"]).T), I0, H)
    w["sg0"] = _scale_cols(inputs["L0_ffn_gate_scale"])
    w["su0"] = _scale_cols(inputs["L0_ffn_up_scale"])
    w["sd0"] = _scale_cols(inputs["L0_ffn_down_scale"])
    w["wq0"] = _nat_chunks(inputs["L0_attn_Wq"])
    w["wk0"] = _nat_chunks(inputs["L0_attn_Wk"])
    w["wv0"] = _nat_chunks(inputs["L0_attn_Wv"])
    w["wo0"] = _nat_chunks(inputs["L0_attn_Wo"])
    w["wg1"] = _block_lhsT(np.ascontiguousarray(np.asarray(inputs["L1_ffn_wq_gate"]).T), H, I1)
    w["wu1"] = _block_lhsT(np.ascontiguousarray(np.asarray(inputs["L1_ffn_wq_up"]).T), H, I1)
    w["wd1"] = _block_lhsT(np.ascontiguousarray(np.asarray(inputs["L1_ffn_wq_down"]).T), I1, H)
    w["sg1"] = _scale_cols(inputs["L1_ffn_s_gate"])
    w["su1"] = _scale_cols(inputs["L1_ffn_s_up"])
    w["sd1"] = _scale_cols(inputs["L1_ffn_s_down"])
    w["wq1"] = _nat_chunks(inputs["L1_attn_Wq"])
    w["wk1"] = _nat_chunks(inputs["L1_attn_Wk"])
    w["wv1"] = _nat_chunks(inputs["L1_attn_Wv"])
    w["wgg1"] = _nat_chunks(inputs["L1_attn_Wg"])
    w["wc"] = _nat_chunks(inputs["L1_attn_Wc"])
    w["bc"] = _scale_cols(inputs["L1_attn_bc"])
    return w


def _const_inputs():
    c = {}
    c["ident_f32"] = np.eye(128, dtype=np.float32)
    c["ident_bf16"] = np.eye(128, dtype=np.float32).astype(BF)
    c["trif"] = np.triu(np.ones((128, 128), dtype=np.float32))
    half = {}
    for hh in range(2):
        d = {}
        ps = np.zeros((16, 128, 8), dtype=np.float32)
        for i in range(8):
            lim = (8 * hh + i) * 128
            for cch in range(16):
                for s in range(128):
                    if cch * 128 + s <= lim:
                        ps[cch, s, i] = 1.0
        d["psel"] = ps
        cs = np.zeros((16, 128, 1), dtype=np.float32)
        cs[:8 * hh] = 1.0
        d["csel"] = cs
        vbm = np.zeros((8, 128, 16), dtype=np.float32)
        for i in range(8):
            gi = 8 * hh + i
            vbm[i, :, :gi + 1] = 1.0
        d["vb"] = vbm
        mk = np.zeros((8, 2, 128, 128), dtype=np.float32)
        tri = np.triu(np.ones((128, 128), dtype=np.float32))
        for i in range(8):
            if hh == 0:
                mk[i, 0] = tri
                mk[i, 1] = 0.0
            else:
                mk[i, 0] = 1.0
                mk[i, 1] = tri
        d["masks"] = mk.astype(BF)
        half[hh] = d
    c["half"] = half
    return c


def kernel(**inputs):
    inputs = {k: np.asarray(v) for k, v in inputs.items()}
    x = inputs["L0_attn_x"]
    w = _prep_weights(inputs)
    cst = _const_inputs()
    xs = _shard_x(x)
    cores = list(range(8))

    global _nonce_ctr
    _nonce_ctr += 16
    nonce = (_NONCE_BASE + _nonce_ctr) & 0x3fffffff
    nflags = np.zeros((1, 8), np.int32)
    nflags[0, 0] = nonce + 1
    nflags[0, 1] = nonce + 2

    nc = _get_prog("fused")
    in_maps = []
    for c in cores:
        hh = c % 2
        d = cst["half"][hh]
        in_maps.append(dict(
            x_in=xs[c],
            wg0=w["wg0"], wu0=w["wu0"], wd0=w["wd0"],
            sg0=w["sg0"], su0=w["su0"], sd0=w["sd0"],
            wq0=w["wq0"], wk0=w["wk0"], wv0=w["wv0"], wo0=w["wo0"],
            wg1=w["wg1"], wu1=w["wu1"], wd1=w["wd1"],
            sg1=w["sg1"], su1=w["su1"], sd1=w["sd1"],
            wq1=w["wq1"], wk1=w["wk1"], wv1=w["wv1"], wgg1=w["wgg1"],
            wc=w["wc"], bc=w["bc"],
            ident=cst["ident_bf16"], identf=cst["ident_f32"],
            trif=cst["trif"],
            psel=d["psel"], csel=d["csel"], vb=d["vb"], masks=d["masks"],
            nflags=nflags))
    res = run_bass_kernel_spmd(nc, in_maps, core_ids=cores).results

    out = np.empty((4, 2048, H), dtype=np.float32)
    for c in cores:
        b, hh = c // 2, c % 2
        out[b, hh * T:(hh + 1) * T, :] = res[c]["out"].T
    return out


# version 09 (final)
# speedup vs baseline: 1.2096x; 1.0297x over previous
import numpy as np
import ml_dtypes
from contextlib import ExitStack

import concourse.bass as bass
import concourse.mybir as mybir
from concourse.engine_type import EngineType
from concourse.bass_utils import run_bass_kernel_spmd

F32 = mybir.dt.float32
BF16 = mybir.dt.bfloat16
I32 = mybir.dt.int32
AF = mybir.ActivationFunctionType
ALU = mybir.AluOpType
BF = ml_dtypes.bfloat16

H = 2048
T = 1024
NH = H // 128
NT = T // 128
I0, I1 = 8192, 5632
NI0, NI1 = I0 // 128, I1 // 128
GATE_OFFSET = 6.906768
ENGINES = ("tensor", "vector", "scalar", "gpsimd", "sync")


class Prog:

    SBUF_SIZE = 212736
    PSUM_BANKS = 8

    def __init__(self):
        self.nc = bass.Bass("TRN2", target_bir_lowering=False)
        self.es = ExitStack()
        self.sched = []
        self.sems = {}
        self._const_vals = {}
        self._sb_arena = self.nc.alloc_sbuf_tensor(
            "__arena", [128, self.SBUF_SIZE], mybir.dt.uint8)
        self._sb_base = self.nc.lookup_mloc(self._sb_arena).addr
        self._sb_free = [(0, self.SBUF_SIZE, {})]
        self._ps_arena = self.nc.alloc_psum_tensor(
            "__psarena", [128, 4096], F32)
        self._ps_base = self.nc.lookup_mloc(self._ps_arena).addr
        self._ps_free = [(0, self.PSUM_BANKS, {})]
        self._alloc_info = {}
        self._waits = {}
        self.sb_peak = 0
        self._const_tile = self.sbuf("__consts", [128, 32], F32)

    @staticmethod
    def _take(freelist, need):
        for i, (start, size, conds) in enumerate(freelist):
            if size >= need:
                if size == need:
                    freelist.pop(i)
                else:
                    freelist[i] = (start + need, size - need, conds)
                return start, conds
        raise MemoryError(f"arena OOM: need {need}, free={freelist}")

    @staticmethod
    def _give(freelist, start, size, conds):
        freelist.append((start, size, conds))
        freelist.sort()
        merged = []
        for s, n, c in freelist:
            if merged and merged[-1][0] + merged[-1][1] == s:
                ps, pn, pc = merged[-1]
                mc = dict(pc)
                for k, v in c.items():
                    mc[k] = max(mc.get(k, 0), v)
                merged[-1] = (ps, pn + n, mc)
            else:
                merged.append((s, n, c))
        freelist[:] = merged

    def _place(self, h, addr):
        self.nc._tensor(h.name, h.shape, h.dtype, type="SB")
        mloc = self.nc.lookup_mloc(h)
        mloc.allocated = True
        mloc.addr = addr

    def parity(self):
        if not hasattr(self, "_parity"):
            pid = self.nc.partition_id([EngineType.Pool])
            self._parity = (pid % 2, (pid + 1) % 2)
        return self._parity

    def ds_copy(self, sh, local, waits, name=""):
        def cp(eng):
            slot, _ = self.parity()
            return eng.dma_start(out=sh.ap()[bass.ds(slot, 1)],
                                 in_=local.ap())
        self.op("gpsimd", cp, waits=waits, inc="st", inc_amt=16)
        return self.cnt("st")

    def pair_signal(self, idx, shf, nf, waits):
        def sig(eng):
            slot, _ = self.parity()
            return eng.dma_start(
                out=shf.ap()[bass.ds(slot, 1), idx:idx + 1],
                in_=nf[0:1, idx:idx + 1])
        self.op("gpsimd", sig, waits=waits, inc="st", inc_amt=16)

    def pair_poll(self, idx, shf, nf, ft, waits=()):
        nc = self.nc
        fp = self.sem("fpoll")[0]

        def poll(eng):
            _, other = self.parity()
            if not hasattr(self, "_pregs"):
                self._pregs = (
                    nc.alloc_registers("ptgt", engines=[EngineType.Pool]),
                    nc.alloc_registers("pcur", engines=[EngineType.Pool]),
                    nc.alloc_registers("pcnt", engines=[EngineType.Pool]))
                self._pscr = self.sbuf("_pollscr", [128, 2048], F32)
            tgt, cur, cnt = self._pregs
            scr = self._pscr
            nc.regs_load(tgt, nf[0:1, idx:idx + 1])
            nc.regs_mov(cnt, 0)
            loop, bodyb, done = f"pl{idx}", f"plb{idx}", f"pld{idx}"
            nc.br(loop, engines=[EngineType.Pool])
            with nc.body(loop, valid_engines=[EngineType.Pool]):
                eng.dma_start(
                    out=ft[0:1, idx:idx + 1],
                    in_=shf.ap()[bass.ds(other, 1), idx:idx + 1]
                ).then_inc(fp, 16)
                for _ in range(2):
                    eng.memset(scr[:, :], 0.0)
                for _ in range(24):
                    nc.regs_alu(cnt, cnt, 1, op=mybir.AluOpType.add)
                nc.regs_load(cur, ft[0:1, idx:idx + 1])
                nc.br_cmp(cur, tgt, done, bodyb, "IS_EQ",
                          engines=[EngineType.Pool])
                with nc.body(bodyb, valid_engines=[EngineType.Pool]):
                    nc.br_cmp(cnt, 100000, loop, done, "IS_LT",
                              engines=[EngineType.Pool])
            nc.switch_bb(done)
            return None
        return self.op("gpsimd", poll, waits=list(waits), inc="pp")

    def sem(self, name):
        if name not in self.sems:
            h = self.es.enter_context(self.nc.semaphore(name))
            self.sems[name] = [h, 0]
        return self.sems[name]

    def cnt(self, name):
        return self.sems[name][1] if name in self.sems else 0

    def sbuf(self, name, shape, dtype):
        nbytes = int(np.prod(shape[1:])) * mybir.dt.size(dtype)
        nbytes = (nbytes + 63) & ~63
        start, conds = self._take(self._sb_free, nbytes)
        h = self.nc.virtual_sbuf_tensor(name, list(shape), dtype)
        self._place(h, self._sb_base + start)
        self._alloc_info[id(h)] = ("sb", start, nbytes)
        self._waits[id(h)] = conds
        used = self.SBUF_SIZE - sum(e[1] for e in self._sb_free)
        self.sb_peak = max(self.sb_peak, used)
        return h

    def psum(self, name, shape, dtype=F32):
        nbytes = int(np.prod(shape[1:])) * mybir.dt.size(dtype)
        banks = (nbytes + 2047) // 2048
        start, conds = self._take(self._ps_free, banks)
        h = self.nc.virtual_psum_tensor(name, list(shape), dtype)
        self.nc._tensor(h.name, h.shape, h.dtype, type="PSUM")
        mloc = self.nc.lookup_mloc(h)
        mloc.allocated = True
        mloc.addr = self._ps_base + start * 2048
        self._alloc_info[id(h)] = ("ps", start, banks)
        self._waits[id(h)] = conds
        return h

    def wof(self, *handles):
        out = {}
        for h in handles:
            for k, v in self._waits.get(id(h), {}).items():
                out[k] = max(out.get(k, 0), v)
        return list(out.items())

    def free(self, *handles):
        snap = {n: s[1] for n, s in self.sems.items() if n != "BAR"}
        for h in handles:
            kind, start, size = self._alloc_info.pop(id(h))
            self._waits.pop(id(h), None)
            self._give(self._sb_free if kind == "sb" else self._ps_free,
                       start, size, snap)

    def dram(self, name, shape, dtype, kind="Internal", addr_space="Local"):
        return self.nc.dram_tensor(name, shape, dtype, kind=kind,
                                   addr_space=addr_space)

    def op(self, engine, fn, waits=(), inc=None, inc_amt=1):
        if inc is not None:
            s = self.sem(inc)
            s[1] += inc_amt
            inc_h, inc_v = s[0], inc_amt
        else:
            inc_h = None
            inc_v = 0
        sem_handles = [(self.sem(n)[0], v) for (n, v) in waits if v > 0]

        def run(eng):
            for h, v in sem_handles:
                eng.wait_ge(h, v)
            ins = fn(eng)
            if inc_h is not None:
                if ins is None:
                    ins = eng.nop()
                ins.then_inc(inc_h, inc_v)
        self.sched.append((engine, run))
        return self.cnt(inc) if inc is not None else None

    def wait(self, engine, *waits):
        self.op(engine, lambda e: None, waits=waits)

    def barrier(self, engines=ENGINES):
        s = self.sem("BAR")
        for _ in engines:
            s[1] += 1
        target = s[1]
        for e in engines:
            def run(eng, h=s[0], target=target):
                eng.nop().then_inc(h, 1)
                eng.wait_ge(h, target)
            self.sched.append((e, run))

    def fconst(self, *vals):
        for val in vals:
            if val not in self._const_vals:
                self._const_vals[val] = len(self._const_vals)

    def _emit_consts(self):
        if not self._const_vals:
            return
        assert len(self._const_vals) <= 32
        tile = self._const_tile
        vals = dict(self._const_vals)

        cs = self.sem("CONST")
        cs[1] += 1

        def run(eng, h=cs[0]):
            for v, i in vals.items():
                eng.memset(tile[:, i:i + 1], float(v))
            eng.nop().then_inc(h, 1)
        self.sched.insert(0, ("vector", run))
        for e in ENGINES:
            if e == "vector":
                continue

            def wrun(eng, h=cs[0]):
                eng.wait_ge(h, 1)
            self.sched.insert(1, (e, wrun))
        for v, i in vals.items():
            self.nc.const_aps.register(F32, float(v), tile[:, i:i + 1])

    def finalize(self):
        self._emit_consts()
        per_engine = {e: [fn for (en, fn) in self.sched if en == e]
                      for e in ENGINES}
        with self.es, self.nc.Block() as block:
            @block.tensor
            def _(eng):
                for fn in per_engine["tensor"]:
                    fn(eng)

            @block.vector
            def _(eng):
                for fn in per_engine["vector"]:
                    fn(eng)

            @block.scalar
            def _(eng):
                for fn in per_engine["scalar"]:
                    fn(eng)

            @block.gpsimd
            def _(eng):
                for fn in per_engine["gpsimd"]:
                    fn(eng)

            @block.sync
            def _(eng):
                for fn in per_engine["sync"]:
                    fn(eng)
        return self.nc



def emit_ln(p, x_dram, xln, name):
    nm = name
    xf = p.sbuf(nm + "_xf", [128, NH * T], F32)
    sq = p.sbuf(nm + "_sq", [128, 2 * T], BF16)
    xb = p.sbuf(nm + "_xb", [128, 2 * T], BF16)
    w1 = p.sbuf(nm + "_w1", [128, 1], BF16)
    m_ps = p.psum(nm + "_mps", [1, T])
    v_ps = p.psum(nm + "_vps", [1, T])
    row = p.sbuf(nm + "_row", [1, 4 * T], F32)
    ones_f = p.sbuf(nm + "_onesf", [1, 128], F32)
    b_ps = p.psum(nm + "_bps", [128, T])
    r_ps = p.psum(nm + "_rps", [128, T])
    mb = p.sbuf(nm + "_mb", [128, T], F32)
    rb = p.sbuf(nm + "_rb", [128, T], BF16)
    tb = p.sbuf(nm + "_tb", [128, 2 * T], BF16)

    p.fconst(1e-5)
    p.op("vector", lambda e: e.memset(w1[:, :], 1.0 / H), waits=p.wof(w1), inc="v")
    p.op("vector", lambda e: e.memset(ones_f[:, :], 1.0), waits=p.wof(ones_f), inc="v")
    for c in range(NH):
        ld = p.op("sync", lambda e, c=c: e.dma_start(
            out=xf[:, c * T:(c + 1) * T], in_=x_dram[c * 128:(c + 1) * 128, :]),
            waits=(p.wof(xf) if c == 0 else ()), inc="ld", inc_amt=16)
        w = [("ld", ld)]
        if c >= 2:
            w.append(("t", p.cnt("t")))
        a = p.op("scalar", lambda e, c=c: e.activation(
            sq[:, (c % 2) * T:(c % 2 + 1) * T], xf[:, c * T:(c + 1) * T],
            AF.Square), waits=w + (p.wof(sq) if c == 0 else []), inc="a")
        v = p.op("vector", lambda e, c=c: e.tensor_copy(
            xb[:, (c % 2) * T:(c % 2 + 1) * T], xf[:, c * T:(c + 1) * T]),
            waits=w + (p.wof(xb) if c == 0 else []), inc="v")
        for half in range(2):
            p.op("tensor", lambda e, c=c, half=half: e.matmul(
                m_ps[0:1, half * 512:(half + 1) * 512], w1[:, 0:1],
                xb[:, (c % 2) * T + half * 512:(c % 2) * T + (half + 1) * 512],
                start=(c == 0), stop=(c == NH - 1)),
                waits=([("v", v)] + (p.wof(m_ps) if c == 0 else [])
                       if half == 0 else ()))
            p.op("tensor", lambda e, c=c, half=half: e.matmul(
                v_ps[0:1, half * 512:(half + 1) * 512], w1[:, 0:1],
                sq[:, (c % 2) * T + half * 512:(c % 2) * T + (half + 1) * 512],
                start=(c == 0), stop=(c == NH - 1)),
                waits=([("a", a)] + (p.wof(v_ps) if c == 0 else [])
                       if half == 0 else ()),
                inc=("t" if half == 1 else None))
    pe_stats = p.cnt("t")
    v1 = p.op("vector", lambda e: e.tensor_copy(row[0:1, 0:T], m_ps[0:1, :]),
              waits=[("t", pe_stats)] + p.wof(row), inc="v")
    v2 = p.op("vector", lambda e: e.tensor_tensor(
        row[0:1, T:2 * T], row[0:1, 0:T], row[0:1, 0:T], ALU.mult),
        waits=[("v", v1)], inc="v")
    v3 = p.op("vector", lambda e: e.tensor_tensor(
        row[0:1, 2 * T:3 * T], v_ps[0:1, :], row[0:1, T:2 * T], ALU.subtract),
        waits=[("v", v2)], inc="v")
    a1 = p.op("scalar", lambda e: e.activation(
        row[0:1, T:2 * T], row[0:1, 2 * T:3 * T], AF.Ln, bias=1e-5),
        waits=[("v", v3)], inc="a")
    a2 = p.op("scalar", lambda e: e.activation(
        row[0:1, 3 * T:4 * T], row[0:1, T:2 * T], AF.Exp, scale=-0.5),
        waits=[("a", a1)], inc="a")
    t1 = None
    for half in range(2):
        p.op("tensor", lambda e, half=half: e.matmul(
            b_ps[:, half * 512:(half + 1) * 512], ones_f[0:1, :],
            row[0:1, half * 512:(half + 1) * 512], start=True, stop=True),
            waits=([("v", v1)] + p.wof(b_ps) if half == 0 else ()))
        t1 = p.op("tensor", lambda e, half=half: e.matmul(
            r_ps[:, half * 512:(half + 1) * 512], ones_f[0:1, :],
            row[0:1, 3 * T + half * 512:3 * T + (half + 1) * 512],
            start=True, stop=True),
            waits=([("a", a2)] + p.wof(r_ps) if half == 0 else ()),
            inc=("t" if half == 1 else None))
    v4 = p.op("vector", lambda e: e.tensor_copy(mb[:, :], b_ps[:, :]),
              waits=[("t", t1)] + p.wof(mb), inc="v")
    v5 = p.op("vector", lambda e: e.tensor_copy(rb[:, :], r_ps[:, :]),
              waits=[("v", v4)] + p.wof(rb), inc="v")
    for c in range(NH):
        vv = p.op("vector", lambda e, c=c: e.tensor_tensor(
            tb[:, (c % 2) * T:(c % 2 + 1) * T], xf[:, c * T:(c + 1) * T],
            mb[:, :], ALU.subtract),
            waits=([("v", v5)] + p.wof(tb, xln) if c == 0 else ()), inc="v")
        p.op("vector", lambda e, c=c: e.tensor_tensor(
            xln[:, c * T:(c + 1) * T], tb[:, (c % 2) * T:(c % 2 + 1) * T],
            rb[:, :], ALU.mult), waits=[("v", vv)], inc="v")
    done = p.cnt("v")
    p.free(xf, sq, xb, w1, row, ones_f, mb, rb, tb, m_ps, v_ps, b_ps, r_ps)
    return done


def emit_colsum_rows(p, arena, ncols, name, ready_v):
    nm = name
    ones1 = p.sbuf(nm + "_o1", [128, 1], BF16)
    ps = p.psum(nm + "_ps", [1, T])
    hi = p.sbuf(nm + "_hi", [1, T], BF16)
    lo = p.sbuf(nm + "_lo", [1, T], BF16)
    hf = p.sbuf(nm + "_hf", [1, T], F32)
    hilo = p.sbuf(nm + "_hl", [2, T], BF16)
    p.op("vector", lambda e: e.memset(ones1[:, :], 1.0), waits=p.wof(ones1),
         inc="v")
    mv = p.cnt("v")
    t = None
    for c in range(ncols):
        for half in range(2):
            t = p.op("tensor", lambda e, c=c, half=half: e.matmul(
                ps[0:1, half * 512:(half + 1) * 512], ones1[:, 0:1],
                arena[:, c * T + half * 512:c * T + (half + 1) * 512],
                start=(c == 0), stop=(c == ncols - 1)),
                waits=([("v", max(ready_v, mv))] + p.wof(ps)
                       if (c == 0 and half == 0) else ()),
                inc=("t" if (c == ncols - 1 and half == 1) else None))
    v1 = p.op("vector", lambda e: e.tensor_copy(hi[0:1, :], ps[0:1, :]),
              waits=[("t", t)] + p.wof(hi, hf, lo, hilo), inc="v")
    v2 = p.op("vector", lambda e: e.tensor_copy(hf[0:1, :], hi[0:1, :]),
              waits=[("v", v1)], inc="v")
    v3 = p.op("vector", lambda e: e.tensor_tensor(
        lo[0:1, :], ps[0:1, :], hf[0:1, :], ALU.subtract),
        waits=[("v", v2)], inc="v")
    p.op("sync", lambda e: e.dma_start(out=hilo[0:1, :], in_=hi[0:1, :]),
         waits=[("v", v1)], inc="ld", inc_amt=16)
    l2 = p.op("sync", lambda e: e.dma_start(out=hilo[1:2, :], in_=lo[0:1, :]),
              waits=[("v", v3)], inc="ld", inc_amt=16)
    p.free(ps, hf, ones1)
    return hilo, [("v", v3), ("ld", l2)], (hi, lo, hilo)


def emit_ffn(p, x_dram, xout_dram, wg_dram, wu_dram, wd_dram,
             sg_dram, su_dram, sd_dram, NI, name):
    nm = name
    NBUF = 3
    sg = p.sbuf(nm + "_sg", [128, NI], F32)
    su = p.sbuf(nm + "_su", [128, NI], F32)
    sd = p.sbuf(nm + "_sd", [128, NH], F32)
    sdh = p.sbuf(nm + "_sdh", [128, NH], F32)
    negk = p.sbuf(nm + "_negk", [2, 128], BF16)
    gbuf = p.sbuf(nm + "_gw", [128, NBUF * NH * 128], BF16)
    ubuf = p.sbuf(nm + "_uw", [128, NBUF * NH * 128], BF16)
    p.op("sync", lambda e: e.dma_start(out=sg[:, :], in_=sg_dram.ap()),
         waits=p.wof(sg, su, sd), inc="ld", inc_amt=16)
    p.op("sync", lambda e: e.dma_start(out=su[:, :], in_=su_dram.ap()),
         inc="ld", inc_amt=16)
    ldsc = p.op("sync", lambda e: e.dma_start(out=sd[:, :], in_=sd_dram.ap()),
                inc="ld", inc_amt=16)
    p.op("vector", lambda e: e.memset(negk[:, :], -128.0),
         waits=p.wof(negk, sdh), inc="v")
    p.op("vector", lambda e: e.tensor_scalar(
        sdh[:, :], sd[:, :], 0.5, None, ALU.mult),
        waits=[("ld", ldsc)], inc="v")
    prep_v = p.cnt("v")

    xln = p.sbuf(nm + "_xln", [128, NH * T], BF16)
    lnv = emit_ln(p, x_dram, xln, nm + "_ln")
    xhl, xs_waits, rows_free = emit_colsum_rows(p, xln, NH, nm + "_xs", lnv)

    harena = p.sbuf(nm + "_h", [128, NI * T], BF16)
    g_ps = [p.psum(nm + "_gps%d" % i, [128, T]) for i in range(2)]
    u_ps = [p.psum(nm + "_ups%d" % i, [128, T]) for i in range(2)]
    sgb = p.sbuf(nm + "_sgb", [128, 2 * T], BF16)
    gend = {}
    aend = {}
    hend = {}
    wend = {}
    for ic in range(NI):
        par = ic % 2
        wpar = ic % NBUF
        dw = []
        if ic == 0:
            dw = p.wof(gbuf, ubuf)
        if ic >= NBUF:
            dw = [("t", gend[ic - NBUF])]
        wv = p.op("gpsimd", lambda e, ic=ic, wpar=wpar: e.dma_start(
            out=gbuf[:, wpar * NH * 128:(wpar + 1) * NH * 128],
            in_=wg_dram[ic]), waits=dw, inc="w", inc_amt=16)
        wv = p.op("gpsimd", lambda e, ic=ic, wpar=wpar: e.dma_start(
            out=ubuf[:, wpar * NH * 128:(wpar + 1) * NH * 128],
            in_=wu_dram[ic]), inc="w", inc_amt=16)
        wend[ic] = wv
        tw = [("w", wend[ic]), ("v", prep_v)] + xs_waits
        if ic == 0:
            tw += p.wof(*g_ps, *u_ps)
        if ic >= 2:
            tw.append(("a", aend[ic - 2]))
            tw.append(("v", hend[ic - 2]))
        for hc in range(NH):
            for half in range(2):
                p.op("tensor", lambda e, wpar=wpar, par=par, hc=hc, half=half: e.matmul(
                    g_ps[par][:, half * 512:(half + 1) * 512],
                    gbuf[:, wpar * NH * 128 + hc * 128:wpar * NH * 128 + (hc + 1) * 128],
                    xln[:, hc * T + half * 512:hc * T + (half + 1) * 512],
                    start=(hc == 0), stop=False),
                    waits=(tw if (hc == 0 and half == 0) else ()))
        for half in range(2):
            p.op("tensor", lambda e, par=par, half=half: e.matmul(
                g_ps[par][:, half * 512:(half + 1) * 512], negk[0:2, :],
                xhl[0:2, half * 512:(half + 1) * 512], start=False, stop=True),
                inc=("t" if half == 1 else None))
        tgd = p.cnt("t")
        for hc in range(NH):
            for half in range(2):
                p.op("tensor", lambda e, wpar=wpar, par=par, hc=hc, half=half: e.matmul(
                    u_ps[par][:, half * 512:(half + 1) * 512],
                    ubuf[:, wpar * NH * 128 + hc * 128:wpar * NH * 128 + (hc + 1) * 128],
                    xln[:, hc * T + half * 512:hc * T + (half + 1) * 512],
                    start=(hc == 0), stop=False))
        for half in range(2):
            p.op("tensor", lambda e, par=par, half=half: e.matmul(
                u_ps[par][:, half * 512:(half + 1) * 512], negk[0:2, :],
                xhl[0:2, half * 512:(half + 1) * 512], start=False, stop=True),
                inc=("t" if half == 1 else None))
        tud = p.cnt("t")
        gend[ic] = tud
        aa = p.op("scalar", lambda e, ic=ic, par=par: e.activation(
            sgb[:, par * T:(par + 1) * T], g_ps[par][:, :], AF.Silu,
            scale=sg[:, ic:ic + 1]),
            waits=[("t", tgd), ("v", prep_v)] + (p.wof(sgb) if ic == 0 else []),
            inc="a")
        aend[ic] = aa
        hend[ic] = p.op("vector", lambda e, ic=ic, par=par: e.scalar_tensor_tensor(
            harena[:, ic * T:(ic + 1) * T], u_ps[par][:, :], su[:, ic:ic + 1],
            sgb[:, par * T:(par + 1) * T], ALU.mult, ALU.mult),
            waits=[("t", tud), ("a", aa)] + (p.wof(harena) if ic == 0 else []),
            inc="v")
    h_done_v = p.cnt("v")
    p.free(gbuf, ubuf, sgb, *g_ps, *u_ps)
    p.free(xln)

    hhl, hs_waits, hrows_free = emit_colsum_rows(
        p, harena, NI, nm + "_hs", h_done_v)

    NBD = 2
    dbuf = p.sbuf(nm + "_dw", [128, NBD * NI * 128], BF16)
    d_ps = [p.psum(nm + "_dps%d" % i, [128, T]) for i in range(2)]
    xres = p.sbuf(nm + "_xr", [128, 2 * T], F32)
    xo = p.sbuf(nm + "_xo", [128, 2 * T], F32)
    st_base = p.cnt("st")
    dend = {}
    vend = {}
    for hc in range(NH):
        par = hc % 2
        wpar = hc % NBD
        ww = []
        if hc == 0:
            ww = [("v", h_done_v)] + p.wof(dbuf)
        if hc >= NBD:
            ww = [("t", dend[hc - NBD])]
        wv = p.op("gpsimd", lambda e, hc=hc, wpar=wpar: e.dma_start(
            out=dbuf[:, wpar * NI * 128:(wpar + 1) * NI * 128],
            in_=wd_dram[hc]), waits=ww, inc="w", inc_amt=16)
        lr = p.op("sync", lambda e, hc=hc, par=par: e.dma_start(
            out=xres[:, par * T:(par + 1) * T],
            in_=x_dram[hc * 128:(hc + 1) * 128, :]),
            waits=(p.wof(xres) if hc == 0 else
                   ([("v", vend[hc - 2])] if hc >= 2 else [])),
            inc="ld", inc_amt=16)
        tw = [("w", wv)] + hs_waits
        if hc == 0:
            tw += p.wof(*d_ps)
        if hc >= 2:
            tw.append(("v", vend[hc - 2]))
        for ic in range(NI):
            for half in range(2):
                p.op("tensor", lambda e, wpar=wpar, par=par, ic=ic, half=half: e.matmul(
                    d_ps[par][:, half * 512:(half + 1) * 512],
                    dbuf[:, wpar * NI * 128 + ic * 128:wpar * NI * 128 + (ic + 1) * 128],
                    harena[:, ic * T + half * 512:ic * T + (half + 1) * 512],
                    start=(ic == 0), stop=False),
                    waits=(tw if (ic == 0 and half == 0) else ()))
        for half in range(2):
            p.op("tensor", lambda e, par=par, half=half: e.matmul(
                d_ps[par][:, half * 512:(half + 1) * 512], negk[0:2, :],
                hhl[0:2, half * 512:(half + 1) * 512], start=False, stop=True),
                inc=("t" if half == 1 else None))
        tdd = p.cnt("t")
        dend[hc] = tdd
        vw = [("t", tdd), ("ld", lr)]
        if hc == 0:
            vw += p.wof(xo)
        if hc >= 2:
            vw.append(("st", st_base + 16 * (hc - 1)))
        vo = p.op("vector", lambda e, hc=hc, par=par: e.scalar_tensor_tensor(
            xo[:, par * T:(par + 1) * T], d_ps[par][:, :], sdh[:, hc:hc + 1],
            xres[:, par * T:(par + 1) * T], ALU.mult, ALU.add),
            waits=vw, inc="v")
        vend[hc] = vo
        p.op("sync", lambda e, hc=hc, par=par: e.dma_start(
            out=xout_dram[hc * 128:(hc + 1) * 128, :],
            in_=xo[:, par * T:(par + 1) * T]), waits=[("v", vo)],
            inc="st", inc_amt=16)
    st_done = p.cnt("st")
    p.free(harena, dbuf, xres, xo, sg, su, sd, sdh, negk,
           *d_ps, *rows_free, *hrows_free)
    return st_done


def emit_proj(p, xln, w_dram, out_dram, n_out, mode, name, xln_ready_v,
              cast=True, stq="sync", wsb_pre=None):
    nm = name
    NO = n_out // 128
    if not callable(out_dram):
        dram_h = out_dram
        out_dram = lambda oc: dram_h[oc * 128:(oc + 1) * 128, :]
    wsb = wsb_pre if wsb_pre is not None else p.sbuf(
        nm + "_w", [128, NH * n_out], BF16)
    ps = [p.psum(nm + "_ps%d" % i, [128, 512]) for i in range(2)]
    stage = p.sbuf(nm + "_st", [128, 2 * max(n_out, T)], BF16)
    wv = None
    wq = "w" if cast else "ld"
    weng = "gpsimd" if cast else "sync"
    for hc in range(NH):
        wv = p.op(weng, lambda e, hc=hc: e.dma_start(
            out=wsb[:, hc * n_out:(hc + 1) * n_out], in_=w_dram[hc]),
            waits=(p.wof(wsb) if hc == 0 else ()), inc=wq, inc_amt=16)
    st_base = p.cnt("st")
    k = 0
    outer = NT if mode == "tm" else NO
    inner = n_out // 512 if mode == "tm" else T // 512
    for oc in range(outer):
        par = oc % 2
        for it in range(inner):
            pp = ps[k % 2]
            tw = [(wq, wv), ("v", xln_ready_v)]
            if k == 0:
                tw += p.wof(*ps)
            if k >= 2:
                tw.append(("v", p.cnt("v")))
            tt = None
            for hc in range(NH):
                if mode == "tm":
                    fn = (lambda e, oc=oc, it=it, hc=hc, pp=pp: e.matmul(
                        pp[:, :],
                        xln[:, hc * T + oc * 128:hc * T + (oc + 1) * 128],
                        wsb[:, hc * n_out + it * 512:hc * n_out + (it + 1) * 512],
                        start=(hc == 0), stop=(hc == NH - 1)))
                else:
                    fn = (lambda e, oc=oc, it=it, hc=hc, pp=pp: e.matmul(
                        pp[:, :],
                        wsb[:, hc * n_out + oc * 128:hc * n_out + (oc + 1) * 128],
                        xln[:, hc * T + it * 512:hc * T + (it + 1) * 512],
                        start=(hc == 0), stop=(hc == NH - 1)))
                tt = p.op("tensor", fn, waits=(tw if hc == 0 else ()),
                          inc=("t" if hc == NH - 1 else None))
            width = n_out if mode == "tm" else T
            cw = [("t", tt)]
            if k <= 1:
                cw += p.wof(stage)
            if oc >= 2 and it == 0:
                cw.append(("st", st_base + 16 * (oc - 1)))
            p.op("vector", lambda e, it=it, par=par, pp=pp, width=width: e.tensor_copy(
                stage[:, par * width + it * 512:par * width + (it + 1) * 512],
                pp[:, :]), waits=cw, inc="v")
            k += 1
        sv = p.cnt("v")
        p.op(stq, lambda e, oc=oc, par=par, width=width: e.dma_start(
            out=out_dram(oc),
            in_=stage[:, par * width:(par + 1) * width]),
            waits=[("v", sv)], inc="st", inc_amt=16)
    if wsb_pre is None:
        p.free(wsb)
    p.free(stage, *ps)
    return p.cnt("st")



def _bcast_mid(ap, count, pos=1):
    import dataclasses
    dims = list(ap.ap)
    dims.insert(pos, [0, count])
    return dataclasses.replace(ap, ap=type(ap.ap)(dims))


def emit_attn0(p, x_dram, xout_dram, qf_dram, kf_dram, v_dram, wo_dram,
               ident_dram, name, gate=()):
    nm = name
    et_dram = p.dram(nm + "_et", [16, 128, H], BF16)
    ksb = p.sbuf(nm + "_k", [128, 16 * H], BF16)
    qsb = p.sbuf(nm + "_q", [128, 16 * H], BF16)
    rz = p.sbuf(nm + "_rz", [128, 16], F32)
    mx = p.sbuf(nm + "_mx", [128, 16], F32)
    zp = p.sbuf(nm + "_zp", [128, 16], F32)
    esb = p.sbuf(nm + "_e", [128, 2 * H], BF16)
    etst = p.sbuf(nm + "_ets", [128, 2 * H], BF16)
    ident = p.sbuf(nm + "_id", [128, 128], BF16)
    t_ps = [p.psum(nm + "_tps%d" % i, [128, 128], BF16) for i in range(2)]
    kl = ql = None
    for c in range(16):
        hh, r = c // 8, c % 8
        kl = p.op("sync", lambda e, c=c, hh=hh, r=r: e.dma_start(
            out=ksb[:, c * H:(c + 1) * H],
            in_=kf_dram.ap()[hh, r * 128:(r + 1) * 128, :]),
            waits=(list(gate) + p.wof(ksb, qsb) if c == 0 else ()),
            inc="ld", inc_amt=16)
        ql = p.op("sync", lambda e, c=c, hh=hh, r=r: e.dma_start(
            out=qsb[:, c * H:(c + 1) * H],
            in_=qf_dram.ap()[hh, r * 128:(r + 1) * 128, :]),
            inc="ld", inc_amt=16)
    mvv = p.op("vector", lambda e: None,
               waits=p.wof(ident, rz, mx, zp, esb, etst), inc="v")
    idl = p.op("sync", lambda e: e.dma_start(out=ident[:, :], in_=ident_dram.ap()),
               waits=[("v", mvv)], inc="ld", inc_amt=16)
    ssb = p.sbuf(nm + "_ssb", [128, 2 * H], F32)
    s_ps2 = [p.psum(nm + "_sps%d" % i, [128, 512]) for i in range(2)]
    kk = 0
    scp = {}
    expa = {}
    tend = {}
    for et in range(16):
        pb = (et % 2) * H
        mb_ = (et % 2) * 8
        for ft in range(4):
            pp = s_ps2[kk % 2]
            tw = [("ld", max(ql, idl))]
            if kk == 0:
                tw += p.wof(*s_ps2) + p.wof(ssb)
            if kk >= 2:
                et2, ft2 = divmod(kk - 2, 4)
                tw.append(("v", scp[(et2, ft2)]))
            if ft == 0 and et >= 2:
                tw.append(("a", expa[et - 2]))
            for sc in range(16):
                p.op("tensor", lambda e, et=et, ft=ft, sc=sc, pp=pp: e.matmul(
                    pp[:, :],
                    qsb[:, sc * H + et * 128:sc * H + (et + 1) * 128],
                    ksb[:, sc * H + ft * 512:sc * H + (ft + 1) * 512],
                    start=(sc == 0), stop=(sc == 15)),
                    waits=(tw if sc == 0 else ()),
                    inc=("t" if sc == 15 else None))
            ts = p.cnt("t")
            scp[(et, ft)] = p.op("vector", lambda e, ft=ft, pp=pp, pb=pb: e.tensor_copy(
                ssb[:, pb + ft * 512:pb + (ft + 1) * 512], pp[:, :]),
                waits=[("t", ts)], inc="v")
            kk += 1
        rw = [("v", scp[(et, 3)])]
        if et >= 1:
            rw.append(("a", expa[et - 1]))
        if et >= 2:
            rw.append(("a", expa[et - 2]))
        for ft in range(4):
            p.op("vector", lambda e, ft=ft, pb=pb, mb_=mb_: e.tensor_reduce(
                mx[:, mb_ + ft:mb_ + ft + 1], ssb[:, pb + ft * 512:pb + (ft + 1) * 512],
                mybir.AxisListType.X, ALU.max),
                waits=(rw if ft == 0 else ()), inc="v")
        vv = p.op("vector", lambda e, mb_=mb_: e.tensor_reduce(
            mx[:, mb_ + 4:mb_ + 5], mx[:, mb_:mb_ + 4], mybir.AxisListType.X,
            ALU.max), waits=[("v", p.cnt("v"))], inc="v")
        vv = p.op("vector", lambda e, mb_=mb_: e.tensor_scalar(
            mx[:, mb_ + 5:mb_ + 6], mx[:, mb_ + 4:mb_ + 5], -0.125, None,
            ALU.mult), waits=[("v", vv)], inc="v")
        aa = None
        for ft in range(4):
            aa = p.op("scalar", lambda e, ft=ft, pb=pb, mb_=mb_: e.activation(
                esb[:, pb + ft * 512:pb + (ft + 1) * 512],
                ssb[:, pb + ft * 512:pb + (ft + 1) * 512],
                AF.Exp, bias=mx[:, mb_ + 5:mb_ + 6], scale=0.125,
                accum_out=zp[:, mb_ + ft:mb_ + ft + 1]),
                waits=([("v", vv)] + ([("t", tend[et - 2])] if et >= 2 else [])
                       if ft == 0 else ()), inc="a")
        expa[et] = aa
        vz = p.op("vector", lambda e, mb_=mb_: e.tensor_reduce(
            zp[:, mb_ + 4:mb_ + 5], zp[:, mb_:mb_ + 4], mybir.AxisListType.X,
            ALU.add), waits=[("a", aa)], inc="v")
        vz = p.op("vector", lambda e, et=et, mb_=mb_: e.reciprocal(
            rz[:, et:et + 1], zp[:, mb_ + 4:mb_ + 5]),
            waits=[("v", vz)], inc="v")
        par = et % 2
        for fc in range(16):
            tt = p.op("tensor", lambda e, fc=fc, pb=pb: e.transpose(
                t_ps[fc % 2][:, :],
                esb[:, pb + fc * 128:pb + (fc + 1) * 128], ident[:, :]),
                waits=([("a", aa)] +
                       (p.wof(*t_ps) if et == 0 and fc == 0 else []) +
                       ([("v", p.cnt("v"))] if (fc >= 2 or et >= 1) else [])),
                inc="t")
            p.op("vector", lambda e, fc=fc, par=par: e.tensor_copy(
                etst[:, par * H + fc * 128:par * H + (fc + 1) * 128],
                t_ps[fc % 2][:, :]),
                waits=[("t", p.cnt("t"))] +
                      ([("st", p.cnt("st"))] if fc == 0 and et >= 2 else []),
                inc="v")
        tend[et] = p.cnt("t")
        sv = p.cnt("v")
        p.op("sync", lambda e, et=et, par=par: e.dma_start(
            out=et_dram[et], in_=etst[:, par * H:(par + 1) * H]),
            waits=[("v", sv)], inc="st", inc_amt=16)
    phaseb_st = p.cnt("st")
    p.free(ksb, qsb, esb, ssb, *s_ps2, *t_ps)

    vsb = p.sbuf(nm + "_v", [128, 16 * T], BF16)
    ao = p.sbuf(nm + "_ao", [128, 16 * T], BF16)
    etb = p.sbuf(nm + "_etb", [128, 2 * H], BF16)
    a_ps = [p.psum(nm + "_aps%d" % i, [128, 512]) for i in range(2)]
    vl = None
    for c in range(16):
        vl = p.op("sync", lambda e, c=c: e.dma_start(
            out=vsb[:, c * T:(c + 1) * T], in_=v_dram[c * 128:(c + 1) * 128, :]),
            waits=(p.wof(vsb, ao, etb) if c == 0 else ()), inc="ld", inc_amt=16)
    k = 0
    for et in range(16):
        par = et % 2
        el = p.op("sync", lambda e, et=et, par=par: e.dma_start(
            out=etb[:, par * H:(par + 1) * H], in_=et_dram[et]),
            waits=([("st", phaseb_st)] if et == 0 else
                   ([("t", p.cnt("t"))] if et >= 2 else [])),
            inc="ld", inc_amt=16)
        for sh in range(2):
            pp = a_ps[k % 2]
            tw = [("ld", max(el, vl))]
            if k == 0:
                tw += p.wof(*a_ps)
            if k >= 2:
                tw.append(("v", p.cnt("v")))
            tt = None
            for fc in range(16):
                tt = p.op("tensor", lambda e, et=et, par=par, sh=sh, fc=fc, pp=pp: e.matmul(
                    pp[:, :], etb[:, par * H + fc * 128:par * H + (fc + 1) * 128],
                    vsb[:, fc * T + sh * 512:fc * T + (sh + 1) * 512],
                    start=(fc == 0), stop=(fc == 15)),
                    waits=(tw if fc == 0 else ()),
                    inc=("t" if fc == 15 else None))
            p.op("vector", lambda e, et=et, sh=sh, pp=pp: e.tensor_scalar(
                ao[:, et * T + sh * 512:et * T + (sh + 1) * 512], pp[:, :],
                rz[:, et:et + 1], None, ALU.mult),
                waits=[("t", tt)], inc="v")
            k += 1
    ao_v = p.cnt("v")
    p.free(etb, *a_ps)

    wsb = p.sbuf(nm + "_wo", [128, NH * H], BF16)
    w_ps = [p.psum(nm + "_wps%d" % i, [128, T]) for i in range(2)]
    xres = p.sbuf(nm + "_xr", [128, 2 * T], F32)
    xo = p.sbuf(nm + "_xo", [128, 2 * T], F32)
    wv = None
    for hc in range(NH):
        wv = p.op("gpsimd", lambda e, hc=hc: e.dma_start(
            out=wsb[:, hc * H:(hc + 1) * H], in_=wo_dram[hc]),
            waits=(p.wof(wsb) if hc == 0 else ()), inc="w", inc_amt=16)
    st_base = p.cnt("st")
    for oc in range(NH):
        par = oc % 2
        lr = p.op("sync", lambda e, oc=oc, par=par: e.dma_start(
            out=xres[:, par * T:(par + 1) * T],
            in_=x_dram[oc * 128:(oc + 1) * 128, :]),
            waits=(p.wof(xres, xo) if oc == 0 else
                   ([("v", p.cnt("v"))] if oc >= 2 else [])),
            inc="ld", inc_amt=16)
        tw = [("w", wv), ("v", ao_v)]
        if oc == 0:
            tw += p.wof(*w_ps)
        if oc >= 2:
            tw.append(("v", p.cnt("v")))
        for ec in range(16):
            for half in range(2):
                p.op("tensor", lambda e, oc=oc, par=par, ec=ec, half=half: e.matmul(
                    w_ps[par][:, half * 512:(half + 1) * 512],
                    wsb[:, ec * H + oc * 128:ec * H + (oc + 1) * 128],
                    ao[:, ec * T + half * 512:ec * T + (half + 1) * 512],
                    start=(ec == 0), stop=(ec == 15)),
                    waits=(tw if (ec == 0 and half == 0) else ()),
                    inc=("t" if (ec == 15 and half == 1) else None))
        tdd = p.cnt("t")
        vw = [("t", tdd), ("ld", lr)]
        if oc >= 2:
            vw.append(("st", st_base + 16 * (oc - 1)))
        vo = p.op("vector", lambda e, oc=oc, par=par: e.tensor_tensor(
            xo[:, par * T:(par + 1) * T], w_ps[par][:, :],
            xres[:, par * T:(par + 1) * T], ALU.add),
            waits=vw, inc="v")
        p.op("sync", lambda e, oc=oc, par=par: e.dma_start(
            out=xout_dram[oc * 128:(oc + 1) * 128, :],
            in_=xo[:, par * T:(par + 1) * T]), waits=[("v", vo)],
            inc="st", inc_amt=16)
    st_done = p.cnt("st")
    p.free(vsb, ao, wsb, xres, xo, rz, mx, zp, ident, etst, *w_ps)
    return st_done


def emit_proj_g(p, xln, wg_dram, lgl_dram, identf_dram, name, xln_ready_v):
    nm = name
    wsb = p.sbuf(nm + "_w", [128, NH * 8], BF16)
    g_ps = p.psum(nm + "_ps", [8, T])
    sc1 = p.sbuf(nm + "_s1", [8, T], F32)
    sc2 = p.sbuf(nm + "_s2", [8, T], F32)
    sc3 = p.sbuf(nm + "_s3", [8, T], F32)
    mt = p.sbuf(nm + "_mt", [128, T], F32)
    idf = p.sbuf(nm + "_idf", [128, 128], F32)
    lgt = p.sbuf(nm + "_lgt", [128, 8 * 8], F32)
    t_ps = p.psum(nm + "_tps", [128, 128])
    p.fconst(GATE_OFFSET, 1.0)
    mz = p.op("vector", lambda e: e.memset(mt[:, :], 0.0),
              waits=p.wof(mt, idf, lgt, t_ps), inc="v")
    idl = p.op("sync", lambda e: e.dma_start(
        out=idf[:, :], in_=identf_dram.ap()), inc="ld", inc_amt=16)
    wv = None
    for hc in range(NH):
        wv = p.op("gpsimd", lambda e, hc=hc: e.dma_start(
            out=wsb[:, hc * 8:(hc + 1) * 8], in_=wg_dram[hc]),
            waits=(p.wof(wsb) if hc == 0 else ()), inc="w", inc_amt=16)
    tt = None
    for hc in range(NH):
        for half in range(2):
            tt = p.op("tensor", lambda e, hc=hc, half=half: e.matmul(
                g_ps[0:8, half * 512:(half + 1) * 512],
                wsb[:, hc * 8:(hc + 1) * 8],
                xln[:, hc * T + half * 512:hc * T + (half + 1) * 512],
                start=(hc == 0), stop=(hc == NH - 1)),
                waits=([("w", wv), ("v", xln_ready_v)] + p.wof(g_ps)
                       if (hc == 0 and half == 0) else ()),
                inc=("t" if (hc == NH - 1 and half == 1) else None))
    a1 = p.op("scalar", lambda e: e.activation(
        sc1[:, :], g_ps[0:8, :], AF.Abs, bias=GATE_OFFSET),
        waits=[("t", tt)] + p.wof(sc1, sc2, sc3), inc="a")
    a2 = p.op("scalar", lambda e: e.activation(
        sc2[:, :], sc1[:, :], AF.Exp, scale=-1.0), waits=[("a", a1)], inc="a")
    a3 = p.op("scalar", lambda e: e.activation(
        sc1[:, :], sc2[:, :], AF.Ln, bias=1.0), waits=[("a", a2)], inc="a")
    v1 = p.op("vector", lambda e: e.tensor_scalar(
        sc3[:, :], g_ps[0:8, :], GATE_OFFSET, 0.0, ALU.add, ALU.min),
        waits=[("t", tt)], inc="v")
    v2 = p.op("vector", lambda e: e.tensor_tensor(
        mt[0:8, :], sc3[0:8, :], sc1[0:8, :], ALU.subtract),
        waits=[("v", v1), ("a", a3), ("v", mz)], inc="v")
    for ch in range(NT):
        tt2 = p.op("tensor", lambda e, ch=ch: e.transpose(
            t_ps[:, :], mt[:, ch * 128:(ch + 1) * 128], idf[:, :]),
            waits=([("v", v2), ("ld", idl)] + p.wof(t_ps) if ch == 0 else
                   [("v", p.cnt("v"))]), inc="t")
        p.op("vector", lambda e, ch=ch: e.tensor_copy(
            lgt[:, ch * 8:(ch + 1) * 8], t_ps[:, 0:8]),
            waits=[("t", tt2)], inc="v")
    lv = p.cnt("v")
    for ch in range(NT):
        p.op("sync", lambda e, ch=ch: e.dma_start(
            out=lgl_dram[ch * 128:(ch + 1) * 128, :],
            in_=lgt[:, ch * 8:(ch + 1) * 8]),
            waits=([("v", lv)] if ch == 0 else ()), inc="st", inc_amt=16)
    st = p.cnt("st")
    p.free(wsb, g_ps, sc1, sc2, sc3, mt, idf, lgt, t_ps)
    return st


def _yoff(k):
    return (k // 3) * 512 + (k % 3) * 129


def emit_attn1(p, x_dram, xout_dram, q_dram, sh_k1, sh_v1, sh_lg, lgl_dram,
               psel_dram, csel_dram, vb_dram, masks_dram,
               tri_dram, ident_dram, wc_dram, bc_dram, name, gate=()):
    nm = name
    NKV, NQ, D = 8, 16, 128
    ksb = p.sbuf(nm + "_k", [128, NKV * 2 * T], BF16)
    vsb = p.sbuf(nm + "_v", [128, 16 * NKV * 129], BF16)
    qsb = p.sbuf(nm + "_q", [128, NKV * NT * 2 * D], BF16)
    lgf = p.sbuf(nm + "_lgf", [128, 16 * 8], F32)
    lgo = p.sbuf(nm + "_lgo", [128, 8 * 8], F32)
    tri = p.sbuf(nm + "_tri", [128, 128], F32)
    onesf = p.sbuf(nm + "_1f", [128, 128], F32)
    psel = p.sbuf(nm + "_psel", [128, 16 * 8], F32)
    csel = p.sbuf(nm + "_csel", [128, 16], F32)
    vb = p.sbuf(nm + "_vb", [128, 8 * 16], F32)
    msk = p.sbuf(nm + "_msk", [128, 8 * 2 * 128], BF16)
    ident = p.sbuf(nm + "_id", [128, 128], BF16)
    cumg = p.sbuf(nm + "_cum", [128, 16 * 8], F32)
    ocum = p.sbuf(nm + "_ocum", [128, 8 * 8], F32)
    mrow = p.sbuf(nm + "_mrow", [1, 64], F32)
    crow = p.sbuf(nm + "_crow", [1, 8], F32)
    msb = p.sbuf(nm + "_msb", [8, 8], F32)
    mgrid = p.sbuf(nm + "_mg", [128, 64], F32)
    esf = p.sbuf(nm + "_esf", [128, 8 * 128], F32)
    ugrid = p.sbuf(nm + "_ug", [128, 64], F32)
    dsc = p.sbuf(nm + "_dsc", [128, 128], F32)
    ld0 = p.cnt("ld")
    first = p.wof(ksb, vsb, qsb, lgf, lgo, tri, onesf, psel, csel, vb, msk,
                  ident, cumg, ocum, mrow, crow, msb, mgrid, esf, ugrid, dsc)
    l = p.op("sync", lambda e: e.dma_start(out=tri[:, :], in_=tri_dram.ap()),
             waits=list(gate) + first, inc="ld", inc_amt=16)
    l = p.op("sync", lambda e: e.dma_start(out=ident[:, :], in_=ident_dram.ap()),
             inc="ld", inc_amt=16)
    vones = p.op("vector", lambda e: e.memset(
        vsb[:, :].rearrange("p (m d) -> p m d", d=129)[:, :, 128:129], 1.0),
        waits=list(gate) + p.wof(vsb), inc="v")
    for c in range(16):
        hh, r = c // 8, c % 8
        l = p.op("sync", lambda e, c=c, hh=hh, r=r: e.dma_start(
            out=lgf[:, c * 8:(c + 1) * 8],
            in_=sh_lg.ap()[hh, r * 128:(r + 1) * 128, :]),
            inc="ld", inc_amt=16)
        l = p.op("sync", lambda e, c=c: e.dma_start(
            out=psel[:, c * 8:(c + 1) * 8], in_=psel_dram[c]), inc="ld", inc_amt=16)
        l = p.op("sync", lambda e, c=c: e.dma_start(
            out=csel[:, c:c + 1], in_=csel_dram[c]), inc="ld", inc_amt=16)
        l = p.op("sync", lambda e, c=c, hh=hh, r=r: e.dma_start(
            out=vsb[:, c * NKV * 129:(c + 1) * NKV * 129]
            .rearrange("p (k d) -> p k d", d=129)[:, :, 0:128],
            in_=sh_v1.ap()[hh, r * 128:(r + 1) * 128, :]
            .rearrange("p (k d) -> p k d", d=128)),
            waits=([("v", vones)] if c == 0 else ()), inc="ld", inc_amt=16)
    for c in range(8):
        l = p.op("sync", lambda e, c=c: e.dma_start(
            out=lgo[:, c * 8:(c + 1) * 8],
            in_=lgl_dram[c * 128:(c + 1) * 128, :]),
            inc="ld", inc_amt=16)
        l = p.op("sync", lambda e, c=c: e.dma_start(
            out=vb[:, c * 16:(c + 1) * 16], in_=vb_dram[c]), inc="ld", inc_amt=16)
        l = p.op("sync", lambda e, c=c: e.dma_start(
            out=msk[:, c * 256:c * 256 + 128], in_=masks_dram[c, 0]),
            inc="ld", inc_amt=16)
        l = p.op("sync", lambda e, c=c: e.dma_start(
            out=msk[:, c * 256 + 128:(c + 1) * 256], in_=masks_dram[c, 1]),
            inc="ld", inc_amt=16)
    for kv in range(NKV):
        for half in range(2):
            l = p.op("sync", lambda e, kv=kv, half=half: e.dma_start(
                out=ksb[:, kv * 2 * T + half * T:kv * 2 * T + (half + 1) * T],
                in_=sh_k1.ap()[half, kv * 128:(kv + 1) * 128, :]),
                inc="ld", inc_amt=16)
    qv = qsb.ap().rearrange("p (k c q d) -> p k c q d", k=NKV, c=NT, q=2, d=D)
    for qh in range(NQ):
        kv, pp = qh // 2, qh % 2
        l = p.op("sync", lambda e, qh=qh, kv=kv, pp=pp: e.dma_start(
            out=qv[:, kv, :, pp, :], in_=q_dram[qh * 128:(qh + 1) * 128, :]
            .rearrange("p (c d) -> p c d", c=NT)), inc="ld", inc_amt=16)
    ldall = p.cnt("ld")
    p.op("vector", lambda e: e.memset(onesf[:, :], 1.0), inc="v")
    mset = p.cnt("v")

    c_ps = p.psum(nm + "_cps", [128, 8])
    m_ps = p.psum(nm + "_mps", [8, 8])
    cy_ps = p.psum(nm + "_cyps", [1, 8])
    tt = None
    for c in range(16):
        tt = p.op("tensor", lambda e, c=c: e.matmul(
            cy_ps[0:1, :], csel[:, c:c + 1], lgf[:, c * 8:(c + 1) * 8],
            start=(c == 0), stop=(c == 15)),
            waits=([("ld", ldall)] + p.wof(cy_ps) if c == 0 else ()),
            inc=("t" if c == 15 else None))
    vcy = p.op("vector", lambda e: e.tensor_copy(crow[0:1, :], cy_ps[0:1, :]),
               waits=[("t", tt)], inc="v")
    tt = None
    for c in range(16):
        tt = p.op("tensor", lambda e, c=c: e.matmul(
            m_ps[0:8, :], psel[:, c * 8:(c + 1) * 8], lgf[:, c * 8:(c + 1) * 8],
            start=(c == 0), stop=(c == 15)),
            waits=([("ld", ldall)] + p.wof(m_ps) if c == 0 else ()),
            inc=("t" if c == 15 else None))
    vm = p.op("vector", lambda e: e.tensor_copy(msb[:, :], m_ps[0:8, :]),
              waits=[("t", tt)], inc="v")
    mtmp = p.dram(nm + "_mtmp", [8, 8], F32)
    dm = p.op("sync", lambda e: e.dma_start(out=mtmp.ap(), in_=msb[:, :]),
              waits=[("v", vm)], inc="ld", inc_amt=16)
    dm = p.op("sync", lambda e: e.dma_start(
        out=mrow[0:1, :], in_=mtmp.ap().rearrange("a b -> (a b)")),
        waits=[("ld", dm)], inc="ld", inc_amt=16)
    mg_ps = p.psum(nm + "_mgps", [128, 64])
    tt = p.op("tensor", lambda e: e.matmul(
        mg_ps[:, :], onesf[0:1, 0:128], mrow[0:1, :], start=True, stop=True),
        waits=[("ld", dm), ("v", mset)] + p.wof(mg_ps), inc="t")
    vmg = p.op("vector", lambda e: e.tensor_copy(mgrid[:, :], mg_ps[:, :]),
               waits=[("t", p.cnt("t"))], inc="v")
    for c in range(16):
        tt = None
        for j in range(c):
            tt = p.op("tensor", lambda e, j=j: e.matmul(
                c_ps[:, :], onesf[:, :], lgf[:, j * 8:(j + 1) * 8],
                start=(j == 0), stop=False),
                waits=([("ld", ldall), ("v", p.cnt("v"))] + p.wof(c_ps)
                       if (c == 0 and j == 0) else
                       ([("v", p.cnt("v"))] if j == 0 else ())))
        tt = p.op("tensor", lambda e, c=c: e.matmul(
            c_ps[:, :], tri[:, :], lgf[:, c * 8:(c + 1) * 8],
            start=(c == 0), stop=True),
            waits=([("ld", ldall), ("v", p.cnt("v"))] + p.wof(c_ps)
                   if c == 0 else ([("v", p.cnt("v"))] if c == 0 else ())),
            inc="t")
        p.op("vector", lambda e, c=c: e.tensor_copy(
            cumg[:, c * 8:(c + 1) * 8], c_ps[:, :]),
            waits=[("t", p.cnt("t"))], inc="v")
    for c in range(8):
        tt = None
        for j in range(c):
            tt = p.op("tensor", lambda e, j=j, c=c: e.matmul(
                c_ps[:, :], onesf[:, :], lgo[:, j * 8:(j + 1) * 8],
                start=(j == 0), stop=False),
                waits=([("v", p.cnt("v"))] if j == 0 else ()))
        tt = p.op("tensor", lambda e, c=c: e.matmul(
            c_ps[:, :], tri[:, :], lgo[:, c * 8:(c + 1) * 8],
            start=(c == 0), stop=False),
            waits=([("v", p.cnt("v"))] if c == 0 else ()))
        tt = p.op("tensor", lambda e: e.matmul(
            c_ps[:, :], onesf[0:1, 0:128], crow[0:1, :],
            start=False, stop=True), waits=[("v", vcy)], inc="t")
        p.op("vector", lambda e, c=c: e.tensor_copy(
            ocum[:, c * 8:(c + 1) * 8], c_ps[:, :]),
            waits=[("t", p.cnt("t"))], inc="v")
    cum_v = p.cnt("v")
    for i in range(8):
        v1 = p.op("vector", lambda e, i=i: e.tensor_tensor(
            dsc[:, :].rearrange("p (c h) -> p c h", c=16),
            _bcast_mid(mgrid[:, i * 8:(i + 1) * 8], 16),
            cumg[:, :].rearrange("p (c h) -> p c h", c=16), ALU.subtract),
            waits=([("v", max(cum_v, vmg))] if i == 0 else
                   [("a", p.cnt("a"))]), inc="v")
        v2 = p.op("vector", lambda e, i=i: e.tensor_scalar(
            dsc[:, :], dsc[:, :], 60.0, None, ALU.min),
            waits=[("v", v1)], inc="v")
        a1 = p.op("scalar", lambda e, i=i: e.activation(
            esf[:, i * 128:(i + 1) * 128], dsc[:, :], AF.Exp),
            waits=[("v", v2)] + (p.wof() if i else []), inc="a")
        v3 = p.op("vector", lambda e, i=i: e.tensor_tensor(
            esf[:, i * 128:(i + 1) * 128].rearrange("p (c h) -> p c h", c=16),
            esf[:, i * 128:(i + 1) * 128].rearrange("p (c h) -> p c h", c=16),
            _bcast_mid(vb[:, i * 16:(i + 1) * 16], 8, pos=2),
            ALU.mult), waits=[("a", a1)], inc="v")
        v4 = p.op("vector", lambda e, i=i: e.tensor_tensor(
            dsc[:, 0:8], ocum[:, i * 8:(i + 1) * 8],
            mgrid[:, i * 8:(i + 1) * 8], ALU.subtract),
            waits=[("v", v3)], inc="v")
        a2 = p.op("scalar", lambda e, i=i: e.activation(
            ugrid[:, i * 8:(i + 1) * 8], dsc[:, 0:8], AF.Exp),
            waits=[("v", v4)], inc="a")
    prep_a = p.cnt("a")
    prep_v = p.cnt("v")
    p.free(c_ps, m_ps, cy_ps, mg_ps, psel, csel, tri, lgf, lgo, dsc, msb,
           mrow, crow, mgrid)

    y_tm = p.sbuf(nm + "_ytm", [128, 8 * NQ * D], BF16)
    sqb = p.sbuf(nm + "_sq", [128, 2 * 1024], BF16)
    atb = p.sbuf(nm + "_at", [128, 2 * 1024], BF16)
    nsc = p.sbuf(nm + "_nsc", [128, 4 * 8], F32)
    s_ps = [p.psum(nm + "_sps%d" % i, [128, 1024]) for i in range(2)]
    y_ps = p.psum(nm + "_yps", [128, 1536])
    slot = 0
    acts = {}
    pend_av = None
    for grp in range(2):
        for i in range(8):
            for j in range(9 + i):
                par = slot % 2
                tw = [("a", prep_a), ("v", prep_v), ("ld", ldall)]
                if slot == 0:
                    tw += p.wof(*s_ps, y_ps)
                if slot >= 2:
                    tw.append(("a", acts[slot - 2]))
                for kk in range(4):
                    kvh = grp * 4 + kk
                    p.op("tensor", lambda e, i=i, j=j, kvh=kvh, kk=kk, par=par: e.matmul(
                        s_ps[par][:, kk * 256:(kk + 1) * 256],
                        ksb[:, kvh * 2 * T + j * 128:kvh * 2 * T + (j + 1) * 128],
                        qsb[:, kvh * NT * 256 + i * 256:kvh * NT * 256 + (i + 1) * 256],
                        start=True, stop=True),
                        waits=(tw if kk == 0 else ()),
                        inc=("t" if kk == 3 else None))
                tqk = p.cnt("t")
                if pend_av is not None:
                    pend_av()
                    pend_av = None
                aa = p.op("scalar", lambda e, par=par: e.activation(
                    sqb[:, par * 1024:(par + 1) * 1024], s_ps[par][:, :],
                    AF.Square, scale=0.08838834764831845),
                    waits=[("t", tqk)] +
                          (p.wof(sqb, atb, nsc, y_tm) if slot == 0 else []) +
                          ([("v", p.cnt("v"))] if slot >= 2 else []),
                    inc="a")
                acts[slot] = aa
                vv = p.op("vector", lambda e, i=i, j=j, grp=grp, par=par: e.tensor_tensor(
                    atb[:, par * 1024:(par + 1) * 1024]
                    .rearrange("p (k t) -> p k t", k=4),
                    sqb[:, par * 1024:(par + 1) * 1024]
                    .rearrange("p (k t) -> p k t", k=4),
                    _bcast_mid(esf[:, i * 128 + j * 8 + grp * 4:
                                   i * 128 + j * 8 + grp * 4 + 4], 256, pos=2),
                    ALU.mult),
                    waits=[("a", aa)] + ([("t", p.cnt("t"))] if slot >= 2 else []),
                    inc="v")
                if j == i or j == i + 8:
                    sel = 0 if j == i else 1
                    vv = p.op("vector", lambda e, i=i, sel=sel, par=par: e.tensor_tensor(
                        atb[:, par * 1024:(par + 1) * 1024]
                        .rearrange("p (k t) -> p k t", k=8),
                        atb[:, par * 1024:(par + 1) * 1024]
                        .rearrange("p (k t) -> p k t", k=8),
                        _bcast_mid(msk[:, i * 256 + sel * 128:
                                       i * 256 + (sel + 1) * 128], 8),
                        ALU.mult), waits=[("v", vv)], inc="v")

                def make_av(i=i, j=j, grp=grp, par=par, vv=vv):
                    def emit():
                        for kk in range(4):
                            kvh = grp * 4 + kk
                            for pp in range(2):
                                gi = kk * 2 + pp
                                p.op("tensor", lambda e, i=i, j=j, kvh=kvh,
                                     kk=kk, pp=pp, gi=gi, par=par: e.matmul(
                                    y_ps[:, _yoff(gi):_yoff(gi) + 129],
                                    atb[:, par * 1024 + (kk * 2 + pp) * 128:
                                        par * 1024 + (kk * 2 + pp + 1) * 128],
                                    vsb[:, j * NKV * 129 + kvh * 129:
                                        j * NKV * 129 + (kvh + 1) * 129],
                                    start=(j == 0 and gi in (0, 3, 6)),
                                    stop=(j == 8 + i)),
                                    waits=([("v", vv)] if (kk == 0 and pp == 0)
                                           else ()),
                                    inc=("t" if (kk == 3 and pp == 1) else None))
                    return emit
                pend_av = make_av()
                slot += 1
            pend_av()
            pend_av = None
            p.op("tensor", lambda e: e.matmul(
                y_ps[0:1, 1535:1536], onesf[0:1, 0:1], onesf[0:1, 0:1],
                start=True, stop=True), inc="t")
            tav = p.cnt("t")
            for gi2 in range(8):
                p.op("vector", lambda e, gi2=gi2: e.tensor_copy(
                    nsc[:, gi2:gi2 + 1], y_ps[:, _yoff(gi2) + 128:_yoff(gi2) + 129]),
                    waits=([("t", tav)] if gi2 == 0 else ()), inc="v")
            vd = p.op("vector", lambda e, i=i, grp=grp: e.tensor_tensor(
                nsc[:, 8:16].rearrange("p (k q) -> p k q", k=4),
                nsc[:, 0:8].rearrange("p (k q) -> p k q", k=4),
                _bcast_mid(ugrid[:, i * 8 + grp * 4:i * 8 + grp * 4 + 4], 2,
                           pos=2), ALU.mult), waits=[("v", p.cnt("v"))], inc="v")
            vd = p.op("vector", lambda e: e.tensor_scalar(
                nsc[:, 8:16], nsc[:, 8:16], 1e-6, None, ALU.add),
                waits=[("v", vd)], inc="v")
            vd = p.op("vector", lambda e: e.reciprocal(
                nsc[:, 16:24], nsc[:, 8:16]), waits=[("v", vd)], inc="v")
            vd = p.op("vector", lambda e, i=i, grp=grp: e.tensor_tensor(
                nsc[:, 24:32].rearrange("p (k q) -> p k q", k=4),
                nsc[:, 16:24].rearrange("p (k q) -> p k q", k=4),
                _bcast_mid(ugrid[:, i * 8 + grp * 4:i * 8 + grp * 4 + 4], 2,
                           pos=2), ALU.mult), waits=[("v", vd)], inc="v")
            for gi2 in range(8):
                qh = grp * 8 + gi2
                vd = p.op("vector", lambda e, i=i, qh=qh, gi2=gi2: e.tensor_scalar(
                    y_tm[:, i * 2048 + qh * 128:i * 2048 + (qh + 1) * 128],
                    y_ps[:, _yoff(gi2):_yoff(gi2) + 128],
                    nsc[:, 24 + gi2:25 + gi2], None, ALU.mult),
                    waits=([("v", vd)] if gi2 == 0 else ()), inc="v")
    ydone_v = p.cnt("v")
    p.free(sqb, atb, nsc, *s_ps, y_ps, ksb, qsb, vsb, esf, ugrid, cumg, ocum,
           vb, msk)

    yfm = p.sbuf(nm + "_yfm", [128, NQ * T], BF16)
    t_ps = [p.psum(nm + "_tps%d" % i2, [128, 128], BF16) for i2 in range(2)]
    k = 0
    for qh in range(NQ):
        for i in range(8):
            tt = p.op("tensor", lambda e, qh=qh, i=i, k=k: e.transpose(
                t_ps[k % 2][:, :],
                y_tm[:, i * 2048 + qh * 128:i * 2048 + (qh + 1) * 128],
                ident[:, :]),
                waits=([("v", ydone_v)] + p.wof(*t_ps, yfm) if k == 0 else
                       ([("v", p.cnt("v"))] if k >= 2 else ())),
                inc="t")
            p.op("vector", lambda e, qh=qh, i=i, k=k: e.tensor_copy(
                yfm[:, qh * T + i * 128:qh * T + (i + 1) * 128],
                t_ps[k % 2][:, :]),
                waits=[("t", p.cnt("t"))], inc="v")
            k += 1
    yfm_v = p.cnt("v")
    p.free(y_tm, *t_ps, ident)

    wsb = p.sbuf(nm + "_wc", [128, NH * H], BF16)
    bcs = p.sbuf(nm + "_bc", [128, 16], F32)
    w_ps = [p.psum(nm + "_wps%d" % i2, [128, T]) for i2 in range(2)]
    xres = p.sbuf(nm + "_xr", [128, 2 * T], F32)
    xo = p.sbuf(nm + "_xo", [128, 2 * T], F32)
    wv = None
    for hc in range(NH):
        wv = p.op("gpsimd", lambda e, hc=hc: e.dma_start(
            out=wsb[:, hc * H:(hc + 1) * H], in_=wc_dram[hc]),
            waits=(p.wof(wsb) if hc == 0 else ()), inc="w", inc_amt=16)
    bl = p.op("sync", lambda e: e.dma_start(out=bcs[:, :], in_=bc_dram.ap()),
              waits=p.wof(bcs), inc="ld", inc_amt=16)
    st_base = p.cnt("st")
    for oc in range(NH):
        par = oc % 2
        lr = p.op("sync", lambda e, oc=oc, par=par: e.dma_start(
            out=xres[:, par * T:(par + 1) * T],
            in_=x_dram[oc * 128:(oc + 1) * 128, :]),
            waits=(p.wof(xres, xo) if oc == 0 else
                   ([("v", p.cnt("v"))] if oc >= 2 else [])),
            inc="ld", inc_amt=16)
        tw = [("w", wv), ("v", yfm_v)]
        if oc == 0:
            tw += p.wof(*w_ps)
        if oc >= 2:
            tw.append(("v", p.cnt("v")))
        for qc in range(NQ):
            for half in range(2):
                p.op("tensor", lambda e, oc=oc, par=par, qc=qc, half=half: e.matmul(
                    w_ps[par][:, half * 512:(half + 1) * 512],
                    wsb[:, qc * H + oc * 128:qc * H + (oc + 1) * 128],
                    yfm[:, qc * T + half * 512:qc * T + (half + 1) * 512],
                    start=(qc == 0), stop=(qc == NQ - 1)),
                    waits=(tw if (qc == 0 and half == 0) else ()),
                    inc=("t" if (qc == NQ - 1 and half == 1) else None))
        tdd = p.cnt("t")
        vw = [("t", tdd), ("ld", max(lr, bl))]
        if oc >= 2:
            vw.append(("st", st_base + 16 * (oc - 1)))
        vo = p.op("vector", lambda e, oc=oc, par=par: e.scalar_tensor_tensor(
            xo[:, par * T:(par + 1) * T], w_ps[par][:, :], bcs[:, oc:oc + 1],
            xres[:, par * T:(par + 1) * T], ALU.add, ALU.add),
            waits=vw, inc="v")
        p.op("sync", lambda e, oc=oc, par=par: e.dma_start(
            out=xout_dram[oc * 128:(oc + 1) * 128, :],
            in_=xo[:, par * T:(par + 1) * T]), waits=[("v", vo)],
            inc="st", inc_amt=16)
    st_done = p.cnt("st")
    p.free(yfm, wsb, bcs, xres, xo, *w_ps)
    return st_done



def build_fused():
    p = Prog()
    x_in = p.dram("x_in", [H, T], F32, kind="ExternalInput")
    wg0 = p.dram("wg0", [NI0, 128, NH * 128], I32, kind="ExternalInput")
    wu0 = p.dram("wu0", [NI0, 128, NH * 128], I32, kind="ExternalInput")
    wd0 = p.dram("wd0", [NH, 128, NI0 * 128], I32, kind="ExternalInput")
    sg0 = p.dram("sg0", [128, NI0], F32, kind="ExternalInput")
    su0 = p.dram("su0", [128, NI0], F32, kind="ExternalInput")
    sd0 = p.dram("sd0", [128, NH], F32, kind="ExternalInput")
    wq0 = p.dram("wq0", [NH, 128, H], F32, kind="ExternalInput")
    wk0 = p.dram("wk0", [NH, 128, H], F32, kind="ExternalInput")
    wv0 = p.dram("wv0", [NH, 128, H], F32, kind="ExternalInput")
    wo0 = p.dram("wo0", [NH, 128, H], F32, kind="ExternalInput")
    wg1 = p.dram("wg1", [NI1, 128, NH * 128], I32, kind="ExternalInput")
    wu1 = p.dram("wu1", [NI1, 128, NH * 128], I32, kind="ExternalInput")
    wd1 = p.dram("wd1", [NH, 128, NI1 * 128], I32, kind="ExternalInput")
    sg1 = p.dram("sg1", [128, NI1], F32, kind="ExternalInput")
    su1 = p.dram("su1", [128, NI1], F32, kind="ExternalInput")
    sd1 = p.dram("sd1", [128, NH], F32, kind="ExternalInput")
    wq1 = p.dram("wq1", [NH, 128, H], F32, kind="ExternalInput")
    wk1 = p.dram("wk1", [NH, 128, 1024], F32, kind="ExternalInput")
    wv1 = p.dram("wv1", [NH, 128, 1024], F32, kind="ExternalInput")
    wgg1 = p.dram("wgg1", [NH, 128, 8], F32, kind="ExternalInput")
    wc = p.dram("wc", [NH, 128, H], F32, kind="ExternalInput")
    bc = p.dram("bc", [128, 16], F32, kind="ExternalInput")
    ident = p.dram("ident", [128, 128], BF16, kind="ExternalInput")
    identf = p.dram("identf", [128, 128], F32, kind="ExternalInput")
    trif = p.dram("trif", [128, 128], F32, kind="ExternalInput")
    psel = p.dram("psel", [16, 128, 8], F32, kind="ExternalInput")
    csel = p.dram("csel", [16, 128, 1], F32, kind="ExternalInput")
    vb = p.dram("vb", [8, 128, 16], F32, kind="ExternalInput")
    masks = p.dram("masks", [8, 2, 128, 128], BF16, kind="ExternalInput")
    nflags = p.dram("nflags", [1, 8], I32, kind="ExternalInput")
    out = p.dram("out", [H, T], F32, kind="ExternalOutput")
    x1 = p.dram("x1", [H, T], F32)
    x2 = p.dram("x2", [H, T], F32)
    x3 = p.dram("x3", [H, T], F32)
    x4 = p.dram("x4", [H, T], F32)
    x5 = p.dram("x5", [H, T], F32)
    v0 = p.dram("v0", [H, T], BF16)
    q1d = p.dram("q1d", [H, T], BF16)
    q0l = p.dram("q0l", [T, H], BF16)
    k0l = p.dram("k0l", [T, H], BF16)
    k1l = p.dram("k1l", [1024, T], BF16)
    v1l = p.dram("v1l", [T, 1024], BF16)
    lgl = p.dram("lgl", [T, 8], F32)
    shq0 = p.dram("shq0", [2, T, H], BF16, addr_space="Shared")
    shk0 = p.dram("shk0", [2, T, H], BF16, addr_space="Shared")
    shk1 = p.dram("shk1", [2, 1024, T], BF16, addr_space="Shared")
    shv1 = p.dram("shv1", [2, T, 1024], BF16, addr_space="Shared")
    shlg = p.dram("shlg", [2, T, 8], F32, addr_space="Shared")
    shf = p.dram("shf", [2, 8], I32, addr_space="Shared")

    nf = p.sbuf("nfb", [1, 8], I32)
    ft = p.sbuf("ftb", [1, 8], I32)
    p.op("gpsimd", lambda e: e.memset(ft[0:1, :], 0),
         waits=p.wof(nf, ft), inc="w")
    nfw = p.op("gpsimd", lambda e: e.dma_start(
        out=nf[0:1, :], in_=nflags.ap()), inc="w", inc_amt=16)

    emit_ffn(p, x_in, x1, wg0, wu0, wd0, sg0, su0, sd0, NI0, "f0a")
    p.wait("sync", ("st", p.cnt("st")))
    p.barrier()
    xln = p.sbuf("l1_xln", [128, NH * T], BF16)
    wsb_q = p.sbuf("wsbq", [128, NH * H], BF16)
    lnv = emit_ln(p, x1, xln, "l1_ln")
    wsb_k = p.sbuf("wsbk", [128, NH * H], BF16)
    emit_proj(p, xln, wq0, q0l, H, "tm", "pq", lnv, wsb_pre=wsb_q)
    stq0 = p.ds_copy(shq0, q0l, waits=[("st", p.cnt("st")), ("w", nfw)])
    p.free(wsb_q)
    wsb_v = p.sbuf("wsbv", [128, NH * H], BF16)
    emit_proj(p, xln, wk0, k0l, H, "tm", "pk", lnv, wsb_pre=wsb_k)
    stqk = p.ds_copy(shk0, k0l, waits=[("st", p.cnt("st"))])
    p.pair_signal(0, shf, nf, waits=[("st", stqk)])
    p.free(wsb_k)
    emit_proj(p, xln, wv0, v0, H, "fm", "pv", lnv, wsb_pre=wsb_v)
    pollA = p.pair_poll(0, shf, nf, ft, waits=[("w", nfw)])
    p.free(wsb_v, xln)
    p.wait("sync", ("st", p.cnt("st")))
    p.barrier()
    emit_attn0(p, x1, x2, shq0, shk0, v0, wo0, ident, "a0",
               gate=[("pp", pollA)])
    p.wait("sync", ("st", p.cnt("st")))
    p.barrier()
    emit_ffn(p, x2, x3, wg0, wu0, wd0, sg0, su0, sd0, NI0, "f0b")
    p.wait("sync", ("st", p.cnt("st")))
    p.barrier()
    emit_ffn(p, x3, x4, wg1, wu1, wd1, sg1, su1, sd1, NI1, "f1a")
    p.wait("sync", ("st", p.cnt("st")))
    p.barrier()
    xln2 = p.sbuf("l2_xln", [128, NH * T], BF16)
    wsb_q1 = p.sbuf("wsbq1", [128, NH * H], BF16)
    lnv2 = emit_ln(p, x4, xln2, "l2_ln")
    wsb_k1 = p.sbuf("wsbk1", [128, NH * 1024], BF16)
    emit_proj(p, xln2, wq1, q1d, H, "fm", "pq1", lnv2, wsb_pre=wsb_q1)
    p.free(wsb_q1)
    wsb_v1 = p.sbuf("wsbv1", [128, NH * 1024], BF16)
    emit_proj(p, xln2, wk1, k1l, 1024, "fm", "pk1", lnv2, wsb_pre=wsb_k1)
    p.ds_copy(shk1, k1l, waits=[("st", p.cnt("st"))])
    p.free(wsb_k1)
    emit_proj(p, xln2, wv1, v1l, 1024, "tm", "pv1", lnv2, wsb_pre=wsb_v1)
    p.ds_copy(shv1, v1l, waits=[("st", p.cnt("st"))])
    p.free(wsb_v1)
    emit_proj_g(p, xln2, wgg1, lgl, identf, "pg1", lnv2)
    lgst = p.ds_copy(shlg, lgl, waits=[("st", p.cnt("st"))])
    p.pair_signal(1, shf, nf, waits=[("st", lgst)])
    pollB = p.pair_poll(1, shf, nf, ft, waits=[("w", nfw)])
    p.free(xln2)
    p.wait("sync", ("st", p.cnt("st")))
    p.barrier()
    emit_attn1(p, x4, x5, q1d, shk1, shv1, shlg, lgl, psel, csel, vb, masks,
               trif, ident, wc, bc, "a1", gate=[("pp", pollB)])
    p.wait("sync", ("st", p.cnt("st")))
    p.barrier()
    emit_ffn(p, x5, out, wg1, wu1, wd1, sg1, su1, sd1, NI1, "f1b")
    p.wait("sync", ("st", p.cnt("st")), ("w", p.cnt("w")), ("ld", p.cnt("ld")))
    return p.finalize()



def _block_lhsT(w_t, nin, nout):
    NK, NO = nin // 128, nout // 128
    a = w_t.reshape(NK, 128, NO, 128)
    return np.ascontiguousarray(a.transpose(2, 1, 0, 3).reshape(NO, 128, NK * 128))


def _scale_cols(s):
    return np.ascontiguousarray(np.asarray(s).reshape(-1, 128).T)


def _nat_chunks(w):
    w = np.asarray(w)
    return np.ascontiguousarray(w.reshape(w.shape[0] // 128, 128, w.shape[1]))


def _shard_x(x):
    out = []
    for b in range(4):
        for hhalf in range(2):
            out.append(np.ascontiguousarray(
                x[b, hhalf * T:(hhalf + 1) * T, :].T))
    return out


_progs = {}

import os as _os
_NONCE_BASE = (int.from_bytes(_os.urandom(4), "little") & 0x1fffffff) | (1 << 24)
_nonce_ctr = 0


def _get_prog(name):
    if name not in _progs:
        _progs[name] = {"fused": build_fused}[name]()
    return _progs[name]


def _prep_weights(inputs):
    w = {}
    w["wg0"] = _block_lhsT(np.ascontiguousarray(np.asarray(inputs["L0_ffn_gate_w"]).T), H, I0)
    w["wu0"] = _block_lhsT(np.ascontiguousarray(np.asarray(inputs["L0_ffn_up_w"]).T), H, I0)
    w["wd0"] = _block_lhsT(np.ascontiguousarray(np.asarray(inputs["L0_ffn_down_w"]).T), I0, H)
    w["sg0"] = _scale_cols(inputs["L0_ffn_gate_scale"])
    w["su0"] = _scale_cols(inputs["L0_ffn_up_scale"])
    w["sd0"] = _scale_cols(inputs["L0_ffn_down_scale"])
    w["wq0"] = _nat_chunks(inputs["L0_attn_Wq"])
    w["wk0"] = _nat_chunks(inputs["L0_attn_Wk"])
    w["wv0"] = _nat_chunks(inputs["L0_attn_Wv"])
    w["wo0"] = _nat_chunks(inputs["L0_attn_Wo"])
    w["wg1"] = _block_lhsT(np.ascontiguousarray(np.asarray(inputs["L1_ffn_wq_gate"]).T), H, I1)
    w["wu1"] = _block_lhsT(np.ascontiguousarray(np.asarray(inputs["L1_ffn_wq_up"]).T), H, I1)
    w["wd1"] = _block_lhsT(np.ascontiguousarray(np.asarray(inputs["L1_ffn_wq_down"]).T), I1, H)
    w["sg1"] = _scale_cols(inputs["L1_ffn_s_gate"])
    w["su1"] = _scale_cols(inputs["L1_ffn_s_up"])
    w["sd1"] = _scale_cols(inputs["L1_ffn_s_down"])
    w["wq1"] = _nat_chunks(inputs["L1_attn_Wq"])
    w["wk1"] = _nat_chunks(inputs["L1_attn_Wk"])
    w["wv1"] = _nat_chunks(inputs["L1_attn_Wv"])
    w["wgg1"] = _nat_chunks(inputs["L1_attn_Wg"])
    w["wc"] = _nat_chunks(inputs["L1_attn_Wc"])
    w["bc"] = _scale_cols(inputs["L1_attn_bc"])
    return w


def _const_inputs():
    c = {}
    c["ident_f32"] = np.eye(128, dtype=np.float32)
    c["ident_bf16"] = np.eye(128, dtype=np.float32).astype(BF)
    c["trif"] = np.triu(np.ones((128, 128), dtype=np.float32))
    half = {}
    for hh in range(2):
        d = {}
        ps = np.zeros((16, 128, 8), dtype=np.float32)
        for i in range(8):
            lim = (8 * hh + i) * 128
            for cch in range(16):
                for s in range(128):
                    if cch * 128 + s <= lim:
                        ps[cch, s, i] = 1.0
        d["psel"] = ps
        cs = np.zeros((16, 128, 1), dtype=np.float32)
        cs[:8 * hh] = 1.0
        d["csel"] = cs
        vbm = np.zeros((8, 128, 16), dtype=np.float32)
        for i in range(8):
            gi = 8 * hh + i
            vbm[i, :, :gi + 1] = 1.0
        d["vb"] = vbm
        mk = np.zeros((8, 2, 128, 128), dtype=np.float32)
        tri = np.triu(np.ones((128, 128), dtype=np.float32))
        for i in range(8):
            if hh == 0:
                mk[i, 0] = tri
                mk[i, 1] = 0.0
            else:
                mk[i, 0] = 1.0
                mk[i, 1] = tri
        d["masks"] = mk.astype(BF)
        half[hh] = d
    c["half"] = half
    return c


def kernel(**inputs):
    inputs = {k: np.asarray(v) for k, v in inputs.items()}
    x = inputs["L0_attn_x"]
    w = _prep_weights(inputs)
    cst = _const_inputs()
    xs = _shard_x(x)
    cores = list(range(8))

    global _nonce_ctr
    _nonce_ctr += 16
    nonce = (_NONCE_BASE + _nonce_ctr) & 0x3fffffff
    nflags = np.zeros((1, 8), np.int32)
    nflags[0, 0] = nonce + 1
    nflags[0, 1] = nonce + 2

    nc = _get_prog("fused")
    in_maps = []
    for c in cores:
        hh = c % 2
        d = cst["half"][hh]
        in_maps.append(dict(
            x_in=xs[c],
            wg0=w["wg0"], wu0=w["wu0"], wd0=w["wd0"],
            sg0=w["sg0"], su0=w["su0"], sd0=w["sd0"],
            wq0=w["wq0"], wk0=w["wk0"], wv0=w["wv0"], wo0=w["wo0"],
            wg1=w["wg1"], wu1=w["wu1"], wd1=w["wd1"],
            sg1=w["sg1"], su1=w["su1"], sd1=w["sd1"],
            wq1=w["wq1"], wk1=w["wk1"], wv1=w["wv1"], wgg1=w["wgg1"],
            wc=w["wc"], bc=w["bc"],
            ident=cst["ident_bf16"], identf=cst["ident_f32"],
            trif=cst["trif"],
            psel=d["psel"], csel=d["csel"], vb=d["vb"], masks=d["masks"],
            nflags=nflags))
    res = run_bass_kernel_spmd(nc, in_maps, core_ids=cores).results

    out = np.empty((4, 2048, H), dtype=np.float32)
    for c in cores:
        b, hh = c // 2, c % 2
        out[b, hh * T:(hh + 1) * T, :] = res[c]["out"].T
    return out
